# version 01
# baseline (speedup 1.0000x reference)
import contextlib
import numpy as np
import ml_dtypes
import concourse.bass as bass
import concourse.mybir as mybir
from concourse.bass_utils import run_bass_kernel_spmd

F32 = mybir.dt.float32
BF16 = mybir.dt.bfloat16
I32 = mybir.dt.int32
U32 = mybir.dt.uint32
AF = mybir.ActivationFunctionType
ALU = mybir.AluOpType
AX = mybir.AxisListType

COMPUTE_ENGINES = ("tensor", "vector", "scalar", "pool", "sp")
QUEUE_ISSUER = {"qsp": "sp", "qact": "scalar", "qpool": "pool", "qdve": "vector"}
ALL_STREAMS = COMPUTE_ENGINES + tuple(QUEUE_ISSUER)
SECTION_NAME = {"tensor": "tensor", "vector": "vector", "scalar": "scalar", "pool": "gpsimd", "sp": "sync"}


class Op:
    __slots__ = ("i", "stream", "emit", "reads", "writes", "deps", "inc", "cum", "tag", "n_inc")

    def __init__(self, i, stream, emit, reads, writes, tag):
        self.i = i
        self.stream = stream
        self.emit = emit
        self.reads = reads
        self.writes = writes
        self.deps = set()
        self.inc = False
        self.cum = 0
        self.tag = tag
        self.n_inc = 1


class KB:
    def __init__(self, nc):
        self.nc = nc
        self.ops = []
        self.last_writer = {}
        self.readers = {}
        self.stack = contextlib.ExitStack()
        self._sb = {}
        self._ps = {}

    def sb(self, name, shape, dtype):
        t = self.stack.enter_context(self.nc.sbuf_tensor(name, list(shape), dtype))
        self._sb[name] = t
        return t

    def ps(self, name, shape, dtype=F32):
        t = self.stack.enter_context(self.nc.psum_tensor(name, list(shape), dtype))
        self._ps[name] = t
        return t

    def sem(self, name):
        return self.stack.enter_context(self.nc.semaphore(name))

    def op(self, stream, emit, reads=(), writes=(), tag=None, n_inc=1):
        o = Op(len(self.ops), stream, emit, tuple(reads), tuple(writes), tag)
        o.n_inc = 16 if stream in QUEUE_ISSUER else n_inc
        for k in o.reads:
            w = self.last_writer.get(k)
            if w is not None:
                o.deps.add(w)
        for k in o.writes:
            w = self.last_writer.get(k)
            if w is not None:
                o.deps.add(w)
            for r in self.readers.get(k, ()):
                o.deps.add(r)
        for k in o.writes:
            self.last_writer[k] = o.i
            self.readers[k] = []
        for k in o.reads:
            self.readers.setdefault(k, []).append(o.i)
        self.ops.append(o)
        return o

    def dma(self, queue, out, in_, reads=(), writes=(), tag=None, **kw):
        issuer = QUEUE_ISSUER[queue]
        eng_name = SECTION_NAME[issuer]
        nc = self.nc

        def emit(eng):
            return getattr(nc, eng_name).dma_start(out=out, in_=in_, **kw)
        return self.op(queue, emit, reads=reads, writes=writes, tag=tag)

    QPOOL = 12

    def finalize(self, same_engine_sync=True):
        ops = self.ops
        for o in ops:
            pruned = set()
            for d in o.deps:
                p = ops[d]
                if p.stream == o.stream:
                    if p.stream == "tensor":
                        continue
                    if p.stream in QUEUE_ISSUER:
                        pass
                    elif not same_engine_sync:
                        continue
                pruned.add(d)
            o.deps = pruned
        for o in ops:
            for d in o.deps:
                ops[d].inc = True
        for o in ops:
            if o.stream in QUEUE_ISSUER:
                o.inc = True
        cum = {s: 0 for s in COMPUTE_ENGINES}
        qidx = {q: 0 for q in QUEUE_ISSUER}
        for o in ops:
            if o.stream in QUEUE_ISSUER:
                i = qidx[o.stream]
                qidx[o.stream] += 1
                o.cum = (i % self.QPOOL, 16 * (i // self.QPOOL + 1))
            else:
                if o.inc:
                    cum[o.stream] += 1
                o.cum = cum[o.stream]
        self.totals = dict(cum)
        self.qcounts = dict(qidx)
        plan = {e: [] for e in COMPUTE_ENGINES}
        watermark = {e: {} for e in COMPUTE_ENGINES}
        for o in ops:
            eng = QUEUE_ISSUER.get(o.stream, o.stream)
            need = {}
            if o.stream in QUEUE_ISSUER and o.cum[1] > 16:
                need[(o.stream, o.cum[0])] = o.cum[1] - 16
            for d in o.deps:
                p = ops[d]
                if p.stream in QUEUE_ISSUER:
                    key = (p.stream, p.cum[0])
                    need[key] = max(need.get(key, 0), p.cum[1])
                else:
                    need[p.stream] = max(need.get(p.stream, 0), p.cum)
            waits = []
            for s, v in sorted(need.items(), key=str):
                if watermark[eng].get(s, 0) < v:
                    waits.append((s, v))
                    watermark[eng][s] = v
            plan[eng].append((waits, o))
        return plan

    def build(self, same_engine_sync=True):
        plan = self.finalize(same_engine_sync)
        nc = self.nc
        sems = {}
        for s in COMPUTE_ENGINES:
            if self.totals.get(s, 0) > 0:
                sems[s] = self.sem(f"sem_{s}")
        qsems = {}
        for q, n in self.qcounts.items():
            if n > 0:
                qsems[q] = [self.sem(f"sem_{q}_{i}") for i in range(min(n, self.QPOOL))]
        n_ops = sum(len(v) for v in plan.values())

        def sem_of(key):
            if isinstance(key, tuple):
                return qsems[key[0]][key[1]]
            return sems[key]

        with nc.Block() as block:
            for e in COMPUTE_ENGINES:
                items = plan[e]
                my_queues = [q for q, iss in QUEUE_ISSUER.items() if iss == e and self.qcounts.get(q, 0) > 0]
                if not items and not my_queues:
                    continue

                def make_body(items=items, my_queues=my_queues, e=e):
                    def body(eng):
                        for waits, o in items:
                            for s, v in waits:
                                eng.wait_ge(sem_of(s), v)
                            inst = o.emit(eng)
                            if o.inc:
                                assert inst is not None, f"op '{o.tag}' on {o.stream} returned None"
                                if o.stream in QUEUE_ISSUER:
                                    inst.then_inc(qsems[o.stream][o.cum[0]], 16)
                                else:
                                    inst.then_inc(sems[o.stream], 1)
                        for q in my_queues:
                            n = self.qcounts[q]
                            for sl in range(min(n, self.QPOOL)):
                                uses = (n - sl + self.QPOOL - 1) // self.QPOOL
                                eng.wait_ge(qsems[q][sl], 16 * uses)
                    return body
                section = getattr(block, SECTION_NAME[e])
                section(make_body())
        return n_ops


"""Slot emitters + stage harness. Layout conventions per core:
- token-major activation: sbuf [128 part, ntile, C]  (token = tile*128 + p)
- feature-major activation: sbuf [128 part, nft, T]  (feature = ft*128 + p, tokens on free)
All matmul operands bf16; PSUM accumulation fp32; elementwise fp32 unless noted.
"""
import numpy as np
import ml_dtypes

T_FULL = 2048
D = 1024
NTT = T_FULL // 128
NFT = D // 128


def f32(x):
    return np.ascontiguousarray(x, dtype=np.float32)


class Banks:

    def __init__(self, kb, nf=6, nb=2):
        self.f = [(kb.ps(f"bkf{i}", [128, 512], F32), f"bkf{i}") for i in range(nf)]
        self.b = [(kb.ps(f"bkb{i}", [128, 1024], BF16), f"bkb{i}") for i in range(nb)]
        self.fi = 0
        self.bi = 0

    def nf(self):
        t = self.f[self.fi % len(self.f)]
        self.fi += 1
        return t

    def nb(self):
        t = self.b[self.bi % len(self.b)]
        self.bi += 1
        return t


def mm_group(kb, out_ap, bank_key, pairs, reads, tag=None):
    def emit(t):
        last = None
        n = len(pairs)
        for i, (l, r) in enumerate(pairs):
            last = t.matmul(out_ap, l, r, start=(i == 0), stop=(i == n - 1))
        return last
    return kb.op("tensor", emit, reads=reads, writes=[bank_key], tag=tag)


def transpose_group(kb, items, ident, ident_key, reads, writes, tag=None):
    def emit(t):
        last = None
        for o, i in items:
            last = t.transpose(o, i, ident)
        return last
    return kb.op("tensor", emit, reads=list(reads) + [ident_key], writes=list(writes), tag=tag)


class Ctx:

    def __init__(self, name):
        self.nc = bass.Bass("TRN2", target_bir_lowering=False)
        self.kb = KB(self.nc)
        self.inputs = {}
        self.host = {}
        self.outputs = []

    def inp(self, name, shape, dtype, fn):
        ap = self.nc.dram_tensor(name, list(shape), dtype, kind="ExternalInput").ap()
        self.host[name] = fn
        return ap

    def out(self, name, shape, dtype):
        ap = self.nc.dram_tensor(name, list(shape), dtype, kind="ExternalOutput").ap()
        self.outputs.append(name)
        return ap

    def const_ident(self):
        kb = self.kb
        ident_in = self.inp("c_ident", [128, 128], F32, lambda c, e: np.eye(128, dtype=np.float32))
        idb = kb.sb("idb", [128, 128], BF16)
        idf = kb.sb("idf", [128, 128], F32)
        kb.dma("qpool", idb[:, :], ident_in[:, :], writes=["idb"], tag="ldidb")
        kb.dma("qpool", idf[:, :], ident_in[:, :], writes=["idf"], tag="ldidf")
        self.idb, self.idf = idb, idf
        return idb, idf

    def run(self, env, n_cores=8, **kw):
        in_maps = [{k: np.ascontiguousarray(fn(c, env)) for k, fn in self.host.items()}
                   for c in range(n_cores)]
        res = run_bass_kernel_spmd(self.nc, in_maps, core_ids=list(range(n_cores)), **kw)
        return res



def emit_cast_load_tokmajor(ctx, name, dram_ap, sb, key, dtype_note="f32->bf16"):
    ctx.kb.dma("qpool", sb[:, :, :], dram_ap.rearrange("(nt p) d -> p nt d", p=128),
               writes=[key], tag=f"ld_{name}")


def emit_transpose_tokmajor_to_featmajor(ctx, src, src_keys, dst, dst_key_prefix,
                                         ntt=NTT, nft=NFT, tag=""):
    kb = ctx.kb
    for ft in range(nft):
        for h in range(0, ntt, 8):
            bank, bkey = ctx.banks.nb()
            n = min(8, ntt - h)
            items = [(bank[:, i * 128:(i + 1) * 128], src[:, h + i, ft * 128:(ft + 1) * 128])
                     for i in range(n)]
            transpose_group(kb, items, ctx.idb[:, :], "idb",
                            reads=src_keys, writes=[bkey], tag=f"tr{tag}_{ft}_{h}")
            kb.op("vector", lambda v, b=bank, ft=ft, h=h, n=n:
                  v.tensor_copy(dst[:, ft, h * 128:(h + n) * 128], b[:, 0:n * 128]),
                  reads=[bkey], writes=[f"{dst_key_prefix}_{ft}_{h}"], tag=f"trc{tag}_{ft}_{h}")


def featmajor_keys(prefix, nft=NFT, ntt=NTT):
    return [f"{prefix}_{ft}_{h}" for ft in range(nft) for h in range(0, ntt, 8)]


def emit_ln(ctx, xs, xs_keys, out_bf, out_keys, eps=1e-5, ntile=NTT, ncol=D, tag="ln",
            out_f32=None, out_f32_keys=None):
    kb = ctx.kb
    stats = kb.sb(f"{tag}_stats", [128, ntile, 12], F32)
    mv = kb.sb(f"{tag}_mv", [128, ntile, 2], F32)
    col = kb.sb(f"{tag}_col", [128, ntile, 2], F32)
    nch = ncol // 512
    assert ncol % 512 == 0
    for i in range(ntile):
        for ch in range(nch):
            kb.op("vector", lambda v, i=i, ch=ch: v.bn_stats(stats[:, i, ch * 6:(ch + 1) * 6],
                                                             xs[:, i, ch * 512:(ch + 1) * 512]),
                  reads=[xs_keys[i]], writes=[f"{tag}st{i}_{ch}"])
        kb.op("vector", lambda v, i=i: v.bn_aggr(mv[:, i, :], stats[:, i, 0:6 * nch]),
              reads=[f"{tag}st{i}_{ch}" for ch in range(nch)], writes=[f"{tag}mv{i}"])
        kb.op("vector", lambda v, i=i: v.tensor_scalar(col[:, i, 0:1], mv[:, i, 0:1], -1.0, None, ALU.mult),
              reads=[f"{tag}mv{i}"], writes=[f"{tag}nm{i}"])
        kb.op("scalar", lambda s, i=i: s.activation(col[:, i, 1:2], mv[:, i, 1:2], AF.Sqrt, bias=ctx.epsc[:, :]),
              reads=[f"{tag}mv{i}", "epsc"], writes=[f"{tag}sq{i}"])
        kb.op("vector", lambda v, i=i: v.reciprocal(col[:, i, 1:2], col[:, i, 1:2]),
              reads=[f"{tag}sq{i}"], writes=[f"{tag}rs{i}"])
        kb.op("vector", lambda v, i=i: v.tensor_scalar(out_bf[:, i, :], xs[:, i, :],
                                                       col[:, i, 0:1], col[:, i, 1:2], ALU.add, ALU.mult),
              reads=[xs_keys[i], f"{tag}nm{i}", f"{tag}rs{i}"], writes=[out_keys[i]])
        if out_f32 is not None:
            kb.op("vector", lambda v, i=i: v.tensor_scalar(out_f32[:, i, :], xs[:, i, :],
                                                           col[:, i, 0:1], col[:, i, 1:2], ALU.add, ALU.mult),
                  reads=[xs_keys[i], f"{tag}nm{i}", f"{tag}rs{i}"], writes=[out_f32_keys[i]])


def emit_eps_consts(ctx):
    kb = ctx.kb
    ctx.epsc = kb.sb("epsc", [128, 1], F32)
    kb.op("vector", lambda v: v.memset(ctx.epsc[:, :], 1e-5), writes=["epsc"])



def emit_l3_ffn(ctx, xT, xT_keys, out_dram, wnames, TBLK=512):
    kb = ctx.kb
    nblk = T_FULL // TBLK
    A = kb.sb("l3f_A", [128, 32, TBLK], BF16)
    B = kb.sb("l3f_B", [128, 32, TBLK], BF16)
    Ct = kb.sb("l3f_C", [128, TBLK], BF16)
    OF = kb.sb("l3f_OF", [128, 8, TBLK], F32)
    OT = kb.sb("l3f_OT", [128, TBLK // 128, 1024], F32)
    w8 = [kb.sb(f"l3f_w8_{i}", [128, 8, 128], BF16) for i in range(2)]
    w32 = [kb.sb(f"l3f_w32_{i}", [128, 32, 128], BF16) for i in range(2)]

    def wslice(dram, K, m):
        return dram[:, m * 128:(m + 1) * 128].rearrange("(kt p) m -> p kt m", p=128)

    for b in range(nblk):
        t0 = b * TBLK
        xTb = [xT[:, kt, t0:t0 + TBLK] for kt in range(8)]
        for m in range(32):
            wi = w8[m % 2]
            wikey = f"l3f_w8_{m % 2}"
            kb.dma("qpool", wi[:, :, :], wslice(wnames["w1T"], 1024, m), writes=[wikey], tag=f"l1w{b}_{m}")
            bank, bkey = ctx.banks.nf()
            mm_group(kb, bank[:, 0:TBLK], bkey,
                     [(wi[:, kt, :], xTb[kt]) for kt in range(8)],
                     reads=[wikey] + xT_keys, tag=f"l1mm{b}_{m}")
            kb.op("scalar", lambda s, bank=bank, m=m, b=b:
                  s.activation(A[:, m, :], bank[:, 0:TBLK], AF.Silu, scale=wnames["s1"][:, m:m + 1]),
                  reads=[bkey, "l3f_scales"], writes=[f"l3fA{b}_{m}"], tag=f"l1ev{b}_{m}")
            kb.dma("qpool", wi[:, :, :], wslice(wnames["w2T"], 1024, m), writes=[wikey], tag=f"l1w2{b}_{m}")
            bank2, bkey2 = ctx.banks.nf()
            mm_group(kb, bank2[:, 0:TBLK], bkey2,
                     [(wi[:, kt, :], xTb[kt]) for kt in range(8)],
                     reads=[wikey] + xT_keys, tag=f"l1mm2{b}_{m}")
            kb.op("scalar", lambda s, bank2=bank2, m=m:
                  s.activation(Ct[:, :], bank2[:, 0:TBLK], AF.Identity, scale=wnames["s2"][:, m:m + 1]),
                  reads=[bkey2, "l3f_scales"], writes=["l3fC"], tag=f"l1ev2{b}_{m}")
            kb.op("vector", lambda v, m=m:
                  v.tensor_tensor(B[:, m, :], A[:, m, :], Ct[:, :], ALU.mult),
                  reads=[f"l3fA{b}_{m}", "l3fC"], writes=[f"l3fB{b}_{m}"], tag=f"l1mu{b}_{m}")
        for (wname, sname, src, skeys, dst, dkeys) in [
                ("h1T", "sh1", B, [f"l3fB{b}_{k}" for k in range(32)], A, [f"l3fA2_{b}_%d" % m for m in range(32)]),
                ("h2T", "sh2", A, [f"l3fA2_{b}_{k}" for k in range(32)], B, [f"l3fB2_{b}_%d" % m for m in range(32)])]:
            for m in range(32):
                wi = w32[m % 2]
                wikey = f"l3f_w32_{m % 2}"
                kb.dma("qpool", wi[:, :, :], wslice(wnames[wname], 4096, m), writes=[wikey],
                       tag=f"{wname}w{b}_{m}")
                bank, bkey = ctx.banks.nf()
                mm_group(kb, bank[:, 0:TBLK], bkey,
                         [(wi[:, kt, :], src[:, kt, :]) for kt in range(32)],
                         reads=[wikey] + skeys, tag=f"{wname}mm{b}_{m}")
                kb.op("scalar", lambda s, bank=bank, m=m, sname=sname, dst=dst:
                      s.activation(dst[:, m, :], bank[:, 0:TBLK], AF.Gelu, scale=wnames[sname][:, m:m + 1]),
                      reads=[bkey, "l3f_scales"], writes=[dkeys[m]], tag=f"{wname}ev{b}_{m}")
        for m in range(8):
            wi = w32[m % 2]
            wikey = f"l3f_w32_{m % 2}"
            kb.dma("qpool", wi[:, :, :], wslice(wnames["w3T"], 4096, m), writes=[wikey], tag=f"w3w{b}_{m}")
            bank, bkey = ctx.banks.nf()
            mm_group(kb, bank[:, 0:TBLK], bkey,
                     [(wi[:, kt, :], B[:, kt, :]) for kt in range(32)],
                     reads=[wikey] + [f"l3fB2_{b}_{k}" for k in range(32)], tag=f"w3mm{b}_{m}")
            kb.op("scalar", lambda s, bank=bank, m=m:
                  s.activation(OF[:, m, :], bank[:, 0:TBLK], AF.Identity, scale=wnames["s3"][:, m:m + 1]),
                  reads=[bkey, "l3f_scales"], writes=[f"l3fOF{b}_{m}"], tag=f"w3ev{b}_{m}")
        for tt in range(TBLK // 128):
            for fh in range(2):
                bank, bkey = ctx.banks.nf()
                items = [(bank[:, i * 128:(i + 1) * 128], OF[:, fh * 4 + i, tt * 128:(tt + 1) * 128])
                         for i in range(4)]
                transpose_group(kb, items, ctx.idf[:, :], "idf",
                                reads=[f"l3fOF{b}_{m}" for m in range(8)], writes=[bkey],
                                tag=f"otr{b}_{tt}_{fh}")
                kb.op("vector", lambda v, bank=bank, tt=tt, fh=fh:
                      v.tensor_copy(OT[:, tt, fh * 512:(fh + 1) * 512], bank[:, 0:512]),
                      reads=[bkey], writes=[f"l3fOT{b}_{tt}_{fh}"], tag=f"otc{b}_{tt}_{fh}")
        kb.dma("qsp", out_dram[t0:t0 + TBLK, :].rearrange("(nt p) d -> p nt d", p=128), OT[:, :, :],
               reads=[f"l3fOT{b}_{tt}_{fh}" for tt in range(TBLK // 128) for fh in range(2)],
               tag=f"l3f_st{b}")


def stage_l3_ffn(weights):
    ctx = Ctx("l3ffn")
    kb = ctx.kb
    ctx.banks = Banks(kb)
    emit_eps_consts(ctx)
    ctx.const_ident()
    xn_in = ctx.inp("xn", [T_FULL, D], F32, lambda c, e: e["xn"][c])

    w = {}
    for nm, arr in [("w1T", weights["w1T"]), ("w2T", weights["w2T"]),
                    ("h1T", weights["h1T"]), ("h2T", weights["h2T"]), ("w3T", weights["w3T"])]:
        w[nm] = ctx.inp(f"l3f_{nm}", list(arr.shape), F32, lambda c, e, a=arr: a)
    sc_host = np.zeros((128, 4 * 32 + 8), np.float32)
    off = 0
    scmap = {}
    for nm, arr in [("s1", weights["s1"]), ("s2", weights["s2"]), ("sh1", weights["sh1"]),
                    ("sh2", weights["sh2"]), ("s3", weights["s3"])]:
        nmt = arr.shape[0] // 128
        sc_host[:, off:off + nmt] = arr.reshape(nmt, 128).T
        scmap[nm] = (off, nmt)
        off += nmt
    sc_in = ctx.inp("l3f_sc", [128, off], F32, lambda c, e, a=sc_host: a)
    scs = kb.sb("l3f_scs", [128, off], F32)
    kb.dma("qpool", scs[:, :], sc_in[:, :], writes=["l3f_scales"], tag="ldsc")
    for nm, (o, n) in scmap.items():
        w[nm] = scs[:, o:o + n]

    out = ctx.out("f", [T_FULL, D], F32)

    xnb = kb.sb("xnb", [128, NTT, D], BF16)
    kb.dma("qpool", xnb[:, :, :], xn_in.rearrange("(nt p) d -> p nt d", p=128),
           writes=["xnb"], tag="ldxn")
    xT = kb.sb("xT", [128, NFT, T_FULL], BF16)
    emit_transpose_tokmajor_to_featmajor(ctx, xnb, ["xnb"], xT, "xT")
    xT_keys = featmajor_keys("xT")
    emit_l3_ffn(ctx, xT, xT_keys, out, w)
    n = kb.build()
    return ctx, n



def make_l3_rope_tables():
    DH = 64
    inv = 1.0 / (10000.0 ** (np.arange(0, DH, 2, dtype=np.float64) / DH))
    t = np.arange(2047, dtype=np.float64)
    fr = t[:, None] * inv[None, :]
    cos = np.concatenate([np.cos(fr), np.cos(fr)], axis=-1)
    sin = np.concatenate([np.sin(fr), np.sin(fr)], axis=-1)
    cosT = np.ones((2048, 64), np.float32)
    sinT = np.zeros((2048, 64), np.float32)
    cosT[1:] = cos
    sinT[1:] = sin
    return cosT, sinT


def emit_ln_tile(ctx, xi, xikey, out_ap, out_key, tag, gamB=None, betB=None, wbuf=None, wkey=None):
    kb = ctx.kb
    st = ctx.ln_stats
    mvc = ctx.ln_mvc
    kb.op("vector", lambda v: v.bn_stats(st[:, 0:6], xi[:, 0:512]), reads=[xikey], writes=[f"{tag}s0"])
    kb.op("vector", lambda v: v.bn_stats(st[:, 6:12], xi[:, 512:1024]), reads=[xikey], writes=[f"{tag}s1"])
    kb.op("vector", lambda v: v.bn_aggr(mvc[:, 0:2], st[:, :]), reads=[f"{tag}s0", f"{tag}s1"], writes=[f"{tag}mv"])
    kb.op("vector", lambda v: v.tensor_scalar(mvc[:, 2:3], mvc[:, 0:1], -1.0, None, ALU.mult),
          reads=[f"{tag}mv"], writes=[f"{tag}nm"])
    kb.op("scalar", lambda s: s.activation(mvc[:, 3:4], mvc[:, 1:2], AF.Sqrt, bias=ctx.epsc[:, :]),
          reads=[f"{tag}mv", "epsc"], writes=[f"{tag}sq"])
    kb.op("vector", lambda v: v.reciprocal(mvc[:, 3:4], mvc[:, 3:4]), reads=[f"{tag}sq"], writes=[f"{tag}rs"])
    if gamB is None:
        kb.op("vector", lambda v: v.tensor_scalar(out_ap, xi[:, :], mvc[:, 2:3], mvc[:, 3:4], ALU.add, ALU.mult),
              reads=[xikey, f"{tag}nm", f"{tag}rs"], writes=[out_key])
    else:
        kb.op("vector", lambda v: v.tensor_scalar(wbuf[:, :], xi[:, :], mvc[:, 2:3], mvc[:, 3:4], ALU.add, ALU.mult),
              reads=[xikey, f"{tag}nm", f"{tag}rs"], writes=[wkey])
        kb.op("vector", lambda v: v.tensor_tensor(wbuf[:, :], wbuf[:, :], gamB[:, :], ALU.mult),
              reads=[wkey, "gamB"], writes=[wkey])
        kb.op("vector", lambda v: v.tensor_tensor(out_ap, wbuf[:, :], betB[:, :], ALU.add),
              reads=[wkey, "betB"], writes=[out_key])


def emit_transpose_tile(ctx, src_ap, src_keys, dst, dst_tag, tile_i, nft=8, ident=None, ikey="idb"):
    kb = ctx.kb
    ident = ident if ident is not None else ctx.idb
    for hh in range(0, nft, 4):
        bank, bkey = ctx.banks.nb()
        n = min(4, nft - hh)
        items = [(bank[:, j * 128:(j + 1) * 128], src_ap[:, (hh + j) * 128:(hh + j + 1) * 128])
                 for j in range(n)]
        transpose_group(kb, items, ident[:, :], ikey, reads=src_keys, writes=[bkey],
                        tag=f"{dst_tag}tr{tile_i}_{hh}")
        kb.op("vector", lambda v, bank=bank, hh=hh, n=n, tile_i=tile_i:
              v.tensor_copy(dst[:, hh:hh + n, tile_i * 128:(tile_i + 1) * 128], bank[:, 0:n * 128]),
              reads=[bkey], writes=[f"{dst_tag}_{hh}_t{tile_i}"], tag=f"{dst_tag}trc{tile_i}_{hh}")


def stage_l3_attn(weights):
    ctx = Ctx("l3attn")
    kb = ctx.kb
    ctx.banks = Banks(kb, nf=2, nb=2)
    big = kb.ps("big", [128, 2048], F32)
    emit_eps_consts(ctx)
    ctx.const_ident()
    ctx.ln_stats = kb.sb("lnst", [128, 12], F32)
    ctx.ln_mvc = kb.sb("lnmv", [128, 4], F32)

    xn_in = ctx.inp("xn", [T_FULL, D], F32, lambda c, e: e["xn"][c])
    wqkv_in = ctx.inp("l3a_wqkv", [1024, 3072], F32, lambda c, e: weights["w_qkv"])
    wout_in = ctx.inp("l3a_wout", [1024, 1024], F32, lambda c, e: weights["w_out"])
    gam_in = ctx.inp("l3a_gam", [1024], F32, lambda c, e: weights["gamma"])
    bet_in = ctx.inp("l3a_bet", [1024], F32, lambda c, e: weights["beta"])
    cosT, sinT = make_l3_rope_tables()
    cos_in = ctx.inp("l3a_cos", [2048, 64], F32, lambda c, e: cosT)
    sin_in = ctx.inp("l3a_sin", [2048, 64], F32, lambda c, e: sinT)
    out = ctx.out("a", [T_FULL, D], F32)

    gamB = kb.sb("gamB", [128, 1024], F32)
    betB = kb.sb("betB", [128, 1024], F32)
    kb.dma("qpool", gamB[:, :], bass.AP(gam_in.tensor, 0, [[0, 128], [1, 1024]]), writes=["gamB"])
    kb.dma("qpool", betB[:, :], bass.AP(bet_in.tensor, 0, [[0, 128], [1, 1024]]), writes=["betB"])
    ropc = kb.sb("ropc", [128, 16, 64], BF16)
    rops = kb.sb("rops", [128, 16, 64], BF16)
    kb.dma("qpool", ropc[:, :, :],
           bass.AP(cos_in.tensor, 0, [[64, 128], [128 * 64, 16], [1, 64]]), writes=["ropc"])
    kb.dma("qpool", rops[:, :, :],
           bass.AP(sin_in.tensor, 0, [[64, 128], [128 * 64, 16], [1, 64]]), writes=["rops"])

    xt = [kb.sb(f"xt{i}", [128, 1024], F32) for i in range(2)]
    xw = kb.sb("xw", [128, 1024], F32)
    xlb = kb.sb("xlb", [128, 1024], BF16)
    xlnT = kb.sb("xlnT", [128, 8, 2048], BF16)
    for i in range(NTT):
        xi = xt[i % 2]
        xikey = f"xt{i % 2}"
        kb.dma("qsp", xi[:, :], xn_in[i * 128:(i + 1) * 128, :], writes=[xikey], tag=f"ldx{i}")
        emit_ln_tile(ctx, xi, xikey, xlb[:, :], "xlb", f"ln{i}", gamB, betB, xw, "xw")
        emit_transpose_tile(ctx, xlb, ["xlb"], xlnT, "xlnT", i)
    xlnT_keys = [f"xlnT_{hh}_t{i}" for hh in range(0, 8, 4) for i in range(NTT)]

    wv = kb.sb("wv", [128, 8, 1024], BF16)
    kb.dma("qpool", wv[:, :, :], wqkv_in[:, 2048:3072].rearrange("(kt p) n -> p kt n", p=128),
           writes=["wv"], tag="ldwv")
    V = kb.sb("V", [128, 16, 1024], BF16)
    for i in range(NTT):
        for nb_ in range(2):
            bank, bkey = ctx.banks.nf()
            mm_group(kb, bank[:, :], bkey,
                     [(xlnT[:, k, i * 128:(i + 1) * 128], wv[:, k, nb_ * 512:(nb_ + 1) * 512]) for k in range(8)],
                     reads=["wv"] + xlnT_keys, tag=f"vmm{i}_{nb_}")
            kb.op("scalar", lambda s, bank=bank, i=i, nb_=nb_:
                  s.activation(V[:, i, nb_ * 512:(nb_ + 1) * 512], bank[:, :], AF.Copy),
                  reads=[bkey], writes=[f"V{i}_{nb_}"], tag=f"vev{i}_{nb_}")
    V_keys = [f"V{i}_{nb_}" for i in range(NTT) for nb_ in range(2)]

    wqk = kb.sb("wqk", [128, 8, 512], BF16)
    Q4 = kb.sb("Q4", [128, 16, 256], BF16)
    K4 = kb.sb("K4", [128, 16, 256], BF16)
    Q4r, K4r = Q4, K4
    rot = kb.sb("rot", [128, 256], BF16)
    tmp = kb.sb("tmpr", [128, 256], BF16)
    QT = kb.sb("QT", [128, 2, 2048], BF16)
    KT = kb.sb("KT", [128, 2, 2048], BF16)
    P = kb.sb("P", [128, 2048], BF16)
    PT = kb.sb("PT", [128, 16, 512], BF16)
    mx4 = kb.sb("mx4", [128, 4], F32)
    mxc = kb.sb("mxc", [128, 2], F32)
    YT = kb.sb("YT", [128, 8, 2048], BF16)
    oute = kb.sb("oute", [128, 1024], F32)

    for g in range(4):
        kb.dma("qpool", wqk[:, :, 0:256],
               wqkv_in[:, g * 256:(g + 1) * 256].rearrange("(kt p) n -> p kt n", p=128),
               writes=["wqkq"], tag=f"ldwq{g}")
        kb.dma("qpool", wqk[:, :, 256:512],
               wqkv_in[:, 1024 + g * 256:1024 + (g + 1) * 256].rearrange("(kt p) n -> p kt n", p=128),
               writes=["wqkk"], tag=f"ldwk{g}")
        for i in range(NTT):
            bank, bkey = ctx.banks.nf()
            mm_group(kb, bank[:, :], bkey,
                     [(xlnT[:, k, i * 128:(i + 1) * 128], wqk[:, k, :]) for k in range(8)],
                     reads=["wqkq", "wqkk"] + xlnT_keys, tag=f"qkmm{g}_{i}")
            kb.op("scalar", lambda s, bank=bank, i=i: s.activation(Q4[:, i, :], bank[:, 0:256], AF.Copy),
                  reads=[bkey], writes=[f"Q4_{i}"], tag=f"qev{g}_{i}")
            kb.op("scalar", lambda s, bank=bank, i=i: s.activation(K4[:, i, :], bank[:, 256:512], AF.Copy),
                  reads=[bkey], writes=[f"K4_{i}"], tag=f"kev{g}_{i}")
        for (SRC, skey, DST, dkey, sgn) in [(Q4, "Q4", Q4r, "Q4r", 1.0), (K4, "K4", K4r, "K4r", 1.0)]:
            for i in range(NTT):
                kb.op("vector", lambda v, SRC=SRC, i=i:
                      v.tensor_scalar(rot.rearrange("p (h d) -> p h d", h=4)[:, :, 0:32],
                                      SRC.rearrange("p t (h d) -> p t h d", h=4)[:, i, :, 32:64],
                                      -1.0, None, ALU.mult),
                      reads=[f"{skey}_{i}"], writes=["rotA"], tag=f"rt1{g}{skey}{i}")
                kb.op("vector", lambda v, SRC=SRC, i=i:
                      v.tensor_copy(rot.rearrange("p (h d) -> p h d", h=4)[:, :, 32:64],
                                    SRC.rearrange("p t (h d) -> p t h d", h=4)[:, i, :, 0:32]),
                      reads=[f"{skey}_{i}"], writes=["rotB"], tag=f"rt2{g}{skey}{i}")
                for hx in range(4):
                    kb.op("vector", lambda v, SRC=SRC, i=i, hx=hx:
                          v.tensor_tensor(tmp[:, hx * 64:(hx + 1) * 64], SRC[:, i, hx * 64:(hx + 1) * 64],
                                          ropc[:, i, :], ALU.mult),
                          reads=[f"{skey}_{i}", "ropc"], writes=["tmpr"], tag=f"rt3{g}{skey}{i}_{hx}")
                    kb.op("vector", lambda v, i=i, hx=hx:
                          v.tensor_tensor(rot[:, hx * 64:(hx + 1) * 64], rot[:, hx * 64:(hx + 1) * 64],
                                          rops[:, i, :], ALU.mult),
                          reads=["rotA", "rotB", "rops"], writes=["rotm"], tag=f"rt4{g}{skey}{i}_{hx}")
                kb.op("vector", lambda v, DST=DST, i=i:
                      v.tensor_tensor(DST[:, i, :], tmp[:, :], rot[:, :], ALU.add),
                      reads=["tmpr", "rotm"], writes=[f"{dkey}_{i}"], tag=f"rt5{g}{skey}{i}")
        for (SRC, dkey, DSTT, dtag) in [(Q4r, "Q4r", QT, "QT"), (K4r, "K4r", KT, "KT")]:
            for hp in range(2):
                for i in range(0, NTT, 4):
                    bank, bkey = ctx.banks.nb()
                    items = [(bank[:, j * 128:(j + 1) * 128], SRC[:, i + j, hp * 128:(hp + 1) * 128])
                             for j in range(4)]
                    transpose_group(kb, items, ctx.idb[:, :], "idb",
                                    reads=[f"{dkey}_{i + j}" for j in range(4)], writes=[bkey],
                                    tag=f"{dtag}tr{g}_{hp}_{i}")
                    kb.op("vector", lambda v, bank=bank, hp=hp, i=i, DSTT=DSTT:
                          v.tensor_copy(DSTT[:, hp, i * 128:(i + 4) * 128], bank[:, 0:512]),
                          reads=[bkey], writes=[f"{dtag}{hp}"], tag=f"{dtag}trc{g}_{hp}_{i}")
        for hl in range(4):
            h = g * 4 + hl
            hp = hl // 2
            base = (hl % 2) * 64
            qkeys = [f"QT{hp}"]
            kkeys = [f"KT{hp}"]
            for qg in range(4):
                for ql in range(4):
                    qt = qg * 4 + ql
                    def mmsc(t, hp=hp, base=base, qt=qt):
                        last = None
                        for kb_ in range(4):
                            last = t.matmul(big[:, kb_ * 512:(kb_ + 1) * 512],
                                            QT[base:base + 64, hp, qt * 128:(qt + 1) * 128],
                                            KT[base:base + 64, hp, kb_ * 512:(kb_ + 1) * 512],
                                            start=True, stop=True)
                        return last
                    kb.op("tensor", mmsc, reads=qkeys + kkeys, writes=["big"], tag=f"sc{h}_{qt}")
                    for kb_ in range(4):
                        kb.op("vector", lambda v, kb_=kb_: v.tensor_reduce(
                            mx4[:, kb_:kb_ + 1], big[:, kb_ * 512:(kb_ + 1) * 512], AX.X, ALU.max),
                            reads=["big"], writes=[f"mx4_{kb_}"], tag=f"mx{h}_{qt}_{kb_}")
                    kb.op("vector", lambda v: v.tensor_reduce(mxc[:, 0:1], mx4[:, :], AX.X, ALU.max),
                          reads=[f"mx4_{k}" for k in range(4)], writes=["mx"], tag=f"mxr{h}_{qt}")
                    kb.op("vector", lambda v: v.tensor_scalar(mxc[:, 0:1], mxc[:, 0:1], -0.125, None, ALU.mult),
                          reads=["mx"], writes=["mx"], tag=f"mxn{h}_{qt}")
                    kb.op("scalar", lambda s: s.activation(P[:, :], big[:, :], AF.Exp,
                                                           bias=mxc[:, 0:1], scale=0.125,
                                                           accum_out=mxc[:, 1:2]),
                          reads=["big", "mx"], writes=["P", "den"], tag=f"exp{h}_{qt}")
                    kb.op("vector", lambda v: v.reciprocal(mxc[:, 1:2], mxc[:, 1:2]),
                          reads=["den"], writes=["rdenc"], tag=f"rd{h}_{qt}")
                    kb.op("vector", lambda v: v.tensor_scalar(P[:, :], P[:, :], mxc[:, 1:2], None, ALU.mult),
                          reads=["P", "rdenc"], writes=["P"], tag=f"pn{h}_{qt}")
                    for hh in range(0, 16, 4):
                        bank, bkey = ctx.banks.nb()
                        items = [(bank[:, j * 128:(j + 1) * 128], P[:, (hh + j) * 128:(hh + j + 1) * 128])
                                 for j in range(4)]
                        transpose_group(kb, items, ctx.idb[:, :], "idb", reads=["P"], writes=[bkey],
                                        tag=f"ptr{h}_{qt}_{hh}")
                        kb.op("vector", lambda v, bank=bank, hh=hh, ql=ql:
                              v.tensor_copy(PT[:, hh:hh + 4, ql * 128:(ql + 1) * 128], bank[:, 0:512]),
                              reads=[bkey], writes=[f"PT{hh}_{ql}"], tag=f"ptc{h}_{qt}_{hh}")
                bank, bkey = ctx.banks.nf()
                def mmav(t, h=h, bank=bank):
                    last = None
                    for kt in range(16):
                        last = t.matmul(bank[0:64, :], V[:, kt, h * 64:(h + 1) * 64], PT[:, kt, :],
                                        start=(kt == 0), stop=(kt == 15))
                    return last
                kb.op("tensor", mmav,
                      reads=V_keys + [f"PT{hh}_{ql}" for hh in range(0, 16, 4) for ql in range(4)],
                      writes=[bkey], tag=f"av{h}_{qg}")
                kb.op("vector", lambda v, bank=bank, h=h, qg=qg:
                      v.tensor_copy(YT[(h % 2) * 64:(h % 2) * 64 + 64, h // 2, qg * 512:(qg + 1) * 512],
                                    bank[0:64, :]),
                      reads=[bkey], writes=[f"YT{h}_{qg}"], tag=f"ytc{h}_{qg}")

    wo = wv
    kb.dma("qpool", wo[:, :, :], wout_in.rearrange("(kt p) n -> p kt n", p=128), writes=["wv"], tag="ldwo")
    YT_keys = [f"YT{h}_{qg}" for h in range(16) for qg in range(4)]
    for i in range(NTT):
        for nb_ in range(2):
            bank, bkey = ctx.banks.nf()
            mm_group(kb, bank[:, :], bkey,
                     [(YT[:, k, i * 128:(i + 1) * 128], wo[:, k, nb_ * 512:(nb_ + 1) * 512]) for k in range(8)],
                     reads=["wv"] + YT_keys, tag=f"omm{i}_{nb_}")
            kb.op("scalar", lambda s, bank=bank, nb_=nb_: s.activation(oute[:, nb_ * 512:(nb_ + 1) * 512],
                                                                      bank[:, :], AF.Copy),
                  reads=[bkey], writes=[f"oute{nb_}"], tag=f"oev{i}_{nb_}")
        kb.dma("qsp", out[i * 128:(i + 1) * 128, :], oute[:, :], reads=["oute0", "oute1"], tag=f"sto{i}")
    n = kb.build()
    return ctx, n



def make_fft_consts():
    D_ = 1024
    k = np.arange(D_)
    ang = 2.0 * np.pi * np.outer(k, k % D_) / D_
    ij = (np.outer(k, k) % D_).astype(np.float64)
    a = 2.0 * np.pi * ij / D_
    Fr = np.cos(a).astype(np.float32)
    Fi_ = (-np.sin(a)).astype(np.float32)
    Fir = (np.cos(a) / D_).astype(np.float32)
    Fii = (np.sin(a) / D_).astype(np.float32)
    return Fr, Fi_, Fir, Fii


def stage_l1_ffn(weights):
    ctx = Ctx("l1ffn")
    kb = ctx.kb
    ctx.banks = Banks(kb, nf=6, nb=2)
    emit_eps_consts(ctx)
    ctx.const_ident()
    xn_in = ctx.inp("xn", [T_FULL, D], F32, lambda c, e: e["xn"][c])
    Fr_h, Fi_h, Fir_h, Fii_h = make_fft_consts()
    cons = {}
    for nm, arr in [("Fr", Fr_h), ("Fi", Fi_h), ("Fir", Fir_h), ("Fii", Fii_h)]:
        inp = ctx.inp(f"c_{nm}", [1024, 1024], F32, lambda c, e, a=arr: a)
        sb = kb.sb(f"sb{nm}", [128, 8, 1024], BF16)
        kb.dma("qpool", sb[:, :, :], inp.rearrange("(kt p) n -> p kt n", p=128), writes=[nm], tag=f"ld{nm}")
        cons[nm] = sb
    wrT_in = ctx.inp("l1f_wrT", [1024, 1024], F32, lambda c, e: weights["wrT"])
    wiT_in = ctx.inp("l1f_wiT", [1024, 1024], F32, lambda c, e: weights["wiT"])
    bias_in = ctx.inp("l1f_bias", [1024], F32, lambda c, e: weights["bias"])
    out = ctx.out("f", [T_FULL, D], F32)

    wrT = kb.sb("wrT", [128, 8, 1024], BF16)
    wiT = kb.sb("wiT", [128, 8, 1024], BF16)
    kb.dma("qpool", wrT[:, :, :], wrT_in.rearrange("(kt p) n -> p kt n", p=128), writes=["wrT"], tag="ldwrT")
    kb.dma("qpool", wiT[:, :, :], wiT_in.rearrange("(kt p) n -> p kt n", p=128), writes=["wiT"], tag="ldwiT")
    biasB = kb.sb("biasB", [128, 1024], F32)
    kb.dma("qpool", biasB[:, :], bass.AP(bias_in.tensor, 0, [[0, 128], [1, 1024]]), writes=["biasB"])

    ptmp = kb.sb("ptmp", [128, 512], F32)
    Cre = kb.sb("Cre", [128, 8, 1024], BF16)
    Cim = kb.sb("Cim", [128, 8, 1024], BF16)
    Mb = kb.sb("Mb", [128, 8, 1024], BF16)

    def prod2(lhs1, lk1, rhs1, rk1, lhs2, lk2, rhs2, rk2, dstsb, dkey, alu, tag):
        for dt in range(8):
            for nb_ in range(2):
                b1, k1 = ctx.banks.nf()
                mm_group(kb, b1[:, :], k1,
                         [(lhs1[:, k, dt * 128:(dt + 1) * 128], rhs1[:, k, nb_ * 512:(nb_ + 1) * 512])
                          for k in range(8)], reads=[lk1, rk1], tag=f"{tag}a{dt}_{nb_}")
                b2, k2 = ctx.banks.nf()
                mm_group(kb, b2[:, :], k2,
                         [(lhs2[:, k, dt * 128:(dt + 1) * 128], rhs2[:, k, nb_ * 512:(nb_ + 1) * 512])
                          for k in range(8)], reads=[lk2, rk2], tag=f"{tag}b{dt}_{nb_}")
                kb.op("scalar", lambda s, b2=b2: s.activation(ptmp[:, :], b2[:, :], AF.Copy),
                      reads=[k2], writes=["ptmp"], tag=f"{tag}c{dt}_{nb_}")
                kb.op("vector", lambda v, b1=b1, dt=dt, nb_=nb_:
                      v.tensor_tensor(dstsb[:, dt, nb_ * 512:(nb_ + 1) * 512], b1[:, :], ptmp[:, :], alu),
                      reads=[k1, "ptmp"], writes=[dkey], tag=f"{tag}e{dt}_{nb_}")

    prod2(wrT, "wrT", cons["Fir"], "Fir", wiT, "wiT", cons["Fii"], "Fii", Cre, "Cre", ALU.subtract, "cre")
    prod2(wrT, "wrT", cons["Fii"], "Fii", wiT, "wiT", cons["Fir"], "Fir", Cim, "Cim", ALU.add, "cim")
    prod2(cons["Fr"], "Fr", Cre, "Cre", cons["Fi"], "Fi", Cim, "Cim", Mb, "Mb", ALU.subtract, "m")

    xt = [kb.sb(f"xt{i}", [128, 1024], F32) for i in range(2)]
    xlb = kb.sb("xlb", [128, 1024], BF16)
    xnT = kb.sb("xnT", [128, 8, 2048], BF16)
    oute = kb.sb("oute", [128, 1024], F32)
    for i in range(NTT):
        xi = xt[i % 2]
        xikey = f"xt{i % 2}"
        kb.dma("qsp", xi[:, :], xn_in[i * 128:(i + 1) * 128, :], writes=[xikey], tag=f"ldx{i}")
        kb.op("vector", lambda v, xi=xi: v.tensor_copy(xlb[:, :], xi[:, :]),
              reads=[xikey], writes=["xlb"], tag=f"cb{i}")
        emit_transpose_tile(ctx, xlb, ["xlb"], xnT, "xnT", i)
    xnT_keys = [f"xnT_{hh}_t{i}" for hh in range(0, 8, 4) for i in range(NTT)]
    for i in range(NTT):
        for nb_ in range(2):
            bank, bkey = ctx.banks.nf()
            mm_group(kb, bank[:, :], bkey,
                     [(xnT[:, k, i * 128:(i + 1) * 128], Mb[:, k, nb_ * 512:(nb_ + 1) * 512]) for k in range(8)],
                     reads=["Mb"] + xnT_keys, tag=f"ymm{i}_{nb_}")
            kb.op("vector", lambda v, bank=bank, nb_=nb_:
                  v.tensor_tensor(oute[:, nb_ * 512:(nb_ + 1) * 512], bank[:, :],
                                  biasB[:, nb_ * 512:(nb_ + 1) * 512], ALU.add),
                  reads=[bkey, "biasB"], writes=[f"oute{nb_}"], tag=f"yev{i}_{nb_}")
        kb.dma("qsp", out[i * 128:(i + 1) * 128, :], oute[:, :], reads=["oute0", "oute1"], tag=f"sty{i}")
    n = kb.build()
    return ctx, n



def stage_l2_attn(weights, debug=False):
    ctx = Ctx("l2attn")
    kb = ctx.kb
    ctx.banks = Banks(kb, nf=6, nb=2)
    emit_eps_consts(ctx)
    ctx.const_ident()
    H, DH = 16, 64
    xn_in = ctx.inp("xn", [T_FULL, D], F32, lambda c, e: e["xn"][c])
    wq_in = ctx.inp("l2a_wq", [1024, 1024], F32, lambda c, e: weights["Wq"])
    wk_in = ctx.inp("l2a_wk", [1024, 1024], F32, lambda c, e: weights["Wk"])
    wv_in = ctx.inp("l2a_wv", [1024, 1024], F32, lambda c, e: weights["Wv"])
    wo_in = ctx.inp("l2a_wo", [1024, 1024], F32, lambda c, e: weights["Wo"])
    bq_in = ctx.inp("l2a_bq", [1024], F32, lambda c, e: weights["bq"])
    bk_in = ctx.inp("l2a_bk", [1024], F32, lambda c, e: weights["bk"])
    bv_in = ctx.inp("l2a_bv", [1024], F32, lambda c, e: weights["bv"])
    bo_in = ctx.inp("l2a_bo", [1024], F32, lambda c, e: weights["bo"])
    maskh = np.triu(np.ones((128, 128), np.float32))
    mask_in = ctx.inp("l2a_mask", [128, 128], F32, lambda c, e: maskh)
    out = ctx.out("a", [T_FULL, D], F32)

    maskb = kb.sb("maskb", [128, 128], BF16)
    kb.dma("qpool", maskb[:, :], mask_in[:, :], writes=["maskb"])
    ones = kb.sb("ones", [128, 1], BF16)
    kb.op("vector", lambda v: v.memset(ones[:, :], 1.0), writes=["ones"])
    bqc = kb.sb("bqc", [128, 8], F32)
    bkc = kb.sb("bkc", [128, 8], F32)
    kb.dma("qpool", bqc[:, :], bq_in.rearrange("(f p) -> p f", p=128), writes=["bqc"],
           allow_slow_non_contiguous=True)
    kb.dma("qpool", bkc[:, :], bk_in.rearrange("(f p) -> p f", p=128), writes=["bkc"],
           allow_slow_non_contiguous=True)
    bvB = kb.sb("bvB", [128, 1024], F32)
    boB = kb.sb("boB", [128, 1024], F32)
    kb.dma("qpool", bvB[:, :], bass.AP(bv_in.tensor, 0, [[0, 128], [1, 1024]]), writes=["bvB"])
    kb.dma("qpool", boB[:, :], bass.AP(bo_in.tensor, 0, [[0, 128], [1, 1024]]), writes=["boB"])

    xt = [kb.sb(f"xt{i}", [128, 1024], F32) for i in range(2)]
    xlb = kb.sb("xlb", [128, 1024], BF16)
    xnT = kb.sb("xnT", [128, 8, 2048], BF16)
    for i in range(NTT):
        xi = xt[i % 2]
        xikey = f"xt{i % 2}"
        kb.dma("qsp", xi[:, :], xn_in[i * 128:(i + 1) * 128, :], writes=[xikey], tag=f"ldx{i}")
        kb.op("vector", lambda v, xi=xi: v.tensor_copy(xlb[:, :], xi[:, :]),
              reads=[xikey], writes=["xlb"], tag=f"cb{i}")
        emit_transpose_tile(ctx, xlb, ["xlb"], xnT, "xnT", i)
    xnT_keys = [f"xnT_{hh}_t{i}" for hh in range(0, 8, 4) for i in range(NTT)]

    wqw = kb.sb("wqw", [128, 8, 1024], BF16)
    QT = kb.sb("QTl2", [128, 8, 2048], BF16)
    KT = kb.sb("KTl2", [128, 8, 2048], BF16)
    phw = kb.sb("phw", [128, 512], F32)
    phw2 = kb.sb("phw2", [128, 512], F32)
    for (win, wkey, DST, dtag, bc) in [(wq_in, "wq", QT, "QTl2", bqc), (wk_in, "wk", KT, "KTl2", bkc)]:
        kb.dma("qpool", wqw[:, :, :], win.rearrange("(kt p) n -> p kt n", p=128), writes=["wqw"], tag=f"ld{wkey}")
        for mt in range(8):
            for nb_ in range(4):
                bank, bkey = ctx.banks.nf()
                mm_group(kb, bank[:, :], bkey,
                         [(wqw[:, k, mt * 128:(mt + 1) * 128], xnT[:, k, nb_ * 512:(nb_ + 1) * 512])
                          for k in range(8)],
                         reads=["wqw"] + xnT_keys, tag=f"{wkey}mm{mt}_{nb_}")
                kb.op("vector", lambda v, bank=bank, bc=bc, mt=mt:
                      v.tensor_scalar(phw[:, :], bank[:, :], bc[:, mt:mt + 1], 0.0, ALU.add, ALU.max),
                      reads=[bkey, "bqc", "bkc"], writes=["phw"], tag=f"{wkey}r{mt}_{nb_}")
                kb.op("vector", lambda v, bank=bank, bc=bc, mt=mt:
                      v.tensor_scalar(phw2[:, :], bank[:, :], bc[:, mt:mt + 1], 0.0, ALU.add, ALU.min),
                      reads=[bkey, "bqc", "bkc"], writes=["phw2"], tag=f"{wkey}n{mt}_{nb_}")
                kb.op("scalar", lambda s: s.activation(phw2[:, :], phw2[:, :], AF.Exp),
                      reads=["phw2"], writes=["phw2e"], tag=f"{wkey}e{mt}_{nb_}")
                kb.op("vector", lambda v, DST=DST, mt=mt, nb_=nb_:
                      v.tensor_tensor(DST[:, mt, nb_ * 512:(nb_ + 1) * 512], phw[:, :], phw2[:, :], ALU.add),
                      reads=["phw", "phw2e"], writes=[dtag], tag=f"{wkey}p{mt}_{nb_}")
    V = kb.sb("Vl2", [128, 16, 1024], BF16)
    kb.dma("qpool", wqw[:, :, :], wv_in.rearrange("(kt p) n -> p kt n", p=128), writes=["wqw"], tag="ldwv")
    for i in range(NTT):
        for nb_ in range(2):
            bank, bkey = ctx.banks.nf()
            mm_group(kb, bank[:, :], bkey,
                     [(xnT[:, k, i * 128:(i + 1) * 128], wqw[:, k, nb_ * 512:(nb_ + 1) * 512]) for k in range(8)],
                     reads=["wqw"] + xnT_keys, tag=f"vmm{i}_{nb_}")
            kb.op("vector", lambda v, bank=bank, i=i, nb_=nb_:
                  v.tensor_tensor(V[:, i, nb_ * 512:(nb_ + 1) * 512], bank[:, :],
                                  bvB[:, nb_ * 512:(nb_ + 1) * 512], ALU.add),
                  reads=[bkey, "bvB"], writes=[f"V{i}_{nb_}"], tag=f"vev{i}_{nb_}")
    V_keys = [f"V{i}_{nb_}" for i in range(NTT) for nb_ in range(2)]

    Ktok = kb.sb("Ktok", [128, 16, 64], BF16)
    Sef = kb.sb("Sef", [128, 65], F32)
    Seb = kb.sb("Seb", [128, 65], BF16)
    Atb = kb.sb("Atb", [128, 128], BF16)
    numf = kb.sb("numf", [65, 128], F32)
    Y = kb.sb("Yl2", [128, 16, 1024], BF16)
    rec = kb.sb("recl2", [128, 1], F32)
    for h in range(H):
        ft, fo = (h * DH) // 128, (h * DH) % 128
        for i in range(0, NTT, 8):
            bank, bkey = ctx.banks.nb()
            items = [(bank[:, j * 128:j * 128 + 64],
                      KT[fo:fo + 64, ft, (i + j) * 128:(i + j + 1) * 128]) for j in range(8)]
            def tremit(t, items=items, fo=fo):
                last = None
                for o_, i_ in items:
                    last = t.transpose(o_, i_, ctx.idb[fo:fo + 64, fo:fo + 64])
                return last
            kb.op("tensor", tremit, reads=["KTl2", "idb"], writes=[bkey], tag=f"ktr{h}_{i}")
            kb.op("vector", lambda v, bank=bank, h=h, i=i:
                  v.tensor_copy(Ktok[:, i:i + 8, :],
                                bank[:, 0:1024].rearrange("p (j q) -> p j q", j=8)[:, :, 0:64]),
                  reads=[bkey], writes=["Ktok"], tag=f"ktc{h}_{i}")
        kb.op("vector", lambda v, fo=fo: v.memset(Sef[fo:fo + 64, :], 0.0), reads=["Sef"], writes=["Sef"], tag=f"sz{h}")
        kb.op("vector", lambda v, fo=fo: v.memset(Seb[fo:fo + 64, :], 0.0), reads=["Seb"], writes=["Seb"], tag=f"szb{h}")
        for c in range(NTT):
            qslice = QT[fo:fo + 64, ft, c * 128:(c + 1) * 128]
            kslice = KT[fo:fo + 64, ft, c * 128:(c + 1) * 128]
            bank, bkey = ctx.banks.nf()
            kb.op("tensor", lambda t, bank=bank, kslice=kslice, qslice=qslice:
                  t.matmul(bank[:, 0:128], kslice, qslice, start=True, stop=True),
                  reads=["QTl2", "KTl2"], writes=[bkey], tag=f"at{c}_{h}")
            kb.op("vector", lambda v, bank=bank: v.tensor_tensor(Atb[:, :], bank[:, 0:128], maskb[:, :], ALU.mult),
                  reads=[bkey, "maskb"], writes=["Atb"], tag=f"atm{c}_{h}")
            bank2, bkey2 = ctx.banks.nf()
            bank2b, bkey2b = ctx.banks.nf()
            def nummm(t, bank2=bank2, bank2b=bank2b, c=c, h=h, qslice=qslice, fo=fo):
                t.matmul(bank2[0:64, 0:128], V[:, c, h * 64:(h + 1) * 64], Atb[:, :], start=True, stop=False)
                t.matmul(bank2[0:64, 0:128], Seb[fo:fo + 64, 0:64], qslice, start=False, stop=True)
                t.matmul(bank2b[0:1, 0:128], ones[:, :], Atb[:, :], start=True, stop=False)
                return t.matmul(bank2b[0:1, 0:128], Seb[fo:fo + 64, 64:65], qslice, start=False, stop=True)
            kb.op("tensor", nummm, reads=V_keys + ["Atb", "Seb", "ones", "QTl2"],
                  writes=[bkey2, bkey2b], tag=f"num{c}_{h}")
            kb.op("scalar", lambda s, bank2=bank2: s.activation(numf[0:64, :], bank2[0:64, 0:128], AF.Copy),
                  reads=[bkey2], writes=["numf"], tag=f"nev{c}_{h}")
            kb.op("scalar", lambda s, bank2b=bank2b: s.activation(numf[64:65, :], bank2b[0:1, 0:128], AF.Copy),
                  reads=[bkey2b], writes=["numfd"], tag=f"nevd{c}_{h}")
            bank3, bkey3 = ctx.banks.nf()
            def semm(t, bank3=bank3, c=c, h=h, fo=fo):
                t.matmul(bank3[fo:fo + 64, 0:64], Ktok[:, c, :], V[:, c, h * 64:(h + 1) * 64],
                         start=True, stop=False)
                return t.matmul(bank3[fo:fo + 64, 64:65], Ktok[:, c, :], ones[:, :],
                                start=False, stop=True)
            kb.op("tensor", semm, reads=["Ktok"] + V_keys + ["ones"], writes=[bkey3], tag=f"se{c}_{h}")
            kb.op("vector", lambda v, bank3=bank3, fo=fo:
                  v.tensor_tensor(Sef[fo:fo + 64, :], Sef[fo:fo + 64, :], bank3[fo:fo + 64, 0:65], ALU.add),
                  reads=[bkey3, "Sef"], writes=["Sef"], tag=f"sea{c}_{h}")
            kb.op("vector", lambda v, fo=fo: v.tensor_copy(Seb[fo:fo + 64, :], Sef[fo:fo + 64, :]),
                  reads=["Sef"], writes=["Seb"], tag=f"seb{c}_{h}")
            bank4, bkey4 = ctx.banks.nf()
            kb.op("tensor", lambda t, bank4=bank4: t.transpose(bank4[:, 0:65], numf[0:65, :],
                                                               ctx.idf[0:65, 0:65]),
                  reads=["numf", "numfd", "idf"], writes=[bkey4], tag=f"ntr{c}_{h}")
            kb.op("vector", lambda v, bank4=bank4: v.tensor_scalar(rec[:, :], bank4[:, 64:65], 1e-6, None, ALU.add),
                  reads=[bkey4], writes=["recl2"], tag=f"nre{c}_{h}")
            kb.op("vector", lambda v: v.reciprocal(rec[:, :], rec[:, :]),
                  reads=["recl2"], writes=["recl2r"], tag=f"nrr{c}_{h}")
            kb.op("vector", lambda v, bank4=bank4, c=c, h=h:
                  v.tensor_scalar(Y[:, c, h * 64:(h + 1) * 64], bank4[:, 0:64], rec[:, :], None, ALU.mult),
                  reads=[bkey4, "recl2r"], writes=[f"Y{c}"], tag=f"ydv{c}_{h}")

    YT = xnT
    for i in range(NTT):
        emit_transpose_tile(ctx, Y[:, i, :], [f"Y{i}"], YT, "YTl2", i)
    YT_keys = [f"YTl2_{hh}_t{i}" for hh in range(0, 8, 4) for i in range(NTT)]
    kb.dma("qpool", wqw[:, :, :], wo_in.rearrange("(kt p) n -> p kt n", p=128), writes=["wqw"], tag="ldwo")
    oute = kb.sb("oute", [128, 1024], F32)
    for i in range(NTT):
        for nb_ in range(2):
            bank, bkey = ctx.banks.nf()
            mm_group(kb, bank[:, :], bkey,
                     [(YT[:, k, i * 128:(i + 1) * 128], wqw[:, k, nb_ * 512:(nb_ + 1) * 512]) for k in range(8)],
                     reads=["wqw"] + YT_keys, tag=f"omm{i}_{nb_}")
            kb.op("vector", lambda v, bank=bank, nb_=nb_:
                  v.tensor_tensor(oute[:, nb_ * 512:(nb_ + 1) * 512], bank[:, :],
                                  boB[:, nb_ * 512:(nb_ + 1) * 512], ALU.add),
                  reads=[bkey, "boB"], writes=[f"oute{nb_}"], tag=f"oev{i}_{nb_}")
        kb.dma("qsp", out[i * 128:(i + 1) * 128, :], oute[:, :], reads=["oute0", "oute1"], tag=f"sto{i}")
    if debug:
        dQT = ctx.out("dQT", [128, 8, 2048], F32)
        dKT = ctx.out("dKT", [128, 8, 2048], F32)
        dV = ctx.out("dV", [128, 16, 1024], F32)
        dY = ctx.out("dY", [128, 16, 1024], F32)
        kb.dma("qpool", dQT[:, :, :], QT[:, :, :], reads=["QTl2"], tag="dbg1")
        kb.dma("qpool", dKT[:, :, :], KT[:, :, :], reads=["KTl2"], tag="dbg2")
        kb.dma("qpool", dV[:, :, :], V[:, :, :], reads=V_keys, tag="dbg3")
        kb.dma("qpool", dY[:, :, :], Y[:, :, :], reads=[f"Y{c}" for c in range(16)], tag="dbg4")
    n = kb.build()
    return ctx, n



def make_l0ffn_consts():
    DIMH, WINDOW = 64, 512
    n = 2048
    half = np.arange(0, DIMH, 2, dtype=np.float64)
    inv_freq = 1.0 / 10000.0 ** (half / DIMH)
    t = np.arange(n, dtype=np.float64)
    fr = t[:, None] * inv_freq[None, :]
    fr = np.concatenate([fr, fr], axis=-1)
    base = (half + 0.4 * DIMH) / (1.4 * DIMH)
    power = (t - n // 2) / float(WINDOW // 2)
    scale = base[None, :] ** power[:, None]
    scale = np.concatenate([scale, scale], axis=-1)
    cos, sin = np.cos(fr), np.sin(fr)
    qc = (cos * scale).astype(np.float32)
    qs = (sin * scale).astype(np.float32)
    kc = (cos / scale).astype(np.float32)
    ks = (sin / scale).astype(np.float32)
    NEG = -1e30
    masks = []
    for w0 in (False, True):
        for a in range(4):
            i = (np.arange(128)[:, None] + a * 128)
            j = np.arange(1024)[None, :] - 512
            keep = (j <= i) & (j >= i - 512)
            if w0:
                keep = keep & (np.arange(1024)[None, :] >= 512)
            masks.append(np.where(keep, 0.0, NEG).astype(np.float32))
    return qc, qs, kc, ks, np.stack(masks)


def perm_l0ffn_w1(w1, b1):
    HID = 2730
    NP = 22
    w1p = np.zeros((1024, NP * 256), np.float32)
    b1p = np.zeros(NP * 256, np.float32)
    for c in range(NP):
        a0, a1 = c * 128, min((c + 1) * 128, HID)
        na = a1 - a0
        if na > 0:
            w1p[:, c * 256:c * 256 + na] = w1[:, a0:a1]
            b1p[c * 256:c * 256 + na] = b1[a0:a1]
            w1p[:, c * 256 + 128:c * 256 + 128 + na] = w1[:, HID + a0:HID + a1]
            b1p[c * 256 + 128:c * 256 + 128 + na] = b1[HID + a0:HID + a1]
    return w1p, b1p


def perm_l0ffn_w2(w2):
    HID = 2730
    w2p = np.zeros((2816, 1024), np.float32)
    w2p[:HID] = w2
    return w2p


def stage_l0_ffn(weights, debug=False):
    ctx = Ctx("l0ffn")
    kb = ctx.kb
    ctx.banks = Banks(kb, nf=4, nb=2)
    big = kb.ps("bigl0", [128, 1024], F32)
    emit_eps_consts(ctx)
    ctx.const_ident()
    H, DH, W = 8, 64, 512
    xn_in = ctx.inp("xn", [T_FULL, D], F32, lambda c, e: e["xn"][c])
    win = {}
    for l in range(2):
        for nm in ["wqkv", "wo", "w1p", "w2p", "qs", "ks", "lg", "lb", "b1p", "b2"]:
            arr = weights[f"{nm}{l}"]
            win[f"{nm}{l}"] = ctx.inp(f"l0f_{nm}{l}", list(arr.shape), F32, lambda c, e, a=arr: a)
    qc_h, qs_h, kc_h, ks_h, masks_h = make_l0ffn_consts()
    cons_in = {}
    for nm, arr in [("qc", qc_h), ("qsn", qs_h), ("kc", kc_h), ("ksn", ks_h)]:
        cons_in[nm] = ctx.inp(f"l0f_{nm}", [2048, 64], F32, lambda c, e, a=arr: a)
    mask_in = ctx.inp("l0f_masks", [4, 128, 1024], F32, lambda c, e: masks_h[:4])
    out = ctx.out("f", [T_FULL, D], F32)

    bscr = ctx.nc.dram_tensor("l0f_bscr", [16, 512], F32).ap()
    bnd = kb.sb("bndl0", [16, 512], F32)
    maskb = kb.sb("maskb", [128, 4, 1024], BF16)
    kb.dma("qpool", maskb[:, :, :], mask_in.rearrange("m p c -> p m c"), writes=["maskb"])
    X = kb.sb("Xl0", [128, 16, 1024], F32)
    kb.dma("qpool", X[:, :, :], xn_in.rearrange("(t p) d -> p t d", p=128), writes=["X"], tag="ldX")

    pool1 = kb.sb("pool1l0", [128, 11264], BF16)
    KTp = pool1[:, 0:10240].rearrange("p (f t) -> p f t", f=4)
    HFT = pool1[:, 0:11264].rearrange("p (k t) -> p k t", k=22)
    sh = kb.sb("shl0", [128, 1024], BF16)
    shf = kb.sb("shfl0", [128, 1024], F32)
    hT = kb.sb("hTl0", [128, 8, 512], BF16)
    wqk = kb.sb("wqkl0", [128, 8, 1024], BF16)
    wv2 = kb.sb("wv2l0", [128, 8, 512], BF16)
    pool2 = kb.sb("pool2l0", [128, 5632], BF16)
    w2m = [pool2[:, 0:2816].rearrange("p (k t) -> p k t", k=22),
           pool2[:, 2816:5632].rearrange("p (k t) -> p k t", k=22)]
    QKt = kb.sb("QKtl0", [128, 1024], BF16)
    Vp = kb.sb("Vpl0", [128, 16, 512], BF16)
    pool3 = kb.sb("pool3l0", [128, 4096], BF16)
    QT = pool3[:, 0:2048].rearrange("p (f t) -> p f t", f=4)
    sqw = kb.sb("sqwl0", [128, 1024], F32)
    rno = kb.sb("rnol0", [128, 16], F32)
    scm = kb.sb("scml0", [128, 1024], F32)
    P = kb.sb("Pl0", [128, 1024], BF16)
    PT = kb.sb("PTl0", [128, 8, 512], BF16)
    mxc = kb.sb("mxcl0", [128, 4], F32)
    YT = pool3[:, 2048:4096].rearrange("p (f t) -> p f t", f=4)
    rot = kb.sb("rotl0", [128, 512], BF16)
    tmp2 = kb.sb("tmp2l0", [128, 512], BF16)
    cq = kb.sb("cql0", [128, 16, 64], BF16)
    sq = kb.sb("sql0", [128, 16, 64], BF16)
    ck = kb.sb("ckl0", [128, 16, 64], BF16)
    sk = kb.sb("skl0", [128, 16, 64], BF16)
    qsB = kb.sb("qsBl0", [128, 64], F32)
    qsR = kb.sb("qsRl0", [128, 64], F32)
    ksB = kb.sb("ksBl0", [128, 64], F32)
    ksR = kb.sb("ksRl0", [128, 64], F32)
    ga = kb.sb("gal0", [128, 128], F32)
    gg = kb.sb("ggl0", [128, 128], F32)
    OTf = pool3[:, 0:4096].rearrange("p (f t) -> p f t", f=8)
    b1s = kb.sb("b1sl0", [128, 4, 128], BF16)
    b2c = kb.sb("b2cl0", [128, 8], F32)
    wob = pool2[:, 0:4096].rearrange("p (k t) -> p k t", k=4)
    lgB = kb.sb("lgBl0", [128, 1024], BF16)
    lbB = kb.sb("lbBl0", [128, 1024], BF16)
    ctx.ln_stats = kb.sb("lnstl0", [128, 12], F32)
    ctx.ln_mvc = kb.sb("lnmvl0", [128, 4], F32)

    def snapshot_bnd(tag):
        kb.dma("qpool", bscr[:, :], X[127:128, :, 512:1024], reads=["X"], writes=["bscr"], tag=f"sn1{tag}")
        kb.dma("qpool", bnd[:, :], bscr[:, :], reads=["bscr"], writes=["bnd"], tag=f"sn2{tag}")

    def shift_tile(i, tag, dst, dkey, f32src=False):
        kb.op("vector", lambda v, i=i, dst=dst: v.tensor_copy(dst[:, 0:512], X[:, i, 0:512]),
              reads=["X"], writes=[dkey], tag=f"shc{tag}")
        kb.dma("qpool", dst[1:128, 512:1024], X[0:127, i, 512:1024],
               reads=["X"], writes=[dkey], tag=f"shd{tag}")
        if i > 0:
            kb.dma("qpool", dst[0:1, 512:1024], bnd[i - 1:i, 0:512],
                   reads=["bnd"], writes=[dkey], tag=f"shb{tag}")
        else:
            kb.op("vector", lambda v, dst=dst: v.memset(dst[0:1, 512:1024], 0.0),
                  reads=[], writes=[dkey], tag=f"shz{tag}")

    KTP_KEYS = ["KTp", "KTpz", "HFT"]

    for l in range(2):
        if debug and l == 1:
            dXl0 = ctx.out("dXl0", [128, 16, 1024], F32)
            kb.dma("qpool", dXl0[:, :, :], X[:, :, :], reads=["X"], tag="dbgxl0")
        kb.dma("qpool", qsB[:, :], bass.AP(win[f"qs{l}"].tensor, 0, [[0, 128], [1, 64]]),
               writes=["qsB"], tag=f"ldqs{l}")
        kb.dma("qpool", qsR[:, 0:32], bass.AP(win[f"qs{l}"].tensor, 32, [[0, 128], [1, 32]]),
               writes=["qsR"], tag=f"ldqsr1{l}")
        kb.dma("qpool", qsR[:, 32:64], bass.AP(win[f"qs{l}"].tensor, 0, [[0, 128], [1, 32]]),
               writes=["qsR"], tag=f"ldqsr2{l}")
        kb.dma("qpool", ksB[:, :], bass.AP(win[f"ks{l}"].tensor, 0, [[0, 128], [1, 64]]),
               writes=["ksB"], tag=f"ldks{l}")
        kb.dma("qpool", ksR[:, 0:32], bass.AP(win[f"ks{l}"].tensor, 32, [[0, 128], [1, 32]]),
               writes=["ksR"], tag=f"ldksr1{l}")
        kb.dma("qpool", ksR[:, 32:64], bass.AP(win[f"ks{l}"].tensor, 0, [[0, 128], [1, 32]]),
               writes=["ksR"], tag=f"ldksr2{l}")
        for (dst, dk, nm, sc, sck) in [(cq, "cq", "qc", qsB, "qsB"), (sq, "sq", "qsn", qsR, "qsR"),
                                       (ck, "ck", "kc", ksB, "ksB"), (sk, "sk", "ksn", ksR, "ksR")]:
            kb.dma("qpool", dst[:, :, :], cons_in[nm].rearrange("(t p) d -> p t d", p=128),
                   writes=[dk], tag=f"ld{dk}{l}")
            for i in range(NTT):
                kb.op("vector", lambda v, dst=dst, sc=sc, i=i:
                      v.tensor_tensor(dst[:, i, :], dst[:, i, :], sc[:, :], ALU.mult),
                      reads=[dk, sck], writes=[dk], tag=f"ct{dk}{l}_{i}")

        kb.dma("qpool", wqk[:, :, :], win[f"wqkv{l}"][:, 0:1024].rearrange("(kt p) n -> p kt n", p=128),
               writes=["wqkA", "wqkB"], tag=f"ldwqk{l}")
        kb.dma("qpool", wv2[:, :, :], win[f"wqkv{l}"][:, 1024:1536].rearrange("(kt p) n -> p kt n", p=128),
               writes=["wv2"], tag=f"ldwv{l}")
        kb.dma("qpool", wob[:, :, :], win[f"wo{l}"].rearrange("(kt p) n -> p kt n", p=128),
               writes=["wob", "w2m0", "w2m1"], tag=f"ldwo{l}")
        snapshot_bnd(f"a{l}")
        first_ktp = [True]

        def ktp_wkeys():
            if first_ktp[0]:
                first_ktp[0] = False
                return KTP_KEYS
            return ["KTp"]
        kb.op("vector", lambda v: v.memset(KTp[:, :, 0:512], 0.0), reads=["HFT"],
              writes=["KTp", "KTpz", "HFT"], tag=f"ktz{l}")

        for qg in range(4):
            for it in range(4):
                i = qg * 4 + it
                shift_tile(i, f"a{l}_{i}", sh, "sh")
                emit_transpose_tile(ctx, sh, ["sh"], hT, "hT", it, nft=8)
                hT_keys = [f"hT_{hh}_t{it}" for hh in range(0, 8, 4)]
                for nb_ in range(2):
                    bank, bkey = ctx.banks.nf()
                    mm_group(kb, bank[:, :], bkey,
                             [(hT[:, k, it * 128:(it + 1) * 128], wqk[:, k, nb_ * 512:(nb_ + 1) * 512])
                              for k in range(8)],
                             reads=["wqkA", "wqkB"] + hT_keys, tag=f"qk{l}_{i}_{nb_}")
                    kb.op("scalar", lambda s, bank=bank, nb_=nb_:
                          s.activation(QKt[:, nb_ * 512:(nb_ + 1) * 512], bank[:, :], AF.Copy),
                          reads=[bkey], writes=["QKt"], tag=f"qkev{l}_{i}_{nb_}")
                bank, bkey = ctx.banks.nf()
                mm_group(kb, bank[:, :], bkey,
                         [(hT[:, k, it * 128:(it + 1) * 128], wv2[:, k, :]) for k in range(8)],
                         reads=["wv2"] + hT_keys, tag=f"v{l}_{i}")
                kb.op("scalar", lambda s, bank=bank, i=i: s.activation(Vp[:, i, :], bank[:, :], AF.Copy),
                      reads=[bkey], writes=["Vp"], tag=f"vev{l}_{i}")
                kb.op("scalar", lambda s: s.activation(sqw[:, :], QKt[:, :], AF.Square),
                      reads=["QKt"], writes=["sqw"], tag=f"l2s{l}_{i}")
                kb.op("vector", lambda v: v.tensor_reduce(rno[:, 0:16],
                                                          sqw[:, :].rearrange("p (g d) -> p g d", d=64),
                                                          AX.X, ALU.add),
                      reads=["sqw"], writes=["rno"], tag=f"l2r{l}_{i}")
                kb.op("scalar", lambda s: s.activation(rno[:, 0:16], rno[:, 0:16], AF.Sqrt),
                      reads=["rno"], writes=["rno"], tag=f"l2q{l}_{i}")
                kb.op("vector", lambda v: v.tensor_scalar(rno[:, 0:16], rno[:, 0:16], 1e-12, None, ALU.max),
                      reads=["rno"], writes=["rno"], tag=f"l2m{l}_{i}")
                kb.op("vector", lambda v: v.reciprocal(rno[:, 0:16], rno[:, 0:16]),
                      reads=["rno"], writes=["rno"], tag=f"l2i{l}_{i}")
                for qk in range(2):
                    base = qk * 512
                    ctab, ctk = (cq, "cq") if qk == 0 else (ck, "ck")
                    stab, stk = (sq, "sq") if qk == 0 else (sk, "sk")
                    kb.op("vector", lambda v, base=base:
                          v.tensor_scalar(rot.rearrange("p (h d) -> p h d", d=64)[:, :, 0:32],
                                          QKt.rearrange("p (h d) -> p h d", d=64)[:, 8 * (base // 512):
                                                                                  8 * (base // 512) + 8, 32:64],
                                          -1.0, None, ALU.mult),
                          reads=["QKt"], writes=["rot"], tag=f"ro1{l}_{i}_{qk}")
                    kb.op("vector", lambda v, base=base:
                          v.tensor_copy(rot.rearrange("p (h d) -> p h d", d=64)[:, :, 32:64],
                                        QKt.rearrange("p (h d) -> p h d", d=64)[:, 8 * (base // 512):
                                                                                8 * (base // 512) + 8, 0:32]),
                          reads=["QKt"], writes=["rot"], tag=f"ro2{l}_{i}_{qk}")
                    for h in range(H):
                        kb.op("vector", lambda v, base=base, h=h, ctab=ctab, i=i:
                              v.tensor_tensor(tmp2[:, h * 64:(h + 1) * 64],
                                              QKt[:, base + h * 64:base + (h + 1) * 64],
                                              ctab[:, i, :], ALU.mult),
                              reads=["QKt", ctk], writes=["tmp2"], tag=f"ro3{l}_{i}_{qk}_{h}")
                        kb.op("vector", lambda v, h=h, stab=stab, i=i:
                              v.tensor_tensor(rot[:, h * 64:(h + 1) * 64], rot[:, h * 64:(h + 1) * 64],
                                              stab[:, i, :], ALU.mult),
                              reads=["rot", stk], writes=["rot"], tag=f"ro4{l}_{i}_{qk}_{h}")
                    kb.op("vector", lambda v: v.tensor_tensor(tmp2[:, :], tmp2[:, :], rot[:, :], ALU.add),
                          reads=["tmp2", "rot"], writes=["tmp2"], tag=f"ro5{l}_{i}_{qk}")
                    for h in range(H):
                        kb.op("vector", lambda v, base=base, h=h, qk=qk:
                              v.tensor_scalar(QKt[:, base + h * 64:base + (h + 1) * 64],
                                              tmp2[:, h * 64:(h + 1) * 64],
                                              rno[:, qk * 8 + h:qk * 8 + h + 1], None, ALU.mult),
                              reads=["tmp2", "rno"], writes=["QKt"], tag=f"ro6{l}_{i}_{qk}_{h}")
                for qk in range(2):
                    bank, bkey = ctx.banks.nb()
                    items = [(bank[:, j * 128:(j + 1) * 128],
                              QKt[:, qk * 512 + j * 128:qk * 512 + (j + 1) * 128]) for j in range(4)]
                    transpose_group(kb, items, ctx.idb[:, :], "idb", reads=["QKt"], writes=[bkey],
                                    tag=f"qtr{l}_{i}_{qk}")
                    if qk == 0:
                        kb.op("vector", lambda v, bank=bank, it=it:
                              v.tensor_copy(QT[:, :, it * 128:(it + 1) * 128],
                                            bank[:, 0:512].rearrange("p (f t) -> p f t", f=4)),
                              reads=[bkey], writes=["QT", "OTf"], tag=f"qtc{l}_{i}")
                    else:
                        kb.op("vector", lambda v, bank=bank, i=i:
                              v.tensor_copy(KTp[:, :, 512 + i * 128:512 + (i + 1) * 128],
                                            bank[:, 0:512].rearrange("p (f t) -> p f t", f=4)),
                              reads=[bkey], writes=["KTp"], tag=f"ktc{l}_{i}")
            for h in range(H):
                fq, fo = h // 2, (h % 2) * 64
                for ql in range(4):
                    qt = qg * 4 + ql
                    def mmsc(t, fq=fq, fo=fo, ql=ql, qg=qg):
                        t.matmul(big[:, 0:512], QT[fo:fo + 64, fq, ql * 128:(ql + 1) * 128],
                                 KTp[fo:fo + 64, fq, qg * 512:qg * 512 + 512], start=True, stop=True)
                        return t.matmul(big[:, 512:1024], QT[fo:fo + 64, fq, ql * 128:(ql + 1) * 128],
                                        KTp[fo:fo + 64, fq, qg * 512 + 512:qg * 512 + 1024],
                                        start=True, stop=True)
                    kb.op("tensor", mmsc, reads=["QT", "KTp", "KTpz"], writes=["big"], tag=f"sc{l}_{h}_{qt}")
                    kb.op("vector", lambda v, ql=ql:
                          v.tensor_tensor(scm[:, :], big[:, :], maskb[:, ql, :], ALU.add),
                          reads=["big", "maskb"], writes=["scm"], tag=f"msk{l}_{h}_{qt}")
                    if qg == 0:
                        kb.op("vector", lambda v: v.memset(scm[:, 0:512], -1e30),
                              reads=[], writes=["scm"], tag=f"m0{l}_{h}_{qt}")
                    kb.op("vector", lambda v: v.tensor_reduce(mxc[:, 0:1], scm[:, :], AX.X, ALU.max),
                          reads=["scm"], writes=["mx"], tag=f"mx{l}_{h}_{qt}")
                    kb.op("vector", lambda v: v.tensor_scalar(mxc[:, 0:1], mxc[:, 0:1], -8.0, None, ALU.mult),
                          reads=["mx"], writes=["mxn"], tag=f"mxn{l}_{h}_{qt}")
                    kb.op("scalar", lambda s: s.activation(P[:, :], scm[:, :], AF.Exp,
                                                           bias=mxc[:, 0:1], scale=8.0, accum_out=mxc[:, 1:2]),
                          reads=["scm", "mxn"], writes=["P", "den"], tag=f"exp{l}_{h}_{qt}")
                    kb.op("vector", lambda v: v.reciprocal(mxc[:, 1:2], mxc[:, 1:2]),
                          reads=["den"], writes=["rden"], tag=f"rd{l}_{h}_{qt}")
                    kb.op("vector", lambda v: v.tensor_scalar(P[:, :], P[:, :], mxc[:, 1:2], None, ALU.mult),
                          reads=["P", "rden"], writes=["P"], tag=f"pn{l}_{h}_{qt}")
                    for hh in range(0, 8, 4):
                        bank, bkey = ctx.banks.nb()
                        items = [(bank[:, j * 128:(j + 1) * 128], P[:, (hh + j) * 128:(hh + j + 1) * 128])
                                 for j in range(4)]
                        transpose_group(kb, items, ctx.idb[:, :], "idb", reads=["P"], writes=[bkey],
                                        tag=f"ptr{l}_{h}_{qt}_{hh}")
                        kb.op("vector", lambda v, bank=bank, hh=hh, ql=ql:
                              v.tensor_copy(PT[:, hh:hh + 4, ql * 128:(ql + 1) * 128], bank[:, 0:512]),
                              reads=[bkey], writes=["PT"], tag=f"ptc{l}_{h}_{qt}_{hh}")
                bank, bkey = ctx.banks.nf()
                def mmav(t, h=h, qg=qg, bank=bank):
                    last = None
                    js = range(4, 8) if qg == 0 else range(8)
                    first = True
                    for j in js:
                        kt = qg * 4 + j - 4
                        last = t.matmul(bank[0:64, :], Vp[:, kt, h * 64:(h + 1) * 64], PT[:, j, :],
                                        start=first, stop=(j == 7))
                        first = False
                    return last
                kb.op("tensor", mmav, reads=["Vp", "PT"], writes=[bkey], tag=f"av{l}_{h}_{qg}")
                kb.op("vector", lambda v, bank=bank, h=h:
                      v.tensor_copy(YT[(h % 2) * 64:(h % 2) * 64 + 64, h // 2, :], bank[0:64, :]),
                      reads=[bkey], writes=["YTl0", "OTf"], tag=f"ytc{l}_{h}_{qg}")
            for it in range(4):
                tile = qg * 4 + it
                for nb_ in range(2):
                    bank, bkey = ctx.banks.nf()
                    mm_group(kb, bank[:, :], bkey,
                             [(YT[:, k, it * 128:(it + 1) * 128], wob[:, k, nb_ * 512:(nb_ + 1) * 512])
                              for k in range(4)],
                             reads=["wob", "YTl0"], tag=f"wo{l}_{tile}_{nb_}")
                    kb.op("vector", lambda v, bank=bank, tile=tile, nb_=nb_:
                          v.tensor_tensor(X[:, tile, nb_ * 512:(nb_ + 1) * 512],
                                          X[:, tile, nb_ * 512:(nb_ + 1) * 512], bank[:, :], ALU.add),
                          reads=[bkey, "X"], writes=["X"], tag=f"xua{l}_{tile}_{nb_}")
        if debug and l == 0:
            dKTp = ctx.out("dKTp", [128, 4, 2560], F32)
            dVp = ctx.out("dVp", [128, 16, 512], F32)
            dXa = ctx.out("dXa", [128, 16, 1024], F32)
            kb.dma("qpool", dKTp[:, :, :], KTp[:, :, :], reads=["KTp", "KTpz"], tag="dbgk")
            kb.dma("qpool", dVp[:, :, :], Vp[:, :, :], reads=["Vp"], tag="dbgv")
            kb.dma("qpool", dXa[:, :, :], X[:, :, :], reads=["X"], tag="dbgx")
        if debug and l == 1:
            dXa1 = ctx.out("dXa1", [128, 16, 1024], F32)
            dKTp1 = ctx.out("dKTp1", [128, 4, 2560], F32)
            kb.dma("qpool", dXa1[:, :, :], X[:, :, :], reads=["X"], tag="dbgx1")
            kb.dma("qpool", dKTp1[:, :, :], KTp[:, :, :], reads=["KTp", "KTpz"], tag="dbgk1")
        kb.dma("qpool", lgB[:, :], bass.AP(win[f"lg{l}"].tensor, 0, [[0, 128], [1, 1024]]),
               writes=["lgB"], tag=f"ldlg{l}")
        kb.dma("qpool", lbB[:, :], bass.AP(win[f"lb{l}"].tensor, 0, [[0, 128], [1, 1024]]),
               writes=["lbB"], tag=f"ldlb{l}")
        kb.dma("qpool", b2c[:, :], win[f"b2{l}"].rearrange("(f p) -> p f", p=128), writes=["b2c"],
               allow_slow_non_contiguous=True, tag=f"ldb2{l}")
        snapshot_bnd(f"f{l}")
        first_hft = [True]
        for pass_ in range(4):
            tiles = list(range(pass_ * 4, pass_ * 4 + 4))
            for i in tiles:
                it = i - pass_ * 4
                kb.op("vector", lambda v, i=i: v.tensor_copy(shf[:, 0:512], X[:, i, 0:512]),
                      reads=["X"], writes=["shf"], tag=f"fshc{l}_{i}")
                kb.dma("qpool", shf[1:128, 512:1024], X[0:127, i, 512:1024],
                       reads=["X"], writes=["shf"], tag=f"fshd{l}_{i}")
                if i > 0:
                    kb.dma("qpool", shf[0:1, 512:1024], bnd[i - 1:i, 0:512],
                           reads=["bnd"], writes=["shf"], tag=f"fshb{l}_{i}")
                else:
                    kb.op("vector", lambda v: v.memset(shf[0:1, 512:1024], 0.0),
                          reads=[], writes=["shf"], tag=f"fshz{l}_{i}")
                emit_ln_tile(ctx, shf, "shf", sh[:, :], "sh", f"fln{l}_{i}", lgB, lbB, scm, "scm")
                emit_transpose_tile(ctx, sh, ["sh"], hT, "hT", it, nft=8)
            hTp_keys = [f"hT_{hh}_t{it}" for hh in range(0, 8, 4) for it in range(4)]
            for blk in range(11):
                wkey = "wqkA" if blk % 2 == 0 else "wqkB"
                woff = 0 if blk % 2 == 0 else 512
                kb.dma("qpool", wqk[:, :, woff:woff + 512],
                       win[f"w1p{l}"][:, blk * 512:(blk + 1) * 512].rearrange("(kt p) n -> p kt n", p=128),
                       writes=[wkey], tag=f"ldw1{l}_{pass_}_{blk}")
                kb.dma("qpool", b1s[:, :, :],
                       bass.AP(win[f"b1p{l}"].tensor, blk * 512, [[0, 128], [128, 4], [1, 128]]),
                       writes=["b1s"], tag=f"ldb1{l}_{pass_}_{blk}")
                for i in tiles:
                    it = i - pass_ * 4
                    bank, bkey = ctx.banks.nf()
                    mm_group(kb, bank[:, :], bkey,
                             [(hT[:, k, it * 128:(it + 1) * 128], wqk[:, k, woff:woff + 512])
                              for k in range(8)],
                             reads=[wkey] + hTp_keys, tag=f"w1{l}_{pass_}_{blk}_{i}")
                    for p_ in range(2):
                        c2 = blk * 2 + p_
                        kb.op("vector", lambda v, bank=bank, p_=p_, blk=blk:
                              v.tensor_tensor(ga[:, :], bank[:, p_ * 256:p_ * 256 + 128],
                                              b1s[:, p_ * 2, :], ALU.add),
                              reads=[bkey, "b1s"], writes=["ga"], tag=f"ga{l}_{pass_}_{blk}_{i}_{p_}")
                        kb.op("vector", lambda v, bank=bank, p_=p_, blk=blk:
                              v.tensor_tensor(gg[:, :], bank[:, p_ * 256 + 128:p_ * 256 + 256],
                                              b1s[:, p_ * 2 + 1, :], ALU.add),
                              reads=[bkey, "b1s"], writes=["gg"], tag=f"gg{l}_{pass_}_{blk}_{i}_{p_}")
                        kb.op("scalar", lambda s: s.activation(gg[:, :], gg[:, :], AF.Gelu),
                              reads=["gg"], writes=["gg"], tag=f"gge{l}_{pass_}_{blk}_{i}_{p_}")
                        kb.op("vector", lambda v: v.tensor_tensor(tmp2[:, 0:128], ga[:, :], gg[:, :], ALU.mult),
                              reads=["ga", "gg"], writes=["hgb"], tag=f"hg{l}_{pass_}_{blk}_{i}_{p_}")
                        bank2, bkey2 = ctx.banks.nb()
                        transpose_group(kb, [(bank2[:, 0:128], tmp2[:, 0:128])], ctx.idb[:, :], "idb",
                                        reads=["hgb"], writes=[bkey2], tag=f"htr{l}_{pass_}_{blk}_{i}_{p_}")
                        wk = (["HFT", "KTp", "KTpz", "QT"] if first_hft[0] else ["HFT"])
                        first_hft[0] = False
                        kb.op("vector", lambda v, bank2=bank2, c2=c2, it=i - pass_ * 4:
                              v.tensor_copy(HFT[:, c2, it * 128:(it + 1) * 128], bank2[:, 0:128]),
                              reads=[bkey2], writes=wk, tag=f"htc{l}_{pass_}_{blk}_{i}_{p_}")
            if debug and l == 0 and pass_ == 0:
                dHFT = ctx.out("dHFT", [128, 22, 512], F32)
                kb.dma("qpool", dHFT[:, :, :], HFT[:, :, :], reads=["HFT"], tag="dbghft")
            for m in range(8):
                wmb = w2m[m % 2]
                wmk = f"w2m{m % 2}"
                kb.dma("qpool", wmb[:, :, :],
                       win[f"w2p{l}"][:, m * 128:(m + 1) * 128].rearrange("(kt p) n -> p kt n", p=128),
                       writes=[wmk, "wob"], tag=f"ldw2{l}_{pass_}_{m}")
                bank, bkey = ctx.banks.nf()
                mm_group(kb, bank[:, :], bkey,
                         [(wmb[:, kt, :], HFT[:, kt, :]) for kt in range(22)],
                         reads=[wmk, "HFT"], tag=f"w2{l}_{pass_}_{m}")
                kb.op("scalar", lambda s, bank=bank, m=m:
                      s.activation(OTf[:, m, :], bank[:, :], AF.Identity, bias=b2c[:, m:m + 1]),
                      reads=[bkey, "b2c"], writes=["OTf", "QT", "YTl0"], tag=f"w2e{l}_{pass_}_{m}")
            if debug and l == 0 and pass_ == 0:
                dOTf = ctx.out("dOTf", [128, 8, 512], F32)
                kb.dma("qpool", dOTf[:, :, :], OTf[:, :, :], reads=["OTf"], tag="dbgotf")
            for it in range(4):
                tile = pass_ * 4 + it
                for fh in range(2):
                    bank, bkey = ctx.banks.nb()
                    items = [(bank[:, j * 128:(j + 1) * 128], OTf[:, fh * 4 + j, it * 128:(it + 1) * 128])
                             for j in range(4)]
                    transpose_group(kb, items, ctx.idb[:, :], "idb", reads=["OTf"], writes=[bkey],
                                    tag=f"otr{l}_{pass_}_{it}_{fh}")
                    kb.op("vector", lambda v, bank=bank, tile=tile, fh=fh:
                          v.tensor_tensor(X[:, tile, fh * 512:(fh + 1) * 512],
                                          X[:, tile, fh * 512:(fh + 1) * 512], bank[:, 0:512], ALU.add),
                          reads=[bkey, "X"], writes=["X"], tag=f"xuf{l}_{tile}_{fh}")
    if debug:
        dX0 = ctx.out("dX0", [128, 16, 1024], F32)
        kb.dma("qpool", dX0[:, :, :], X[:, :, :], reads=["X"], tag="dbgx0")
    kb.dma("qsp", out.rearrange("(t p) d -> p t d", p=128), X[:, :, :], reads=["X"], tag="stoX")
    n = kb.build()
    return ctx, n



def perm_l0attn_wqk(w):
    L_, D_, _ = w.shape
    idx = np.zeros(1024, np.int64)
    for h in range(8):
        base = h * 128
        idx[base:base + 64] = base + np.arange(0, 128, 2)
        idx[base + 64:base + 128] = base + np.arange(1, 128, 2)
    return np.ascontiguousarray(w[:, :, idx])


def make_l0attn_consts():
    S, DH, H = 2048, 128, 8
    inv = 1.0 / (10000.0 ** (np.arange(0, DH, 2, dtype=np.float64) / DH))
    pos = np.arange(S, dtype=np.float64)
    ang = pos[:, None] * inv[None, :]
    cos1 = np.cos(ang)
    sin1 = np.sin(ang)
    cosr = np.tile(cos1, (1, H)).astype(np.float32)
    sinr = np.tile(sin1, (1, H)).astype(np.float32)
    gam = 1.0 - np.exp(np.linspace(np.log(1 / 32), np.log(1 / 512), H))
    ell = np.arange(128, dtype=np.float64)
    Dt = np.zeros((H, 128, 128), np.float32)
    for h in range(H):
        diff = ell[None, :] - ell[:, None]
        Dt[h] = np.where(diff >= 0, gam[h] ** diff, 0.0)
    grow = np.zeros((H, 128), np.float32)
    gcol = np.zeros((H, 128), np.float32)
    for h in range(H):
        grow[h] = gam[h] ** (ell + 1)
        gcol[h] = gam[h] ** (127 - ell)
    g128 = (gam ** 128).astype(np.float32)
    return cosr, sinr, Dt, grow, gcol, g128


def stage_l0_attn(weights, debug=False):
    ctx = Ctx("l0attn")
    kb = ctx.kb
    ctx.banks = Banks(kb, nf=6, nb=2)
    emit_eps_consts(ctx)
    ctx.const_ident()
    H, DH = 8, 128
    xn_in = ctx.inp("xn", [T_FULL, D], F32, lambda c, e: e["xn"][c])
    win = {}
    for nm in ["WqP", "WkP", "Wv", "Wg", "Wo", "W1", "W2", "ln_s", "ln_b", "gns", "gnb"]:
        arr = weights[nm]
        win[nm] = ctx.inp(f"l0a_{nm}", list(arr.shape), F32, lambda c, e, a=arr: a)
    cosr, sinr, Dt_h, grow_h, gcol_h, g128_h = make_l0attn_consts()
    cos_in = ctx.inp("l0a_cos", [2048, 128], F32, lambda c, e: np.tile(cosr[:, 0:64], (1, 2)).astype(np.float32))
    sin_in = ctx.inp("l0a_sin", [2048, 128], F32, lambda c, e: np.tile(sinr[:, 0:64], (1, 2)).astype(np.float32))
    dt_in = ctx.inp("l0a_dt", [8, 128, 128], F32, lambda c, e: Dt_h)
    grow_in = ctx.inp("l0a_grow", [8, 128], F32, lambda c, e: grow_h)
    gcol_in = ctx.inp("l0a_gcol", [8, 128], F32, lambda c, e: gcol_h)
    out = ctx.out("a", [T_FULL, D], F32)
    g128 = [float(x) for x in g128_h]

    Xd = ctx.nc.dram_tensor("l0a_Xd", [T_FULL, D], F32).ap()
    xt = [kb.sb(f"xtl0a{i}", [128, 1024], F32) for i in range(2)]
    for i in range(NTT):
        kb.dma("qsp", xt[i % 2][:, :], xn_in[i * 128:(i + 1) * 128, :], writes=[f"xt{i % 2}"],
               tag=f"ldX{i}")
        kb.dma("qsp", Xd[i * 128:(i + 1) * 128, :], xt[i % 2][:, :], reads=[f"xt{i % 2}"],
               writes=["Xd"], tag=f"stX{i}")
    if debug:
        dXd0 = ctx.out("dXd0", [T_FULL, D], F32)
        kb.dma("qpool", dXd0[:, :], Xd[:, :], reads=["Xd"], tag="dbgxd0")
    ctab = kb.sb("ctabl0a", [128, 16, 128], BF16)
    stab = kb.sb("stabl0a", [128, 16, 128], BF16)
    kb.dma("qpool", ctab[:, :, :], cos_in.rearrange("(t p) d -> p t d", p=128), writes=["ctab"])
    kb.dma("qpool", stab[:, :, :], sin_in.rearrange("(t p) d -> p t d", p=128), writes=["stab"])
    Dt = kb.sb("Dtl0", [128, 8, 128], BF16)
    kb.dma("qpool", Dt[:, :, :], dt_in.rearrange("h k q -> k h q"), writes=["Dt"])
    growB = kb.sb("growB", [128, 8, 128], BF16)
    kb.dma("qpool", growB[:, :, :], bass.AP(grow_in.tensor, 0, [[0, 128], [128, 8], [1, 128]]),
           writes=["growB"])
    gcolc = kb.sb("gcolc", [128, 8], F32)
    kb.dma("qpool", gcolc[:, :], gcol_in.rearrange("h j -> j h"), writes=["gcolc"],
           allow_slow_non_contiguous=True)
    lnsB = kb.sb("lnsB", [128, 1024], BF16)
    lnbB = kb.sb("lnbB", [128, 1024], BF16)
    kb.dma("qpool", lnsB[:, :], bass.AP(win["ln_s"].tensor, 0, [[0, 128], [1, 1024]]), writes=["lnsB"])
    kb.dma("qpool", lnbB[:, :], bass.AP(win["ln_b"].tensor, 0, [[0, 128], [1, 1024]]), writes=["lnbB"])
    gnsc = kb.sb("gnsc", [128, 4, 8], F32)
    gnbc = kb.sb("gnbc", [128, 4, 8], F32)
    kb.dma("qpool", gnsc[:, :, :], win["gns"].rearrange("l (h p) -> p l h", p=128), writes=["gnsc"],
           allow_slow_non_contiguous=True)
    kb.dma("qpool", gnbc[:, :, :], win["gnb"].rearrange("l (h p) -> p l h", p=128), writes=["gnbc"],
           allow_slow_non_contiguous=True)

    lnT = kb.sb("lnTl0a", [128, 8, 2048], BF16)
    pool4 = kb.sb("pool4l0a", [128, 16384], BF16)
    YT = pool4[:, 0:16384].rearrange("p (f t) -> p f t", f=8)
    hGT = pool4[:, 0:16384].rearrange("p (k t) -> p k t", k=16)
    wq = kb.sb("wql0a", [128, 8, 1024], BF16)
    wk = kb.sb("wkl0a", [128, 8, 1024], BF16)
    wv = kb.sb("wvl0a", [128, 8, 1024], BF16)
    W2r = wk[:, :, :]
    QKc = kb.sb("QKcl0a", [128, 2, 1024], BF16)
    Vc = kb.sb("Vcl0a", [128, 1024], BF16)
    QTc = kb.sb("QTcl0a", [128, 8, 128], BF16)
    KTc = kb.sb("KTcl0a", [128, 8, 128], BF16)
    Kg = kb.sb("Kgl0a", [128, 128], BF16)
    AtD = kb.sb("AtDl0a", [128, 128], BF16)
    Sf = kb.sb("Sfl0a", [128, 8, 128], F32)
    Sb = kb.sb("Sbl0a", [128, 8, 128], BF16)
    QTg = kb.sb("QTgl0a", [128, 128], BF16)
    rot = kb.sb("rotl0a", [128, 1024], BF16)
    tmpr = kb.sb("tmprl0a", [128, 1024], BF16)
    tmpb = kb.sb("tmpbl0a", [128, 1024], BF16)
    mrow = kb.sb("mrowl0a", [1, 8, 512], BF16)
    gnw = kb.sb("gnwl0a", [128, 512], BF16)
    gnw2 = kb.sb("gnw2l0a", [128, 512], BF16)
    ctx.ln_stats = kb.sb("lnstl0a", [128, 12], F32)
    ctx.ln_mvc = kb.sb("lnmvl0a", [128, 4], F32)
    onescol = kb.sb("onescol", [128, 1], BF16)
    kb.op("vector", lambda v: v.memset(onescol[:, :], 1.0), writes=["onescol"])
    oneN = kb.sb("oneN", [1, 128], BF16)
    kb.op("vector", lambda v: v.memset(oneN[:, :], 1.0), writes=["oneN"])

    def rope_chunk(l, c, slot, neg_sin, tag):
        sgn = -1.0 if neg_sin else 1.0
        kb.op("vector", lambda v, c=c, slot=slot, sgn=sgn:
              v.tensor_scalar(rot.rearrange("p (h d) -> p h d", d=128)[:, :, 0:64],
                              QKc.rearrange("p s (h d) -> p s h d", d=128)[:, slot, :, 64:128],
                              -sgn, None, ALU.mult),
              reads=["QKc"], writes=["rot"], tag=f"rr1{tag}")
        kb.op("vector", lambda v, c=c, slot=slot, sgn=sgn:
              v.tensor_scalar(rot.rearrange("p (h d) -> p h d", d=128)[:, :, 64:128],
                              QKc.rearrange("p s (h d) -> p s h d", d=128)[:, slot, :, 0:64],
                              sgn, None, ALU.mult),
              reads=["QKc"], writes=["rot"], tag=f"rr2{tag}")
        for h in range(H):
            kb.op("vector", lambda v, c=c, slot=slot, h=h:
                  v.tensor_tensor(tmpb[:, h * 128:(h + 1) * 128], QKc[:, slot, h * 128:(h + 1) * 128],
                                  ctab[:, c, :], ALU.mult),
                  reads=["QKc", "ctab"], writes=["tmpb"], tag=f"rc{tag}_{h}")
            kb.op("vector", lambda v, c=c, h=h:
                  v.tensor_tensor(rot[:, h * 128:(h + 1) * 128], rot[:, h * 128:(h + 1) * 128],
                                  stab[:, c, :], ALU.mult),
                  reads=["rot", "stab"], writes=["rot"], tag=f"rs{tag}_{h}")
        kb.op("vector", lambda v, slot=slot:
              v.tensor_tensor(QKc[:, slot, :], tmpb[:, :], rot[:, :], ALU.add),
              reads=["tmpb", "rot"], writes=["QKc"], tag=f"ra{tag}")

    for l in range(4):
        for i in range(NTT):
            xi = xt[i % 2]
            xikey = f"xt{i % 2}"
            kb.dma("qsp", xi[:, :], Xd[i * 128:(i + 1) * 128, :], reads=["Xd"], writes=[xikey],
                   tag=f"lx1{l}_{i}")
            emit_ln_tile(ctx, xi, xikey, tmpb[:, :], "lnout", f"l1n{l}_{i}", lnsB, lnbB, tmpr, "tmpr")
            if debug and l == 0 and i == 1:
                dtmpb = ctx.out("dtmpb", [128, 1024], F32)
                dxi = ctx.out("dxi", [128, 1024], F32)
                kb.dma("qpool", dtmpb[:, :], tmpb[:, :], reads=["lnout"], tag="dbgtmpb")
                kb.dma("qpool", dxi[:, :], xi[:, :], reads=[xikey], tag="dbgxi")
            emit_transpose_tile(ctx, tmpb, ["lnout"], lnT, "lnT", i)
        lnT_keys = [f"lnT_{hh}_t{i}" for hh in range(0, 8, 4) for i in range(NTT)]
        if debug and l == 0:
            dlnT = ctx.out("dlnT", [128, 8, 2048], F32)
            kb.dma("qpool", dlnT[:, :, :], lnT[:, :, :], reads=lnT_keys, tag="dbglnt")
        kb.dma("qpool", wq[:, :, :], win["WqP"][l].rearrange("(kt p) n -> p kt n", p=128),
               writes=["wq"], tag=f"ldwq{l}")
        kb.dma("qpool", wk[:, :, :], win["WkP"][l].rearrange("(kt p) n -> p kt n", p=128),
               writes=["wk", "W2r"], tag=f"ldwk{l}")
        kb.dma("qpool", wv[:, :, :], win["Wv"][l].rearrange("(kt p) n -> p kt n", p=128),
               writes=["wv", "W2r2"], tag=f"ldwv{l}")
        kb.op("vector", lambda v: v.memset(Sf[:, :, :], 0.0), reads=["Sf"], writes=["Sf"], tag=f"s0{l}")
        kb.op("vector", lambda v: v.memset(Sb[:, :, :], 0.0), reads=["Sb"], writes=["Sb"], tag=f"s0b{l}")
        for c in range(NTT):
            for (slot, wbuf_, wkey) in [(0, wq, "wq"), (1, wk, "wk")]:
                for nb_ in range(2):
                    bank, bkey = ctx.banks.nf()
                    mm_group(kb, bank[:, :], bkey,
                             [(lnT[:, k, c * 128:(c + 1) * 128], wbuf_[:, k, nb_ * 512:(nb_ + 1) * 512])
                              for k in range(8)],
                             reads=[wkey] + lnT_keys, tag=f"p{slot}_{l}_{c}_{nb_}")
                    kb.op("scalar", lambda s, bank=bank, slot=slot, nb_=nb_:
                          s.activation(QKc[:, slot, nb_ * 512:(nb_ + 1) * 512], bank[:, :], AF.Copy),
                          reads=[bkey], writes=["QKc"], tag=f"pe{slot}_{l}_{c}_{nb_}")
            for nb_ in range(2):
                bank, bkey = ctx.banks.nf()
                mm_group(kb, bank[:, :], bkey,
                         [(lnT[:, k, c * 128:(c + 1) * 128], wv[:, k, nb_ * 512:(nb_ + 1) * 512])
                          for k in range(8)],
                         reads=["wv"] + lnT_keys, tag=f"pv{l}_{c}_{nb_}")
                kb.op("scalar", lambda s, bank=bank, nb_=nb_:
                      s.activation(Vc[:, nb_ * 512:(nb_ + 1) * 512], bank[:, :], AF.Copy),
                      reads=[bkey], writes=["Vc"], tag=f"pve{l}_{c}_{nb_}")
            if debug and l == 0 and c == 1:
                dQpre = ctx.out("dQpre", [128, 2, 1024], F32)
                kb.dma("qpool", dQpre[:, :, :], QKc[:, :, :], reads=["QKc"], tag="dbgqpre")
            rope_chunk(l, c, 0, False, f"q{l}_{c}")
            rope_chunk(l, c, 1, True, f"k{l}_{c}")
            for (slot, DSTT, dkey) in [(0, QTc, "QTc"), (1, KTc, "KTc")]:
                for hh in range(0, 8, 4):
                    bank, bkey = ctx.banks.nb()
                    items = [(bank[:, j * 128:(j + 1) * 128], QKc[:, slot, (hh + j) * 128:(hh + j + 1) * 128])
                             for j in range(4)]
                    transpose_group(kb, items, ctx.idb[:, :], "idb", reads=["QKc"], writes=[bkey],
                                    tag=f"t{slot}{l}_{c}_{hh}")
                    kb.op("vector", lambda v, bank=bank, hh=hh, DSTT=DSTT:
                          v.tensor_copy(DSTT[:, hh:hh + 4, :], bank[:, 0:512]),
                          reads=[bkey], writes=[dkey], tag=f"tc{slot}{l}_{c}_{hh}")
            for h in range(H):
                bank, bkey = ctx.banks.nf()
                kb.op("tensor", lambda t, bank=bank, h=h:
                      t.matmul(bank[:, 0:128], KTc[:, h, :], QTc[:, h, :], start=True, stop=True),
                      reads=["KTc", "QTc"], writes=[bkey], tag=f"at{l}_{c}_{h}")
                kb.op("vector", lambda v, bank=bank, h=h:
                      v.tensor_tensor(AtD[:, :], bank[:, 0:128], Dt[:, h, :], ALU.mult),
                      reads=[bkey, "Dt"], writes=["AtD"], tag=f"atd{l}_{c}_{h}")
                if debug and l == 0 and c == 1 and h == 0:
                    dAt = ctx.out("dAt", [128, 128], F32)
                    dAtD = ctx.out("dAtD", [128, 128], F32)
                    dQT = ctx.out("dQTc", [128, 8, 128], F32)
                    dKT = ctx.out("dKTc", [128, 8, 128], F32)
                    dDt = ctx.out("dDt", [128, 8, 128], F32)
                    kb.dma("qpool", dAt[:, :], bank[:, 0:128], reads=[bkey], tag="dbgat") if False else None
                    kb.dma("qpool", dAtD[:, :], AtD[:, :], reads=["AtD"], tag="dbgatd")
                    kb.dma("qpool", dQT[:, :, :], QTc[:, :, :], reads=["QTc"], tag="dbgqt")
                    kb.dma("qpool", dKT[:, :, :], KTc[:, :, :], reads=["KTc"], tag="dbgkt")
                    kb.dma("qpool", dDt[:, :, :], Dt[:, :, :], reads=["Dt"], tag="dbgdt")
                kb.op("vector", lambda v, h=h:
                      v.tensor_tensor(QTg[:, :], QTc[:, h, :], growB[:, h, :], ALU.mult),
                      reads=["QTc", "growB"], writes=["QTg"], tag=f"qtg{l}_{c}_{h}")
                bank2, bkey2 = ctx.banks.nf()
                def ymm(t, bank2=bank2, h=h):
                    t.matmul(bank2[:, 0:128], Vc[:, h * 128:(h + 1) * 128], AtD[:, :], start=True, stop=False)
                    return t.matmul(bank2[:, 0:128], Sb[:, h, :], QTg[:, :], start=False, stop=True)
                kb.op("tensor", ymm, reads=["Vc", "AtD", "Sb", "QTg"], writes=[bkey2], tag=f"y{l}_{c}_{h}")
                kb.op("vector", lambda v, bank2=bank2, h=h, c=c:
                      v.tensor_copy(YT[:, h, c * 128:(c + 1) * 128], bank2[:, 0:128]),
                      reads=[bkey2], writes=["YT", "hGT"], tag=f"ye{l}_{c}_{h}")
                kb.op("vector", lambda v, h=h:
                      v.tensor_scalar(Kg[:, :], QKc[:, 1, h * 128:(h + 1) * 128], gcolc[:, h:h + 1],
                                      None, ALU.mult),
                      reads=["QKc", "gcolc"], writes=["Kg"], tag=f"kg{l}_{c}_{h}")
                bank3, bkey3 = ctx.banks.nf()
                kb.op("tensor", lambda t, bank3=bank3, h=h:
                      t.matmul(bank3[:, 0:128], Kg[:, :], Vc[:, h * 128:(h + 1) * 128], start=True, stop=True),
                      reads=["Kg", "Vc"], writes=[bkey3], tag=f"su{l}_{c}_{h}")
                kb.op("vector", lambda v, h=h, g=g128[h]:
                      v.tensor_scalar(Sf[:, h, :], Sf[:, h, :], g, None, ALU.mult),
                      reads=["Sf"], writes=["Sf"], tag=f"sg{l}_{c}_{h}")
                kb.op("vector", lambda v, bank3=bank3, h=h:
                      v.tensor_tensor(Sf[:, h, :], Sf[:, h, :], bank3[:, 0:128], ALU.add),
                      reads=["Sf", bkey3], writes=["Sf"], tag=f"sa{l}_{c}_{h}")
                kb.op("vector", lambda v, h=h: v.tensor_copy(Sb[:, h, :], Sf[:, h, :]),
                      reads=["Sf"], writes=["Sb"], tag=f"sb{l}_{c}_{h}")
        if debug and l == 0:
            dYT0 = ctx.out("dYT0", [128, 8, 2048], F32)
            kb.dma("qpool", dYT0[:, :, :], YT[:, :, :], reads=["YT"], tag="dbgyt0")
        for h in range(H):
            for blkq in range(4):
                kb.op("scalar", lambda s, h=h, blkq=blkq:
                      s.activation(gnw[:, :], YT[:, h, blkq * 512:(blkq + 1) * 512], AF.Square),
                      reads=["YT"], writes=["gnwsq"], tag=f"gs{l}_{h}_{blkq}")
                bank, bkey = ctx.banks.nf()
                bankS, bkeyS = ctx.banks.nf()
                def statmm(t, bank=bank, bankS=bankS, h=h, blkq=blkq):
                    t.matmul(bank[0:1, 0:512], onescol[:, :], YT[:, h, blkq * 512:(blkq + 1) * 512],
                             start=True, stop=True)
                    return t.matmul(bankS[0:1, 0:512], onescol[:, :], gnw[:, :],
                                    start=True, stop=True)
                kb.op("tensor", statmm, reads=["YT", "gnwsq", "onescol"], writes=[bkey, bkeyS],
                      tag=f"st{l}_{h}_{blkq}")
                kb.op("scalar", lambda s, bank=bank: s.activation(mrow[:, 0, :], bank[0:1, 0:512],
                                                                  AF.Identity, scale=1.0 / 128),
                      reads=[bkey], writes=["m0"], tag=f"me{l}_{h}_{blkq}")
                kb.op("scalar", lambda s, bankS=bankS: s.activation(mrow[:, 1, :], bankS[0:1, 0:512],
                                                                    AF.Identity, scale=1.0 / 128),
                      reads=[bkeyS], writes=["m1"], tag=f"sq2{l}_{h}_{blkq}")
                kb.op("scalar", lambda s: s.activation(mrow[:, 2, :], mrow[:, 0, :], AF.Square),
                      reads=["m0"], writes=["m2"], tag=f"mu2{l}_{h}_{blkq}")
                kb.op("vector", lambda v: v.tensor_tensor(mrow[:, 1, :], mrow[:, 1, :], mrow[:, 2, :],
                                                          ALU.subtract),
                      reads=["m1", "m2"], writes=["m1"], tag=f"var{l}_{h}_{blkq}")
                kb.op("scalar", lambda s: s.activation(mrow[:, 1, :], mrow[:, 1, :], AF.Sqrt,
                                                       bias=ctx.epsc[0:1, :]),
                      reads=["m1", "epsc"], writes=["m1"], tag=f"sd{l}_{h}_{blkq}")
                def _recip(v):
                    with ctx.nc.allow_low_precision(reason="bf16 groupnorm row rstd"):
                        return v.reciprocal(mrow[:, 1, :], mrow[:, 1, :])
                kb.op("vector", _recip, reads=["m1"], writes=["m1r"], tag=f"rs{l}_{h}_{blkq}")
                kb.op("vector", lambda v: v.tensor_tensor(mrow[:, 3, :], mrow[:, 0, :], mrow[:, 1, :],
                                                          ALU.mult),
                      reads=["m0", "m1r"], writes=["m3"], tag=f"mr{l}_{h}_{blkq}")
                bank4, bkey4 = ctx.banks.nf()
                kb.op("tensor", lambda t, bank4=bank4: t.matmul(bank4[:, 0:512], oneN[:, :], mrow[:, 1, :],
                                                                start=True, stop=True),
                      reads=["m1r", "oneN"], writes=[bkey4], tag=f"bc1{l}_{h}_{blkq}")
                bank5, bkey5 = ctx.banks.nf()
                kb.op("tensor", lambda t, bank5=bank5: t.matmul(bank5[:, 0:512], oneN[:, :], mrow[:, 3, :],
                                                                start=True, stop=True),
                      reads=["m3", "oneN"], writes=[bkey5], tag=f"bc2{l}_{h}_{blkq}")
                kb.op("vector", lambda v, bank4=bank4, h=h, blkq=blkq:
                      v.tensor_tensor(gnw[:, :], YT[:, h, blkq * 512:(blkq + 1) * 512], bank4[:, 0:512],
                                      ALU.mult),
                      reads=["YT", bkey4], writes=["gnw", "gnwsq"], tag=f"g1{l}_{h}_{blkq}")
                kb.op("vector", lambda v, bank5=bank5:
                      v.tensor_tensor(gnw2[:, :], gnw[:, :], bank5[:, 0:512], ALU.subtract),
                      reads=["gnw", bkey5], writes=["gnw2"], tag=f"g2{l}_{h}_{blkq}")
                kb.op("vector", lambda v, h=h, blkq=blkq, l=l:
                      v.tensor_scalar(YT[:, h, blkq * 512:(blkq + 1) * 512], gnw2[:, :],
                                      gnsc[:, l, h:h + 1], gnbc[:, l, h:h + 1], ALU.mult, ALU.add),
                      reads=["gnw2", "gnsc", "gnbc"], writes=["YT"], tag=f"g3{l}_{h}_{blkq}")
        kb.dma("qpool", wq[:, :, :], win["Wg"][l].rearrange("(kt p) n -> p kt n", p=128),
               writes=["wq"], tag=f"ldwg{l}")
        for mt in range(8):
            for nb_ in range(4):
                bank, bkey = ctx.banks.nf()
                mm_group(kb, bank[:, :], bkey,
                         [(wq[:, k, mt * 128:(mt + 1) * 128], lnT[:, k, nb_ * 512:(nb_ + 1) * 512])
                          for k in range(8)],
                         reads=["wq"] + lnT_keys, tag=f"gg{l}_{mt}_{nb_}")
                kb.op("scalar", lambda s, bank=bank: s.activation(gnw[:, :], bank[:, :], AF.Silu),
                      reads=[bkey], writes=["gnw", "gnwsq"], tag=f"gsw{l}_{mt}_{nb_}")
                kb.op("vector", lambda v, mt=mt, nb_=nb_:
                      v.tensor_tensor(YT[:, mt, nb_ * 512:(nb_ + 1) * 512],
                                      YT[:, mt, nb_ * 512:(nb_ + 1) * 512], gnw[:, :], ALU.mult),
                      reads=["YT", "gnw"], writes=["YT"], tag=f"gm{l}_{mt}_{nb_}")
        if debug and l == 0:
            dYT1 = ctx.out("dYT1", [128, 8, 2048], F32)
            kb.dma("qpool", dYT1[:, :, :], YT[:, :, :], reads=["YT"], tag="dbgyt1")
        kb.dma("qpool", wq[:, :, :], win["Wo"][l].rearrange("(kt p) n -> p kt n", p=128),
               writes=["wq"], tag=f"ldwo{l}")
        for i in range(NTT):
            xi = xt[i % 2]
            xikey = f"xt{i % 2}"
            kb.dma("qsp", xi[:, :], Xd[i * 128:(i + 1) * 128, :], reads=["Xd"], writes=[xikey],
                   tag=f"lxo{l}_{i}")
            for nb_ in range(2):
                bank, bkey = ctx.banks.nf()
                mm_group(kb, bank[:, :], bkey,
                         [(YT[:, k, i * 128:(i + 1) * 128], wq[:, k, nb_ * 512:(nb_ + 1) * 512])
                          for k in range(8)],
                         reads=["wq", "YT"], tag=f"wo{l}_{i}_{nb_}")
                kb.op("vector", lambda v, bank=bank, xi=xi, nb_=nb_:
                      v.tensor_tensor(xi[:, nb_ * 512:(nb_ + 1) * 512],
                                      xi[:, nb_ * 512:(nb_ + 1) * 512], bank[:, :], ALU.add),
                      reads=[bkey, xikey], writes=[xikey], tag=f"xu{l}_{i}_{nb_}")
            kb.dma("qsp", Xd[i * 128:(i + 1) * 128, :], xi[:, :], reads=[xikey], writes=["Xd"],
                   tag=f"sxo{l}_{i}")
        if debug and l == 0:
            dXa = ctx.out("dXa", [2048, 1024], F32)
            kb.dma("qpool", dXa[:, :], Xd[:, :], reads=["Xd"], tag="dbgxa")
        for i in range(NTT):
            xi = xt[i % 2]
            xikey = f"xt{i % 2}"
            kb.dma("qsp", xi[:, :], Xd[i * 128:(i + 1) * 128, :], reads=["Xd"], writes=[xikey],
                   tag=f"lx2{l}_{i}")
            emit_ln_tile(ctx, xi, xikey, tmpb[:, :], "lnout", f"l2n{l}_{i}", lnsB, lnbB, tmpr, "tmpr")
            emit_transpose_tile(ctx, tmpb, ["lnout"], lnT, "lnT", i)
        SQ2PI = 0.7978845608028654
        for half in range(2):
            tiles = list(range(half * 8, half * 8 + 8))
            for nbw in range(4):
                kb.dma("qpool", wq[:, :, 0:512],
                       win["W1"][l][:, nbw * 512:(nbw + 1) * 512].rearrange("(kt p) n -> p kt n", p=128),
                       writes=["wq"], tag=f"ldw1{l}_{half}_{nbw}")
                for i in tiles:
                    bank, bkey = ctx.banks.nf()
                    mm_group(kb, bank[:, :], bkey,
                             [(lnT[:, k, i * 128:(i + 1) * 128], wq[:, k, 0:512]) for k in range(8)],
                             reads=["wq"] + lnT_keys, tag=f"w1{l}_{half}_{nbw}_{i}")
                    kb.op("scalar", lambda s, bank=bank: s.activation(tmpr[:, 0:512], bank[:, :], AF.Copy),
                          reads=[bkey], writes=["tmpr"], tag=f"gc{l}_{half}_{nbw}_{i}")
                    kb.op("scalar", lambda s, bank=bank: s.activation(tmpr[:, 512:1024], bank[:, :], AF.Square),
                          reads=[bkey], writes=["tmpr"], tag=f"gq{l}_{half}_{nbw}_{i}")
                    kb.op("vector", lambda v: v.scalar_tensor_tensor(tmpr[:, 512:1024], tmpr[:, 512:1024],
                                                                     0.044715, tmpr[:, 0:512],
                                                                     ALU.mult, ALU.mult),
                          reads=["tmpr", "tmpr"], writes=["tmpr"], tag=f"gx{l}_{half}_{nbw}_{i}")
                    kb.op("vector", lambda v: v.tensor_tensor(tmpr[:, 512:1024], tmpr[:, 512:1024],
                                                              tmpr[:, 0:512], ALU.add),
                          reads=["tmpr", "tmpr"], writes=["tmpr"], tag=f"ga{l}_{half}_{nbw}_{i}")
                    kb.op("scalar", lambda s: s.activation(tmpr[:, 512:1024], tmpr[:, 512:1024], AF.Tanh,
                                                           scale=SQ2PI),
                          reads=["tmpr"], writes=["tmpr"], tag=f"gt{l}_{half}_{nbw}_{i}")
                    kb.op("vector", lambda v: v.tensor_scalar(tmpr[:, 512:1024], tmpr[:, 512:1024],
                                                              1.0, 0.5, ALU.add, ALU.mult),
                          reads=["tmpr"], writes=["tmpr"], tag=f"gh{l}_{half}_{nbw}_{i}")
                    kb.op("vector", lambda v: v.tensor_tensor(tmpb[:, 0:512], tmpr[:, 512:1024],
                                                              tmpr[:, 0:512], ALU.mult),
                          reads=["tmpr", "tmpr"], writes=["tmpb"], tag=f"gm{l}_{half}_{nbw}_{i}")
                    for hh in range(0, 4, 4):
                        bank2, bkey2 = ctx.banks.nb()
                        items = [(bank2[:, j * 128:(j + 1) * 128], tmpb[:, (hh + j) * 128:(hh + j + 1) * 128])
                                 for j in range(4)]
                        transpose_group(kb, items, ctx.idb[:, :], "idb", reads=["tmpb"], writes=[bkey2],
                                        tag=f"w1t{l}_{half}_{nbw}_{i}")
                        wkk = ["hGT", "YT"] if (half == 0 and nbw == 0 and i == tiles[0]) else ["hGT"]
                        kb.op("vector", lambda v, bank2=bank2, nbw=nbw, i=i, half=half:
                              v.tensor_copy(hGT[:, nbw * 4:nbw * 4 + 4, (i - half * 8) * 128:
                                                (i - half * 8 + 1) * 128],
                                            bank2[:, 0:512].rearrange("p (f t) -> p f t", f=4)),
                              reads=[bkey2], writes=wkk, tag=f"w1c{l}_{half}_{nbw}_{i}")
            kb.dma("qpool", wk[:, :, :], win["W2"][l][0:1024].rearrange("(kt p) n -> p kt n", p=128),
                   writes=["wk", "W2r"], tag=f"ldw2a{l}_{half}")
            kb.dma("qpool", wv[:, :, :], win["W2"][l][1024:2048].rearrange("(kt p) n -> p kt n", p=128),
                   writes=["wv", "W2r2"], tag=f"ldw2b{l}_{half}")
            for i in tiles:
                it = i - half * 8
                xi = xt[i % 2]
                xikey = f"xt{i % 2}"
                kb.dma("qsp", xi[:, :], Xd[i * 128:(i + 1) * 128, :], reads=["Xd"], writes=[xikey],
                       tag=f"lxf{l}_{i}")
                for nb_ in range(2):
                    bank, bkey = ctx.banks.nf()
                    def w2mm(t, bank=bank, it=it, nb_=nb_):
                        last = None
                        for k in range(16):
                            wsrc = wk if k < 8 else wv
                            last = t.matmul(bank[:, :], hGT[:, k, it * 128:(it + 1) * 128],
                                            wsrc[:, k % 8, nb_ * 512:(nb_ + 1) * 512],
                                            start=(k == 0), stop=(k == 15))
                        return last
                    kb.op("tensor", w2mm, reads=["hGT", "wk", "wv", "W2r", "W2r2"], writes=[bkey],
                          tag=f"w2{l}_{half}_{i}_{nb_}")
                    kb.op("vector", lambda v, bank=bank, xi=xi, nb_=nb_:
                          v.tensor_tensor(xi[:, nb_ * 512:(nb_ + 1) * 512],
                                          xi[:, nb_ * 512:(nb_ + 1) * 512], bank[:, :], ALU.add),
                          reads=[bkey, xikey], writes=[xikey], tag=f"xf{l}_{i}_{nb_}")
                dst = out if l == 3 else Xd
                kb.dma("qsp", dst[i * 128:(i + 1) * 128, :], xi[:, :], reads=[xikey], writes=["Xd"],
                       tag=f"sxf{l}_{i}")
    n = kb.build()
    return ctx, n



def stage_l1_attn(weights, debug=False):
    ctx = Ctx("l1attn")
    kb = ctx.kb
    ctx.banks = Banks(kb, nf=6, nb=2)
    emit_eps_consts(ctx)
    ctx.const_ident()
    H, DK, C, NC = 4, 256, 128, 16
    xn_in = ctx.inp("xn", [T_FULL, D], F32, lambda c, e: e["xn"][c])
    win = {}
    for nm in ["WqT", "WkT", "WvT", "WbT", "qcw", "kcw", "vcw", "fsw", "flw",
               "gw1T", "gb1", "gw2T", "log_temp", "base_bias", "floor_raw", "onorm_w", "WoT"]:
        arr = weights[nm]
        win[nm] = ctx.inp(f"l1a_{nm}", list(arr.shape), F32, lambda c, e, a=arr: a)
    mUS = np.triu(np.ones((128, 128), np.float32), 1) * -1.0
    mLS = np.tril(np.ones((128, 128), np.float32), -1) * -1.0
    mUI = np.triu(np.ones((128, 128), np.float32))
    mask_in = ctx.inp("l1a_masks", [3, 128, 128], F32, lambda c, e: np.stack([mUS, mLS, mUI]))
    out = ctx.out("a", [T_FULL, D], F32)
    sldram = ctx.nc.dram_tensor("l1a_paths", [3, T_FULL, D], F32).ap()
    xnd = ctx.nc.dram_tensor("l1a_xnd", [D, T_FULL], F32).ap()

    masks = kb.sb("masksl1", [128, 3, 128], BF16)
    kb.dma("qpool", masks[:, :, :], mask_in.rearrange("m p q -> p m q"), writes=["masks"])
    xnT = kb.sb("xnTl1", [128, 8, 2048], BF16)
    outF = None
    qhT = kb.sb("qhTl1", [128, 8, 2048], BF16)
    khT = kb.sb("khTl1", [128, 8, 2048], BF16)
    hmidT = [qhT, khT]
    vT = kb.sb("vTl1", [128, 8, 2112], BF16)
    dF2 = kb.sb("dF2l1", [128, 2, 2048], BF16)
    ot = kb.sb("otl1", [128, 256], BF16)
    statsF = kb.sb("statsFl1", [128, 2048], BF16)
    acc1 = kb.sb("acc1l1", [128, 2048], F32)
    acc2 = kb.sb("acc2l1", [128, 2112], F32)
    sqb = kb.sb("sqbl1", [128, 2048], BF16)
    wbuf = kb.sb("wbufl1", [128, 8, 512], BF16)
    gbuf = kb.sb("gbufl1", [128, 8, 128], BF16)
    wcols = kb.sb("wcolsl1", [128, 8, 12], F32)
    fwcol = kb.sb("fwcoll1", [128, 8, 66], F32)
    beta = kb.sb("betal1", [128, 16, 4], F32)
    uvw = kb.sb("uvwl1", [128, 512], F32)
    Pp = kb.sb("Ppl1", [128, 128], F32)
    St = kb.sb("Stl1", [128, 128], F32)
    Pt32 = kb.sb("Pt32l1", [128, 128], F32)
    ktok = kb.sb("ktokl1", [128, 256], BF16)
    vtok = kb.sb("vtokl1", [128, 256], BF16)
    attnT = kb.sb("attnTl1", [128, 128], BF16)
    up = kb.sb("upl1", [128, 256], BF16)
    wT2 = kb.sb("wT2l1", [128, 2, 128], BF16)
    Sfh = kb.sb("Sfhl1", [128, 2, 256], F32)
    Sbh = kb.sb("Sbhl1", [128, 2, 256], BF16)
    rows = kb.sb("rowsl1", [1, 2048], BF16)
    rows2 = kb.sb("rows2l1", [1, 2048], BF16)
    logitsT = kb.sb("logitsTl1", [16, 2048], BF16)
    lgt = kb.sb("lgtl1", [128, 16, 16], BF16)
    prb = kb.sb("prbl1", [128, 16, 16], BF16)
    prT = kb.sb("prTl1", [16, 2048], BF16)
    tcol = kb.sb("tcoll1", [128, 16], F32)
    fcol = kb.sb("fcoll1", [128, 16], F32)
    smw = kb.sb("smwl1", [128, 16], F32)
    onw = kb.sb("onwl1", [128, 2], F32)
    kb.dma("qpool", onw[:, :], win["onorm_w"].rearrange("(t p) -> p t", p=128), writes=["onw"],
           allow_slow_non_contiguous=True)
    oneN = kb.sb("oneNl1", [1, 128], BF16)
    kb.op("vector", lambda v: v.memset(oneN[:, :], 1.0), writes=["oneN"])
    onescol = kb.sb("onescl1", [128, 1], BF16)
    kb.op("vector", lambda v: v.memset(onescol[:, :], 1.0), writes=["onescol"])
    ctx.ln_stats = kb.sb("lnstl1", [128, 12], F32)
    ctx.ln_mvc = kb.sb("lnmvl1", [128, 4], F32)
    kb.dma("qpool", wcols[:, :, 0:4], win["qcw"].rearrange("(f p) k -> p f k", p=128), writes=["wcols"],
           allow_slow_non_contiguous=True)
    kb.dma("qpool", wcols[:, :, 4:8], win["kcw"].rearrange("(f p) k -> p f k", p=128), writes=["wcols"],
           allow_slow_non_contiguous=True)
    kb.dma("qpool", wcols[:, :, 8:12], win["vcw"].rearrange("(f p) k -> p f k", p=128), writes=["wcols"],
           allow_slow_non_contiguous=True)
    kb.dma("qpool", fwcol[:, :, 0:3], win["fsw"].rearrange("(h d) (tp p) k -> p (h tp) k", p=128)
           if False else win["fsw"].rearrange("h (tp p) k -> p (h tp) k", p=128),
           writes=["fwcol"], allow_slow_non_contiguous=True)
    kb.dma("qpool", fwcol[:, :, 3:66], win["flw"].rearrange("h (tp p) k -> p (h tp) k", p=128),
           writes=["fwcol"], allow_slow_non_contiguous=True)

    xt = [kb.sb("xtl10", [128, 1024], F32)]
    xt = [xt[0], xt[0]]
    xlb = sqb[:, 0:1024]
    for i in range(NTT):
        xi = xt[i % 2]
        xikey = "xtl10"
        kb.dma("qsp", xi[:, :], xn_in[i * 128:(i + 1) * 128, :], writes=[xikey], tag=f"ldx{i}")
        kb.op("vector", lambda v, xi=xi: v.tensor_copy(xlb[:, :], xi[:, :]),
              reads=[xikey], writes=["sqb"], tag=f"cb{i}")
        emit_transpose_tile(ctx, xlb, ["sqb"], xnT, "xnT", i)
    xnT_keys = [f"xnT_{hh}_t{i}" for hh in range(0, 8, 4) for i in range(NTT)]

    kb.dma("qpool", wbuf[:, :, 0:4], win["WbT"].rearrange("(kt p) n -> p kt n", p=128),
           writes=["wbuf"], tag="ldwb")
    for i in range(NTT):
        bank, bkey = ctx.banks.nf()
        mm_group(kb, bank[:, 0:4], bkey,
                 [(xnT[:, k, i * 128:(i + 1) * 128], wbuf[:, k, 0:4]) for k in range(8)],
                 reads=["wbuf"] + xnT_keys, tag=f"bmm{i}")
        kb.op("scalar", lambda s, bank=bank, i=i: s.activation(beta[:, i, :], bank[:, 0:4], AF.Sigmoid),
              reads=[bkey], writes=["beta"], tag=f"bev{i}")

    kb.dma("qpool", xnd.rearrange("(ft p) t -> p ft t", p=128), xnT[:, :, :],
           reads=xnT_keys, writes=["xnd"], tag="parkxn")
    def proj_conv(wname, coff, dst, dkey, tag):
        for wh in range(2):
            kb.dma("qpool", wbuf[:, :, :],
                   win[wname][:, wh * 512:(wh + 1) * 512].rearrange("(kt p) n -> p kt n", p=128),
                   writes=["wbuf"], tag=f"ldw{tag}{wh}")
            for mtl in range(4):
                mt = wh * 4 + mtl
                kb.op("vector", lambda v: v.memset(acc2[:, 0:64], 0.0),
                      reads=["acc2"], writes=["acc2"], tag=f"pz{tag}_{mt}")
                for nb_ in range(4):
                    bank, bkey = ctx.banks.nf()
                    mm_group(kb, bank[:, :], bkey,
                             [(wbuf[:, k, mtl * 128:(mtl + 1) * 128], xnT[:, k, nb_ * 512:(nb_ + 1) * 512])
                              for k in range(8)],
                             reads=["wbuf"] + xnT_keys, tag=f"pj{tag}_{mt}_{nb_}")
                    kb.op("scalar", lambda s, bank=bank, nb_=nb_:
                          s.activation(acc2[:, 64 + nb_ * 512:64 + (nb_ + 1) * 512], bank[:, :], AF.Copy),
                          reads=[bkey], writes=["acc2"], tag=f"pe{tag}_{mt}_{nb_}")
                kb.op("vector", lambda v, mt=mt, coff=coff:
                      v.tensor_scalar(acc1[:, :], acc2[:, 61:61 + 2048], wcols[:, mt, coff:coff + 1],
                                      None, ALU.mult),
                      reads=["acc2", "wcols"], writes=["acc1"], tag=f"cv0{tag}_{mt}")
                for j in range(1, 4):
                    kb.op("vector", lambda v, mt=mt, j=j, coff=coff:
                          v.scalar_tensor_tensor(acc1[:, :], acc2[:, 61 + j:61 + j + 2048],
                                                 wcols[:, mt, coff + j:coff + j + 1], acc1[:, :],
                                                 ALU.mult, ALU.add),
                          reads=["acc2", "wcols", "acc1"], writes=["acc1"], tag=f"cv{j}{tag}_{mt}")
                if dkey == "vT":
                    kb.op("vector", lambda v, mt=mt: v.memset(dst[:, mt, 0:64], 0.0),
                          reads=[], writes=[dkey], tag=f"vz{tag}_{mt}")
                    kb.op("scalar", lambda s, mt=mt: s.activation(dst[:, mt, 64:2112], acc1[:, :], AF.Silu),
                          reads=["acc1"], writes=[dkey], tag=f"cs{tag}_{mt}")
                else:
                    kb.op("scalar", lambda s, mt=mt: s.activation(dst[:, mt, :], acc1[:, :], AF.Silu),
                          reads=["acc1"], writes=[dkey], tag=f"cs{tag}_{mt}")
    proj_conv("WqT", 0, qhT, "qhT", "q")
    proj_conv("WkT", 4, khT, "khT", "k")
    proj_conv("WvT", 8, vT, "vT", "v")

    def l2norm_head(src, skey, h, tag):
        banks4 = [ctx.banks.nf() for _ in range(4)]
        for sub in range(2):
            kb.op("scalar", lambda s, sub=sub: s.activation(sqb[:, :], src[:, 2 * h + sub, :], AF.Square),
                  reads=[skey] + ([bk for _, bk in banks4] if sub == 1 else []),
                  writes=["sqb"], tag=f"n{sub}{tag}")
            for blk in range(4):
                bank, bkey = banks4[blk]
                kb.op("tensor", lambda t, bank=bank, blk=blk, sub=sub:
                      t.matmul(bank[0:1, :], onescol[:, :], sqb[:, blk * 512:(blk + 1) * 512],
                               start=(sub == 0), stop=(sub == 1)),
                      reads=["sqb", "onescol"], writes=[bkey], tag=f"nm{sub}{tag}_{blk}")
        for blk in range(4):
            bank, bkey = banks4[blk]
            kb.op("scalar", lambda s, bank=bank, blk=blk:
                  s.activation(rows[:, blk * 512:(blk + 1) * 512], bank[0:1, :], AF.Sqrt,
                               bias=ctx.epsc2[0:1, :]),
                  reads=[bkey, "epsc2"], writes=["rows"], tag=f"nq{tag}_{blk}")
        def _rcp(v):
            with ctx.nc.allow_low_precision(reason="bf16 l2norm row"):
                return v.reciprocal(rows[:, :], rows[:, :])
        kb.op("vector", _rcp, reads=["rows"], writes=["rows"], tag=f"nr{tag}")
        kb.op("vector", lambda v: v.tensor_copy(rows2[:, :], rows[:, :]), reads=["rows"], writes=["rows2"],
              tag=f"nc{tag}")
        for blk in range(4):
            bank, bkey = ctx.banks.nf()
            kb.op("tensor", lambda t, bank=bank, blk=blk:
                  t.matmul(bank[:, :], oneN[:, :], rows2[:, blk * 512:(blk + 1) * 512].rearrange(
                      "o n -> o n") if False else rows2[:, blk * 512:(blk + 1) * 512],
                      start=True, stop=True),
                  reads=["rows2", "oneN"], writes=[bkey], tag=f"nb{tag}_{blk}")
            for sub in range(2):
                kb.op("vector", lambda v, bank=bank, blk=blk, sub=sub:
                      v.tensor_tensor(src[:, 2 * h + sub, blk * 512:(blk + 1) * 512],
                                      src[:, 2 * h + sub, blk * 512:(blk + 1) * 512], bank[:, :], ALU.mult),
                      reads=[skey, bkey], writes=[skey], tag=f"na{tag}_{blk}_{sub}")
    ctx.epsc2 = kb.sb("epsc2l1", [128, 1], F32)
    kb.op("vector", lambda v: v.memset(ctx.epsc2[:, :], 1e-6), writes=["epsc2"])
    for h in range(H):
        l2norm_head(qhT, "qhT", h, f"q{h}")
        l2norm_head(khT, "khT", h, f"k{h}")

    def row_stats(srcs, skeys, r0, tag, iscale):
        banks4 = [ctx.banks.nf() for _ in range(4)]
        for blk in range(4):
            bank, bkey = ctx.banks.nf()
            def smm(t, bank=bank, blk=blk, srcs=srcs):
                t.matmul(bank[0:1, :], onescol[:, :], srcs[0][:, blk * 512:(blk + 1) * 512],
                         start=True, stop=False)
                return t.matmul(bank[0:1, :], onescol[:, :], srcs[1][:, blk * 512:(blk + 1) * 512],
                                start=False, stop=True)
            kb.op("tensor", smm, reads=list(skeys) + ["onescol"], writes=[bkey], tag=f"sm{tag}_{blk}")
            kb.op("scalar", lambda s, bank=bank, blk=blk:
                  s.activation(rows[:, blk * 512:(blk + 1) * 512], bank[0:1, :], AF.Identity, scale=iscale),
                  reads=[bkey], writes=["rows"], tag=f"se{tag}_{blk}")
        for sub in range(2):
            kb.op("scalar", lambda s, sub=sub, srcs=srcs: s.activation(sqb[:, :], srcs[sub], AF.Square),
                  reads=list(skeys) + ([bk for _, bk in banks4] if sub == 1 else []),
                  writes=["sqb"], tag=f"sq{sub}{tag}")
            for blk in range(4):
                bank, bkey = banks4[blk]
                kb.op("tensor", lambda t, bank=bank, blk=blk, sub=sub:
                      t.matmul(bank[0:1, :], onescol[:, :], sqb[:, blk * 512:(blk + 1) * 512],
                               start=(sub == 0), stop=(sub == 1)),
                      reads=["sqb", "onescol"], writes=[bkey], tag=f"sm2{sub}{tag}_{blk}")
        for blk in range(4):
            bank, bkey = banks4[blk]
            kb.op("scalar", lambda s, bank=bank, blk=blk:
                  s.activation(rows2[:, blk * 512:(blk + 1) * 512], bank[0:1, :], AF.Identity, scale=iscale),
                  reads=[bkey], writes=["rows2"], tag=f"se2{tag}_{blk}")
        kb.op("scalar", lambda s: s.activation(acc1[0:1, 0:2048], rows[:, :], AF.Square),
              reads=["rows"], writes=["acc1"], tag=f"vr{tag}")
        kb.op("vector", lambda v: v.tensor_tensor(rows2[:, :], rows2[:, :], acc1[0:1, 0:2048], ALU.subtract),
              reads=["rows2", "acc1"], writes=["rows2"], tag=f"vs{tag}")
        kb.dma("qpool", statsF[r0:r0 + 1, :], rows[:, :], reads=["rows"], writes=["statsF"], tag=f"sd{tag}")
        kb.dma("qpool", statsF[r0 + 1:r0 + 2, :], rows2[:, :], reads=["rows2"], writes=["statsF"],
               tag=f"sd2{tag}")


    if debug:
        dq = ctx.out("dq", [128, 8, 2048], F32)
        dk = ctx.out("dk", [128, 8, 2048], F32)
        dv = ctx.out("dv", [128, 8, 2112], F32)
        dbeta = ctx.out("dbeta", [128, 16, 4], F32)
        kb.dma("qpool", dq[:, :, :], qhT[:, :, :], reads=["qhT"], tag="dbgq")
        kb.dma("qpool", dk[:, :, :], khT[:, :, :], reads=["khT"], tag="dbgk")
        kb.dma("qpool", dv[:, :, :], vT[:, :, :], reads=["vT"], tag="dbgv")
        kb.dma("qpool", dbeta[:, :, :], beta[:, :, :], reads=["beta"], tag="dbgb")
    for h in range(H):
        kb.op("vector", lambda v: v.memset(Sfh[:, :, :], 0.0), reads=["Sfh"], writes=["Sfh"], tag=f"sz{h}")
        kb.op("vector", lambda v: v.memset(Sbh[:, :, :], 0.0), reads=["Sbh"], writes=["Sbh"], tag=f"szb{h}")
        for c in range(NC):
            cs = slice(c * 128, (c + 1) * 128)
            qts = [qhT[:, 2 * h, cs], qhT[:, 2 * h + 1, cs]]
            kts = [khT[:, 2 * h, cs], khT[:, 2 * h + 1, cs]]
            vts = [vT[:, 2 * h, 64 + c * 128:64 + (c + 1) * 128],
                   vT[:, 2 * h + 1, 64 + c * 128:64 + (c + 1) * 128]]
            for (srcs, dsttok, dkey) in [(kts, ktok, "ktok"), (vts, vtok, "vtok")]:
                bank, bkey = ctx.banks.nb()
                def tre(t, srcs=srcs, bank=bank):
                    t.transpose(bank[:, 0:128], srcs[0], ctx.idb[:, :])
                    return t.transpose(bank[:, 128:256], srcs[1], ctx.idb[:, :])
                kb.op("tensor", tre, reads=["qhT", "khT", "vT", "idb"], writes=[bkey], tag=f"tr{dkey}{h}_{c}")
                kb.op("vector", lambda v, bank=bank, dsttok=dsttok: v.tensor_copy(dsttok[:, :], bank[:, 0:256]),
                      reads=[bkey], writes=[dkey], tag=f"tc{dkey}{h}_{c}")
            kb.op("vector", lambda v, c=c, h=h: v.tensor_scalar(uvw[:, 0:256], vtok[:, :],
                                                                beta[:, c, h:h + 1], None, ALU.mult),
                  reads=["vtok", "beta"], writes=["uvw"], tag=f"ub{h}_{c}")
            kb.op("vector", lambda v, c=c, h=h: v.tensor_scalar(uvw[:, 256:512], ktok[:, :],
                                                                beta[:, c, h:h + 1], None, ALU.mult),
                  reads=["ktok", "beta"], writes=["uvw"], tag=f"kb{h}_{c}")
            bank, bkey = ctx.banks.nf()
            mm_group(kb, bank[:, 0:128], bkey, [(kts[0], kts[0]), (kts[1], kts[1])],
                     reads=["khT"], tag=f"G{h}_{c}")
            kb.op("vector", lambda v, bank=bank, c=c, h=h:
                  v.tensor_scalar(Pt32[:, :], bank[:, 0:128], beta[:, c, h:h + 1], None, ALU.mult),
                  reads=[bkey, "beta"], writes=["Pt32"], tag=f"bg{h}_{c}")
            kb.op("vector", lambda v: v.tensor_tensor(Pp[:, :], Pt32[:, :], masks[:, 1, :], ALU.mult),
                  reads=["Pt32", "masks"], writes=["Pp"], tag=f"p0{h}_{c}")
            bank, bkey = ctx.banks.nf()
            kb.op("tensor", lambda t, bank=bank: t.transpose(bank[:, 0:128], Pp[:, :], ctx.idf[:, :]),
                  reads=["Pp", "idf"], writes=[bkey], tag=f"p0t{h}_{c}")
            kb.op("vector", lambda v, bank=bank: v.tensor_copy(St[:, :], bank[:, 0:128]),
                  reads=[bkey], writes=["St"], tag=f"p0tc{h}_{c}")
            for kk in range(7):
                bank, bkey = ctx.banks.nf()
                kb.op("tensor", lambda t, bank=bank: t.matmul(bank[:, :], St[:, :], uvw[:, :],
                                                              start=True, stop=True),
                      reads=["St", "uvw"], writes=[bkey], tag=f"ap{h}_{c}_{kk}")
                kb.op("vector", lambda v, bank=bank: v.tensor_tensor(uvw[:, :], uvw[:, :], bank[:, :],
                                                                     ALU.add),
                      reads=[bkey, "uvw"], writes=["uvw"], tag=f"apa{h}_{c}_{kk}")
                if kk < 6:
                    bank1, bkey1 = ctx.banks.nf()
                    kb.op("tensor", lambda t, bank1=bank1: t.matmul(bank1[:, 0:128], St[:, :], Pp[:, :],
                                                                    start=True, stop=True),
                          reads=["St", "Pp"], writes=[bkey1], tag=f"sq1{h}_{c}_{kk}")
                    bank2, bkey2 = ctx.banks.nf()
                    kb.op("tensor", lambda t, bank2=bank2: t.matmul(bank2[:, 0:128], Pp[:, :], St[:, :],
                                                                    start=True, stop=True),
                          reads=["St", "Pp"], writes=[bkey2], tag=f"sq2{h}_{c}_{kk}")
                    kb.op("scalar", lambda s, bank1=bank1: s.activation(Pp[:, :], bank1[:, 0:128], AF.Copy),
                          reads=[bkey1], writes=["Pp"], tag=f"sqe1{h}_{c}_{kk}")
                    kb.op("scalar", lambda s, bank2=bank2: s.activation(St[:, :], bank2[:, 0:128], AF.Copy),
                          reads=[bkey2], writes=["St"], tag=f"sqe2{h}_{c}_{kk}")
            bank, bkey = ctx.banks.nf()
            mm_group(kb, bank[:, 0:128], bkey, [(kts[0], qts[0]), (kts[1], qts[1])],
                     reads=["khT", "qhT"], tag=f"an{h}_{c}")
            kb.op("vector", lambda v, bank=bank: v.tensor_tensor(attnT[:, :], bank[:, 0:128],
                                                                 masks[:, 2, :], ALU.mult),
                  reads=[bkey, "masks"], writes=["attnT"], tag=f"anm{h}_{c}")
            bank, bkey = ctx.banks.nf()
            def wtr(t, bank=bank):
                t.transpose(bank[:, 0:128], uvw[:, 256:384], ctx.idf[:, :])
                return t.transpose(bank[:, 128:256], uvw[:, 384:512], ctx.idf[:, :])
            kb.op("tensor", wtr, reads=["uvw", "idf"], writes=[bkey], tag=f"wt{h}_{c}")
            kb.op("vector", lambda v, bank=bank: v.tensor_copy(wT2[:, :, :],
                                                               bank[:, 0:256].rearrange("p (a b) -> p a b", a=2)),
                  reads=[bkey], writes=["wT2"], tag=f"wtc{h}_{c}")
            bank, bkey = ctx.banks.nf()
            mm_group(kb, bank[:, 0:256], bkey,
                     [(wT2[:, 0, :], Sbh[:, 0, :]), (wT2[:, 1, :], Sbh[:, 1, :])],
                     reads=["wT2", "Sbh"], tag=f"ws{h}_{c}")
            kb.op("vector", lambda v, bank=bank: v.tensor_tensor(up[:, :], uvw[:, 0:256], bank[:, 0:256],
                                                                 ALU.subtract),
                  reads=[bkey, "uvw"], writes=["up"], tag=f"upd{h}_{c}")
            bank, bkey = ctx.banks.nf()
            def omm(t, bank=bank, qts=qts):
                t.matmul(bank[:, 0:256], qts[0], Sbh[:, 0, :], start=True, stop=False)
                t.matmul(bank[:, 0:256], qts[1], Sbh[:, 1, :], start=False, stop=False)
                return t.matmul(bank[:, 0:256], attnT[:, :], up[:, :], start=False, stop=True)
            kb.op("tensor", omm, reads=["qhT", "Sbh", "attnT", "up"], writes=[bkey], tag=f"o{h}_{c}")
            kb.op("scalar", lambda s, bank=bank: s.activation(ot[:, :], bank[:, 0:256], AF.Copy),
                  reads=[bkey], writes=["ot"], tag=f"oe{h}_{c}")
            bankd, bkeyd = ctx.banks.nb()
            def dtr(t, bankd=bankd):
                t.transpose(bankd[:, 0:128], ot[:, 0:128], ctx.idb[:, :])
                return t.transpose(bankd[:, 128:256], ot[:, 128:256], ctx.idb[:, :])
            kb.op("tensor", dtr, reads=["ot", "idb"], writes=[bkeyd], tag=f"dtr{h}_{c}")
            kb.op("vector", lambda v, bankd=bankd, c=c:
                  v.tensor_copy(dF2[:, :, c * 128:(c + 1) * 128],
                                bankd[:, 0:256].rearrange("p (a t) -> p a t", a=2)),
                  reads=[bkeyd], writes=["dF2"], tag=f"dtc{h}_{c}")
            for sub in range(2):
                bank, bkey = ctx.banks.nf()
                kb.op("tensor", lambda t, bank=bank, sub=sub:
                      t.matmul(bank[:, 0:256], ktok[:, sub * 128:(sub + 1) * 128], up[:, :],
                               start=True, stop=True),
                      reads=["ktok", "up"], writes=[bkey], tag=f"su{h}_{c}_{sub}")
                kb.op("vector", lambda v, bank=bank, sub=sub:
                      v.tensor_tensor(Sfh[:, sub, :], Sfh[:, sub, :], bank[:, 0:256], ALU.add),
                      reads=[bkey, "Sfh"], writes=["Sfh"], tag=f"sa{h}_{c}_{sub}")
                kb.op("vector", lambda v, sub=sub: v.tensor_copy(Sbh[:, sub, :], Sfh[:, sub, :]),
                      reads=["Sfh"], writes=["Sbh"], tag=f"sc{h}_{c}_{sub}")
        kb.dma("qpool",
               bass.AP(sldram.tensor, 2 * T_FULL * D + h * 256 * 2048,
                       [[2048, 128], [128 * 2048, 2], [1, 2048]]),
               dF2[:, :, :], reads=["dF2"], writes=["sldram"], tag=f"parkd{h}")
        row_stats([dF2[:, 0, :], dF2[:, 1, :]], ["dF2"], 16 + 2 * h, f"p2h{h}", 1.0 / 256)

    for h in range(H):
        for (pi, K_, woff) in [(0, 3, 0), (1, 63, 3)]:
            for sub in range(2):
                ft = 2 * h + sub
                kb.op("vector", lambda v, ft=ft, woff=woff, K_=K_:
                      v.tensor_scalar(acc1[:, :], vT[:, ft, 64 - K_ + 1:64 - K_ + 1 + 2048],
                                      fwcol[:, ft, woff:woff + 1], None, ALU.mult),
                      reads=["vT", "fwcol"], writes=["acc1"], tag=f"f0_{h}_{pi}_{sub}")
                for j in range(1, K_):
                    kb.op("vector", lambda v, ft=ft, woff=woff, K_=K_, j=j:
                          v.scalar_tensor_tensor(acc1[:, :], vT[:, ft, 64 - K_ + 1 + j:64 - K_ + 1 + j + 2048],
                                                 fwcol[:, ft, woff + j:woff + j + 1], acc1[:, :],
                                                 ALU.mult, ALU.add),
                          reads=["vT", "fwcol", "acc1"], writes=["acc1"], tag=f"fv{j}_{h}_{pi}_{sub}")
                kb.op("vector", lambda v, sub=sub: v.tensor_copy(dF2[:, sub, :], acc1[:, :]),
                      reads=["acc1"], writes=["dF2"], tag=f"fc_{h}_{pi}_{sub}")
            srcs = [dF2[:, 0, :], dF2[:, 1, :]]
            row_stats(srcs, ["dF2"], 8 * pi + 2 * h, f"p{pi}h{h}", 1.0 / 256)
            kb.dma("qpool",
                   bass.AP(sldram.tensor, pi * T_FULL * D + h * 256 * 2048,
                           [[2048, 128], [128 * 2048, 2], [1, 2048]]),
                   dF2[:, :, :],
                   reads=["dF2"], writes=["sldram"], tag=f"park{h}_{pi}")
        row_stats([vT[:, 2 * h, 64:2112], vT[:, 2 * h + 1, 64:2112]], ["vT"], 24 + 2 * h, f"p3h{h}", 1.0 / 256)

    if debug:
        dstats = ctx.out("dstats", [32, 2048], F32)
        kb.dma("qpool", dstats[:, :], statsF[0:32, :], reads=["statsF"], tag="dbgs")
        dsl = ctx.out("dsl", [3, T_FULL, D], F32)
        kb.dma("qpool", dsl[:, :, :], sldram[:, :, :], reads=["sldram"], tag="dbgsl")
    kb.op("vector", lambda v: v.memset(statsF[32:64, :], 0.0), writes=["statsFz"], tag="sfz")
    kb.op("vector", lambda v: v.memset(statsF[64:128, :], 0.0), writes=["statsFz"], tag="sfz2")
    kb.op("vector", lambda v: v.memset(w32[32:64, :], 0.0), writes=["w32"], tag="w32z")
    kb.op("vector", lambda v: v.memset(w32[64:128, :], 0.0), writes=["w32"], tag="w32z2")
    hmT = [qhT, khT]
    w32 = kb.sb("w32l1", [128, 128], BF16)
    gb1c = kb.sb("gb1cl1", [128, 16], F32)
    kb.dma("qpool", gb1c[:, :], win["gb1"].rearrange("(m p) -> p m", p=128), writes=["gb1c"],
           allow_slow_non_contiguous=True)
    rbuf = xnT[:, 0:2, :].rearrange("p a (b t) -> p (a b) t", b=4)
    for nb_ in range(4):
        kb.dma("qpool", rbuf[:, :, :],
               bass.AP(xnd.tensor, nb_ * 512, [[2048, 128], [128 * 2048, 8], [1, 512]]),
               reads=["xnd"], writes=["rbuf"] + (xnT_keys if nb_ == 0 else []), tag=f"ldrb{nb_}")
        for m in range(16):
            kb.dma("qpool", gbuf[:, :, :], win["gw1T"][0:1024, m * 128:(m + 1) * 128]
                   .rearrange("(kt p) n -> p kt n", p=128), writes=["gbuf"], tag=f"ldg1a{m}_{nb_}")
            kb.dma("qpool", w32[0:32, :], win["gw1T"][1024:1056, m * 128:(m + 1) * 128], writes=["w32"],
                   tag=f"ldg1b{m}_{nb_}")
            bank, bkey = ctx.banks.nf()
            def gmm(t, bank=bank, m=m, nb_=nb_):
                last = None
                for k in range(8):
                    last = t.matmul(bank[:, :], gbuf[:, k, :], rbuf[:, k, :],
                                    start=(k == 0), stop=False)
                return t.matmul(bank[:, :], w32[:, :], statsF[:, nb_ * 512:(nb_ + 1) * 512],
                                start=False, stop=True)
            kb.op("tensor", gmm, reads=["gbuf", "w32", "statsF", "statsFz", "rbuf"], writes=[bkey],
                  tag=f"g1_{m}_{nb_}")
            dstt = hmT[m // 8]
            dkey = ["qhT", "khT"][m // 8]
            kb.op("scalar", lambda s, bank=bank, dstt=dstt, m=m, nb_=nb_:
                  s.activation(dstt[:, m % 8, nb_ * 512:(nb_ + 1) * 512], bank[:, :], AF.Gelu,
                               bias=gb1c[:, m:m + 1]),
                  reads=[bkey, "gb1c"], writes=[dkey], tag=f"g1e_{m}_{nb_}")
    w2g = kb.sb("w2gl1", [128, 16, 16], BF16)
    kb.dma("qpool", w2g[:, :, :], win["gw2T"].rearrange("(kt p) n -> p kt n", p=128), writes=["w2g"],
           tag="ldg2")
    bbc = kb.sb("bbcl1", [16, 1], F32)
    kb.dma("qpool", bbc[:, :], bass.AP(win["base_bias"].tensor, 0, [[1, 16], [1, 1]]), writes=["bbc"])
    for nb_ in range(4):
        bank, bkey = ctx.banks.nf()
        def lmm(t, bank=bank, nb_=nb_):
            last = None
            for k in range(16):
                src = hmT[k // 8]
                last = t.matmul(bank[0:16, :], w2g[:, k, :], src[:, k % 8, nb_ * 512:(nb_ + 1) * 512],
                                start=(k == 0), stop=(k == 15))
            return last
        kb.op("tensor", lmm, reads=["w2g", "qhT", "khT"], writes=[bkey], tag=f"l2_{nb_}")
        kb.op("scalar", lambda s, bank=bank, nb_=nb_:
              s.activation(logitsT[:, nb_ * 512:(nb_ + 1) * 512], bank[0:16, :], AF.Identity,
                           bias=bbc[:, :]),
              reads=[bkey, "bbc"], writes=["logitsT"], tag=f"l2e_{nb_}")
    ltr = kb.sb("ltrl1", [1, 4], F32)
    flr = kb.sb("flrl1", [1, 16], BF16)
    tmpq = kb.sb("tmpql1", [1, 16], BF16)
    kb.dma("qpool", ltr[:, :], bass.AP(win["log_temp"].tensor, 0, [[4, 1], [1, 4]]), writes=["ltr"])
    kb.dma("qpool", flr[:, :], bass.AP(win["floor_raw"].tensor, 0, [[16, 1], [1, 16]]), writes=["flr"])
    kb.op("scalar", lambda s: s.activation(tmpq[:, 0:4], ltr[:, :], AF.Abs), reads=["ltr"],
          writes=["tmpq"], tag="sp1")
    kb.op("scalar", lambda s: s.activation(tmpq[:, 0:4], tmpq[:, 0:4], AF.Exp, scale=-1.0),
          reads=["tmpq"], writes=["tmpq"], tag="sp2")
    kb.op("scalar", lambda s: s.activation(tmpq[:, 0:4], tmpq[:, 0:4], AF.Ln, bias=1.0),
          reads=["tmpq"], writes=["tmpq"], tag="sp3")
    kb.op("vector", lambda v: v.tensor_scalar(tmpq[:, 4:8], ltr[:, :], 0.0, None, ALU.max),
          reads=["ltr"], writes=["tmpq"], tag="sp4")
    kb.op("vector", lambda v: v.tensor_tensor(tmpq[:, 0:4], tmpq[:, 0:4], tmpq[:, 4:8], ALU.add),
          reads=["tmpq"], writes=["tmpq"], tag="sp5")
    kb.op("vector", lambda v: v.tensor_scalar(tmpq[:, 0:4], tmpq[:, 0:4], 1e-4, None, ALU.add),
          reads=["tmpq"], writes=["tmpq"], tag="sp6")
    def _rcp3(v):
        with ctx.nc.allow_low_precision(reason="bf16 temp recip"):
            return v.reciprocal(tmpq[:, 0:4], tmpq[:, 0:4])
    kb.op("vector", _rcp3, reads=["tmpq"], writes=["tmpq"], tag="sp7")
    kb.op("scalar", lambda s: s.activation(flr[:, :], flr[:, :], AF.Sigmoid), reads=["flr"],
          writes=["flr"], tag="fl1")
    kb.op("vector", lambda v: v.tensor_scalar(flr[:, :], flr[:, :], 0.05, None, ALU.mult),
          reads=["flr"], writes=["flr"], tag="fl2")
    bank, bkey = ctx.banks.nf()
    kb.op("tensor", lambda t, bank=bank: t.matmul(bank[:, 0:4], oneN[:, :], tmpq[:, 0:4],
                                                  start=True, stop=True),
          reads=["tmpq", "oneN"], writes=[bkey], tag="tb1")
    kb.op("vector", lambda v, bank=bank: v.tensor_copy(tcol[:, 0:4], bank[:, 0:4]),
          reads=[bkey], writes=["tcol"], tag="tb2")
    bank2, bkey2 = ctx.banks.nf()
    kb.op("tensor", lambda t, bank2=bank2: t.matmul(bank2[:, 0:16], oneN[:, :], flr[:, :],
                                                    start=True, stop=True),
          reads=["flr", "oneN"], writes=[bkey2], tag="fb1")
    kb.op("vector", lambda v, bank2=bank2: v.tensor_copy(fcol[:, :], bank2[:, 0:16]),
          reads=[bkey2], writes=["fcol"], tag="fb2")
    for i in range(NTT):
        bank, bkey = ctx.banks.nb()
        kb.op("tensor", lambda t, bank=bank, i=i:
              t.transpose(bank[:, 0:16], logitsT[0:16, i * 128:(i + 1) * 128], ctx.idb[0:16, 0:16]),
              reads=["logitsT", "idb"], writes=[bkey], tag=f"lt{i}")
        kb.op("vector", lambda v, bank=bank, i=i: v.tensor_copy(lgt[:, i, :], bank[:, 0:16]),
              reads=[bkey], writes=["lgt"], tag=f"ltc{i}")
        kb.op("vector", lambda v, i=i: v.tensor_reduce(smw[:, 0:4],
                                                       lgt[:, i, :].rearrange("p (h q) -> p h q", q=4),
                                                       AX.X, ALU.max),
              reads=["lgt"], writes=["smw"], tag=f"sx{i}")
        kb.op("vector", lambda v: v.tensor_scalar(smw[:, 4:8], smw[:, 0:4], -1.0, None, ALU.mult),
              reads=["smw"], writes=["smw"], tag=f"sxn{i}")
        for h in range(H):
            kb.op("vector", lambda v, i=i, h=h:
                  v.tensor_scalar(lgt[:, i, 4 * h:4 * h + 4], lgt[:, i, 4 * h:4 * h + 4],
                                  smw[:, 4 + h:5 + h], tcol[:, h:h + 1], ALU.add, ALU.mult),
                  reads=["lgt", "smw", "tcol"], writes=["lgt"], tag=f"st{i}_{h}")
        kb.op("scalar", lambda s, i=i: s.activation(lgt[:, i, :], lgt[:, i, :], AF.Exp),
              reads=["lgt"], writes=["lgt"], tag=f"se{i}")
        kb.op("vector", lambda v, i=i: v.tensor_reduce(smw[:, 0:4],
                                                       lgt[:, i, :].rearrange("p (h q) -> p h q", q=4),
                                                       AX.X, ALU.add),
              reads=["lgt"], writes=["smw"], tag=f"ss{i}")
        kb.op("vector", lambda v: v.reciprocal(smw[:, 0:4], smw[:, 0:4]), reads=["smw"], writes=["smw"],
              tag=f"sr{i}")
        for h in range(H):
            kb.op("vector", lambda v, i=i, h=h:
                  v.tensor_scalar(lgt[:, i, 4 * h:4 * h + 4], lgt[:, i, 4 * h:4 * h + 4],
                                  smw[:, h:h + 1], None, ALU.mult),
                  reads=["lgt", "smw"], writes=["lgt"], tag=f"sn{i}_{h}")
        kb.op("vector", lambda v, i=i: v.tensor_tensor(lgt[:, i, :], lgt[:, i, :], fcol[:, :], ALU.max),
              reads=["lgt", "fcol"], writes=["lgt"], tag=f"fx{i}")
        kb.op("vector", lambda v, i=i: v.tensor_reduce(smw[:, 0:4],
                                                       lgt[:, i, :].rearrange("p (h q) -> p h q", q=4),
                                                       AX.X, ALU.add),
              reads=["lgt"], writes=["smw"], tag=f"fs{i}")
        kb.op("vector", lambda v: v.reciprocal(smw[:, 0:4], smw[:, 0:4]), reads=["smw"], writes=["smw"],
              tag=f"fr{i}")
        for h in range(H):
            kb.op("vector", lambda v, i=i, h=h:
                  v.tensor_scalar(prb[:, i, 4 * h:4 * h + 4], lgt[:, i, 4 * h:4 * h + 4],
                                  smw[:, h:h + 1], None, ALU.mult),
                  reads=["lgt", "smw"], writes=["prb"], tag=f"fn{i}_{h}")
        bank2b, bkey2b = ctx.banks.nb()
        kb.op("tensor", lambda t, bank2b=bank2b, i=i:
              t.transpose(bank2b[0:16, 0:128], prb[:, i, :], ctx.idb[:, :]),
              reads=["prb", "idb"], writes=[bkey2b], tag=f"pt{i}")
        kb.op("vector", lambda v, bank2b=bank2b, i=i: v.tensor_copy(prT[0:16, i * 128:(i + 1) * 128],
                                                                    bank2b[0:16, 0:128]),
              reads=[bkey2b], writes=["prT"], tag=f"ptc{i}")

    if debug:
        dlg = ctx.out("dlg", [16, 2048], F32)
        dpr = ctx.out("dpr", [16, 2048], F32)
        kb.dma("qpool", dlg[:, :], logitsT[:, :], reads=["logitsT"], tag="dbglg")
        kb.dma("qpool", dpr[:, :], prT[:, :], reads=["prT"], tag="dbgpr")
    outF = khT
    prd = ctx.nc.dram_tensor("l1a_prd", [16, 2048], F32).ap()
    kb.dma("qpool", prd[:, :], prT[0:16, :], reads=["prT"], writes=["prd"], tag="parkpr")
    prB = dF2[:, 0, :]
    svb = [xnT[:, 2 * j:2 * j + 2, :] for j in range(3)]
    for h in range(H):
        for pi in range(3):
            kb.dma("qpool", svb[pi][:, :, :],
                   bass.AP(sldram.tensor, pi * T_FULL * D + h * 256 * 2048,
                           [[2048, 128], [128 * 2048, 2], [1, 2048]]),
                   reads=["sldram", "rbuf"], writes=[f"svb{pi}"], tag=f"rel{pi}_{h}")
        paths = [(svb[0], "svb0", 0, 0), (svb[1], "svb1", 1, 0), (svb[2], "svb2", 2, 0), (vT, "vT", 3, 64)]
        for (pt_, pkey, pi, off) in paths:
            r = 4 * h + pi
            kb.dma("qpool", prB[:, :], bass.AP(prd.tensor, r * 2048, [[0, 128], [1, 2048]]),
                   reads=["prd"], writes=["prB", "dF2"], tag=f"cb{h}_{pi}")
            for blk in range(4):
                for sub in range(2):
                    psl = (pt_[:, sub, off + blk * 512:off + (blk + 1) * 512] if pi < 3 else
                           pt_[:, 2 * h + sub, off + blk * 512:off + (blk + 1) * 512])
                    if pi == 0:
                        kb.op("vector", lambda v, psl=psl, h=h, sub=sub, blk=blk:
                              v.tensor_tensor(outF[:, 2 * h + sub, blk * 512:(blk + 1) * 512],
                                              psl, prB[:, blk * 512:(blk + 1) * 512], ALU.mult),
                              reads=["prB", pkey], writes=["khT"], tag=f"cm{h}_{pi}_{blk}_{sub}")
                    else:
                        kb.op("vector", lambda v, psl=psl, blk=blk:
                              v.tensor_tensor(sqb[:, blk * 512:(blk + 1) * 512], psl,
                                              prB[:, blk * 512:(blk + 1) * 512], ALU.mult),
                              reads=["prB", pkey], writes=["sqb"], tag=f"cm{h}_{pi}_{blk}_{sub}")
                        kb.op("vector", lambda v, h=h, sub=sub, blk=blk:
                              v.tensor_tensor(outF[:, 2 * h + sub, blk * 512:(blk + 1) * 512],
                                              outF[:, 2 * h + sub, blk * 512:(blk + 1) * 512],
                                              sqb[:, blk * 512:(blk + 1) * 512], ALU.add),
                              reads=["sqb", "khT"], writes=["khT"], tag=f"ca{h}_{pi}_{blk}_{sub}")
        banks4o = [ctx.banks.nf() for _ in range(4)]
        for sub in range(2):
            kb.op("scalar", lambda s, h=h, sub=sub: s.activation(sqb[:, :], outF[:, 2 * h + sub, :], AF.Square),
                  reads=["khT"] + ([bk for _, bk in banks4o] if sub == 1 else []),
                  writes=["sqb"], tag=f"on{sub}_{h}")
            for blk in range(4):
                bank, bkey = banks4o[blk]
                kb.op("tensor", lambda t, bank=bank, blk=blk, sub=sub:
                      t.matmul(bank[0:1, :], onescol[:, :], sqb[:, blk * 512:(blk + 1) * 512],
                               start=(sub == 0), stop=(sub == 1)),
                      reads=["sqb", "onescol"], writes=[bkey], tag=f"on3{sub}_{h}_{blk}")
        for blk in range(4):
            bank, bkey = banks4o[blk]
            kb.op("scalar", lambda s, bank=bank, blk=blk:
                  s.activation(rows[:, blk * 512:(blk + 1) * 512], bank[0:1, :], AF.Identity, scale=1.0 / 256),
                  reads=[bkey], writes=["rows"], tag=f"on4_{h}_{blk}")
        kb.op("scalar", lambda s: s.activation(rows[:, :], rows[:, :], AF.Sqrt, bias=ctx.epsc[0:1, :]),
              reads=["rows", "epsc"], writes=["rows"], tag=f"on5_{h}")
        def _rcp2(v):
            with ctx.nc.allow_low_precision(reason="bf16 onorm row"):
                return v.reciprocal(rows[:, :], rows[:, :])
        kb.op("vector", _rcp2, reads=["rows"], writes=["rows"], tag=f"on6_{h}")
        kb.op("vector", lambda v: v.tensor_copy(rows2[:, :], rows[:, :]), reads=["rows"], writes=["rows2"],
              tag=f"on6c_{h}")
        for blk in range(4):
            bank, bkey = ctx.banks.nf()
            kb.op("tensor", lambda t, bank=bank, blk=blk:
                  t.matmul(bank[:, :], oneN[:, :], rows2[:, blk * 512:(blk + 1) * 512], start=True, stop=True),
                  reads=["rows2", "oneN"], writes=[bkey], tag=f"on7_{h}_{blk}")
            for sub in range(2):
                kb.op("vector", lambda v, bank=bank, h=h, sub=sub, blk=blk:
                      v.tensor_tensor(outF[:, 2 * h + sub, blk * 512:(blk + 1) * 512],
                                      outF[:, 2 * h + sub, blk * 512:(blk + 1) * 512], bank[:, :], ALU.mult),
                      reads=[bkey, "khT"], writes=["khT"], tag=f"on8_{h}_{blk}_{sub}")
        for sub in range(2):
            kb.op("vector", lambda v, h=h, sub=sub:
                  v.tensor_scalar(outF[:, 2 * h + sub, :], outF[:, 2 * h + sub, :], onw[:, sub:sub + 1],
                                  None, ALU.mult),
                  reads=["khT", "onw"], writes=["khT"], tag=f"on9_{h}_{sub}")
    oute = acc1[:, 0:512]
    for nb_ in range(2):
        kb.dma("qpool", wbuf[:, :, :],
               win["WoT"][:, nb_ * 512:(nb_ + 1) * 512].rearrange("(kt p) n -> p kt n", p=128),
               writes=["wbuf"], tag=f"ldwo{nb_}")
        for i in range(NTT):
            bank, bkey = ctx.banks.nf()
            mm_group(kb, bank[:, :], bkey,
                     [(outF[:, k, i * 128:(i + 1) * 128], wbuf[:, k, :]) for k in range(8)],
                     reads=["wbuf", "khT"], tag=f"wo_{i}_{nb_}")
            kb.op("scalar", lambda s, bank=bank: s.activation(oute[:, :], bank[:, :], AF.Copy),
                  reads=[bkey], writes=["acc1"], tag=f"woe_{i}_{nb_}")
            kb.dma("qsp", out[i * 128:(i + 1) * 128, nb_ * 512:(nb_ + 1) * 512], oute[:, :],
                   reads=["acc1"], tag=f"sto{i}_{nb_}")
    n = kb.build()
    return ctx, n



def emit_l2ffn_routing(ctx, kb, lgt8, tag=""):
    m12 = kb.sb(f"m12{tag}", [128, 6], F32)
    msk = kb.sb(f"msk{tag}", [128, 16, 8], F32)
    gg8 = kb.sb(f"gg8{tag}", [128, 16, 8], F32)
    for i in range(NTT):
        kb.op("vector", lambda v, i=i: v.tensor_reduce(m12[:, 0:1], lgt8[:, i, :], AX.X, ALU.max),
              reads=["lgt8"], writes=["m12"], tag=f"m1{tag}_{i}")
        kb.op("vector", lambda v, i=i: v.tensor_scalar(msk[:, i, :], lgt8[:, i, :], m12[:, 0:1], None,
                                                       ALU.is_ge),
              reads=["lgt8", "m12"], writes=["msk"], tag=f"im{tag}_{i}")
        kb.op("vector", lambda v, i=i:
              v.scalar_tensor_tensor(gg8[:, i, :], msk[:, i, :], -1e30, lgt8[:, i, :], ALU.mult, ALU.add),
              reads=["msk", "lgt8"], writes=["gg8"], tag=f"lm{tag}_{i}")
        kb.op("vector", lambda v, i=i: v.tensor_reduce(m12[:, 1:2], gg8[:, i, :], AX.X, ALU.max),
              reads=["gg8"], writes=["m12"], tag=f"m2{tag}_{i}")
        kb.op("vector", lambda v, i=i: v.tensor_scalar(m12[:, 2:3], m12[:, 1:2], m12[:, 0:1], None,
                                                       ALU.subtract),
              reads=["m12"], writes=["m12"], tag=f"d21{tag}_{i}")
        kb.op("scalar", lambda s: s.activation(m12[:, 3:4], m12[:, 2:3], AF.Exp),
              reads=["m12"], writes=["m12"], tag=f"e2{tag}_{i}")
        kb.op("vector", lambda v: v.tensor_scalar(m12[:, 4:5], m12[:, 3:4], 1.0, None, ALU.add),
              reads=["m12"], writes=["m12"], tag=f"dn{tag}_{i}")
        kb.op("vector", lambda v: v.reciprocal(m12[:, 4:5], m12[:, 4:5]),
              reads=["m12"], writes=["m12"], tag=f"rd{tag}_{i}")
        kb.op("vector", lambda v, i=i: v.tensor_scalar(gg8[:, i, :], gg8[:, i, :], m12[:, 1:2], None,
                                                       ALU.is_ge),
              reads=["gg8", "m12"], writes=["gg8"], tag=f"i2{tag}_{i}")
        kb.op("vector", lambda v, i=i: v.tensor_tensor(msk[:, i, :], msk[:, i, :], gg8[:, i, :], ALU.add),
              reads=["msk", "gg8"], writes=["msk"], tag=f"mm{tag}_{i}")
        kb.op("vector", lambda v: v.tensor_scalar(m12[:, 5:6], m12[:, 0:1], -1.0, None, ALU.mult),
              reads=["m12"], writes=["m12n"], tag=f"nm1{tag}_{i}")
        kb.op("scalar", lambda s, i=i: s.activation(gg8[:, i, :], lgt8[:, i, :], AF.Exp, bias=m12[:, 5:6]),
              reads=["lgt8", "m12n"], writes=["gg8"], tag=f"ge{tag}_{i}")
        kb.op("vector", lambda v, i=i: v.tensor_scalar(gg8[:, i, :], gg8[:, i, :], m12[:, 4:5], None,
                                                       ALU.mult),
              reads=["gg8", "m12"], writes=["gg8"], tag=f"gn{tag}_{i}")
        kb.op("vector", lambda v, i=i: v.tensor_tensor(gg8[:, i, :], gg8[:, i, :], msk[:, i, :], ALU.mult),
              reads=["gg8", "msk"], writes=["gg8"], tag=f"gm{tag}_{i}")
    return m12, msk, gg8


def emit_l2_logits_tile(ctx, kb, xt, xikey, wr32, lgt8, i, tag):
    bank1, bkey1 = ctx.banks.nf()
    bank2, bkey2 = ctx.banks.nf()
    def tr(t, bank1=bank1, bank2=bank2):
        last = None
        for j in range(8):
            bnk = bank1 if j < 4 else bank2
            last = t.transpose(bnk[:, (j % 4) * 128:(j % 4 + 1) * 128],
                               xt[:, j * 128:(j + 1) * 128], ctx.idf[:, :])
        return last
    kb.op("tensor", tr, reads=[xikey, "idf"], writes=[bkey1, bkey2], tag=f"r32t{tag}{i}")
    xf = ctx.l2f_xf32
    kb.op("vector", lambda v, bank1=bank1: v.tensor_copy(xf[:, 0:4, :],
                                                         bank1[:, 0:512].rearrange("p (a t) -> p a t", a=4)),
          reads=[bkey1], writes=["xf32"], tag=f"r32c1{tag}{i}")
    kb.op("vector", lambda v, bank2=bank2: v.tensor_copy(xf[:, 4:8, :],
                                                         bank2[:, 0:512].rearrange("p (a t) -> p a t", a=4)),
          reads=[bkey2], writes=["xf32"], tag=f"r32c2{tag}{i}")
    bank3, bkey3 = ctx.banks.nf()
    mm_group(kb, bank3[:, 0:8], bkey3,
             [(xf[:, k, :], wr32[:, k, :]) for k in range(8)],
             reads=["xf32", "wr32"], tag=f"r32m{tag}{i}")
    kb.op("scalar", lambda s, bank3=bank3, i=i: s.activation(lgt8[:, i, :], bank3[:, 0:8], AF.Copy),
          reads=[bkey3], writes=["lgt8"], tag=f"r32e{tag}{i}")


def stage_l2_ffn_counts(weights):
    ctx = Ctx("l2fc")
    kb = ctx.kb
    ctx.banks = Banks(kb, nf=6, nb=2)
    emit_eps_consts(ctx)
    ctx.const_ident()
    xn_in = ctx.inp("xn", [T_FULL, D], F32, lambda c, e: e["xn"][c])
    win = {"wrT": ctx.inp("l2f_wrT", [1024, 8], F32, lambda c, e: weights["wrT"])}
    cnt_out = ctx.out("counts", [1, 8], F32)
    xt = kb.sb("xtc", [128, 1024], F32)
    lgt8 = kb.sb("lgt8c", [128, 16, 8], F32)
    ctx.l2f_xf32 = kb.sb("xf32c", [128, 8, 128], F32)
    wr32 = kb.sb("wr32c", [128, 8, 8], F32)
    kb.dma("qpool", wr32[:, :, :], win["wrT"].rearrange("(kt p) n -> p kt n", p=128), writes=["wr32"],
           tag="ldwr32")
    for i in range(NTT):
        kb.dma("qsp", xt[:, :], xn_in[i * 128:(i + 1) * 128, :], writes=["xt"], tag=f"ldx{i}")
        emit_l2_logits_tile(ctx, kb, xt, "xt", wr32, lgt8, i, "c")
    m12, msk, gg8 = emit_l2ffn_routing(ctx, kb, lgt8, tag="c")
    bank, bkey = ctx.banks.nf()
    onescol = kb.sb("onescc", [128, 1], F32)
    kb.op("vector", lambda v: v.memset(onescol[:, :], 1.0), writes=["onescol"])
    def cmm(t, bank=bank):
        last = None
        for i in range(NTT):
            last = t.matmul(bank[0:1, 0:8], onescol[:, :], msk[:, i, :], start=(i == 0), stop=(i == 15))
        return last
    kb.op("tensor", cmm, reads=["msk", "onescol"], writes=[bkey], tag="cnt")
    cnt = kb.sb("cnt", [1, 8], F32)
    kb.op("scalar", lambda s, bank=bank: s.activation(cnt[:, :], bank[0:1, 0:8], AF.Copy),
          reads=[bkey], writes=["cnt"], tag="cev")
    kb.dma("qsp", cnt_out[:, :], cnt[:, :], reads=["cnt"], tag="stc")
    n = kb.build()
    return ctx, n


def stage_l2_ffn_main(weights):
    ctx = Ctx("l2fm")
    kb = ctx.kb
    ctx.banks = Banks(kb, nf=6, nb=2)
    emit_eps_consts(ctx)
    ctx.const_ident()
    E, CAP, SQ2PI = 8, 5120, 0.7978845608028654
    xn_in = ctx.inp("xn", [T_FULL, D], F32, lambda c, e: e["xn"][c])
    win = {"wrT": ctx.inp("l2f_wrT", [1024, 8], F32, lambda c, e: weights["wrT"])}
    w1_in = ctx.inp("l2f_w1", [8, 1024, 4096], F32, lambda c, e: weights["w1"])
    w2_in = ctx.inp("l2f_w2", [8, 4096, 1024], F32, lambda c, e: weights["w2"])
    b1_in = ctx.inp("l2f_b1", [8, 4096], F32, lambda c, e: weights["b1"])
    b2_in = ctx.inp("l2f_b2", [8, 1024], F32, lambda c, e: weights["b2"])
    cnts_in = ctx.inp("l2f_cnts", [8, 8], F32, lambda c, e: e["l2f_counts_all"])
    lt_in = ctx.inp("l2f_lt", [8, 1], F32, lambda c, e: e["l2f_ltmask"][c])
    out = ctx.out("f", [T_FULL, D], F32)
    accd = ctx.nc.dram_tensor("l2f_accd", [T_FULL, D], F32).ap()

    xt = kb.sb("xtm", [128, 1024], F32)
    xnT = kb.sb("xnTm", [128, 8, 2048], BF16)
    hs = kb.sb("hsm", [128, 4096], BF16)
    gw = kb.sb("gwm", [128, 4096], BF16)
    yac = kb.sb("yacm", [128, 1024], F32)
    xlb = hs[:, 0:1024]
    lgt8 = kb.sb("lgt8m", [128, 16, 8], F32)
    ctx.l2f_xf32 = kb.sb("xf32m", [128, 8, 128], F32)
    wr32 = kb.sb("wr32m", [128, 8, 8], F32)
    kb.dma("qpool", wr32[:, :, :], win["wrT"].rearrange("(kt p) n -> p kt n", p=128), writes=["wr32"],
           tag="ldwr32")
    for i in range(NTT):
        kb.dma("qsp", xt[:, :], xn_in[i * 128:(i + 1) * 128, :], writes=["xt"], tag=f"ldx{i}")
        kb.op("vector", lambda v: v.tensor_copy(xlb[:, :], xt[:, :]), reads=["xt"], writes=["hs"],
              tag=f"cb{i}")
        emit_transpose_tile(ctx, xlb, ["hs"], xnT, "xnT", i)
        emit_l2_logits_tile(ctx, kb, xt, "xt", wr32, lgt8, i, "m")
    xnT_keys = [f"xnT_{hh}_t{i}" for hh in range(0, 8, 4) for i in range(NTT)]
    m12, msk, gg8 = emit_l2ffn_routing(ctx, kb, lgt8, tag="m")

    mskT = kb.sb("mskTm", [8, 2048], BF16)
    gTm = kb.sb("gTm", [8, 2048], BF16)
    posT = kb.sb("posTm", [8, 2048], F32)
    cnts = kb.sb("cntsm", [8, 8], BF16)
    ltc = kb.sb("ltcm", [8, 1], BF16)
    pref = kb.sb("prefm", [8, 1], F32)
    wtok = kb.sb("wtokm", [128, 16, 8], F32)
    kb.dma("qpool", cnts[:, :], cnts_in[:, :], writes=["cnts"], tag="ldcn")
    kb.dma("qpool", ltc[:, :], lt_in[:, :], writes=["ltc"], tag="ldlt")
    for i in range(NTT):
        bank, bkey = ctx.banks.nf()
        def tr2(t, bank=bank, i=i):
            t.transpose(bank[0:8, 0:128], msk[:, i, :], ctx.idf[:, :])
            return t.transpose(bank[0:8, 128:256], gg8[:, i, :], ctx.idf[:, :])
        kb.op("tensor", tr2, reads=["msk", "gg8", "idf"], writes=[bkey], tag=f"mt{i}")
        kb.op("vector", lambda v, bank=bank, i=i: v.tensor_copy(mskT[:, i * 128:(i + 1) * 128],
                                                                bank[0:8, 0:128]),
              reads=[bkey], writes=["mskT"], tag=f"mtc{i}")
        kb.op("vector", lambda v, bank=bank, i=i: v.tensor_copy(gTm[:, i * 128:(i + 1) * 128],
                                                                bank[0:8, 128:256]),
              reads=[bkey], writes=["gTm"], tag=f"gtc{i}")
    bank, bkey = ctx.banks.nf()
    kb.op("tensor", lambda t, bank=bank: t.matmul(bank[0:8, 0:1], cnts[:, :], ltc[:, :],
                                                  start=True, stop=True),
          reads=["cnts", "ltc"], writes=[bkey], tag="pref")
    kb.op("scalar", lambda s, bank=bank: s.activation(pref[:, :], bank[0:8, 0:1], AF.Copy),
          reads=[bkey], writes=["pref"], tag="prefe")
    kb.op("vector", lambda v: v.tensor_tensor_scan(posT[:, :], mskT[:, :], mskT[:, :], 0.0,
                                                   ALU.bypass, ALU.add)
          if False else v.tensor_tensor_scan(posT[:, :], mskT[:, :], mskT[:, :], 0.0, ALU.mult, ALU.add),
          reads=["mskT"], writes=["posT"], tag="scan")
    onesrow = hs[0:8, 0:2048]
    kb.op("vector", lambda v: v.memset(onesrow[:, :], 1.0), reads=["hs"], writes=["onesrow", "hs"], tag="or1")
    kb.op("vector", lambda v: v.tensor_tensor_scan(posT[:, :], onesrow[:, :], mskT[:, :], 0.0,
                                                   ALU.mult, ALU.add),
          reads=["mskT", "onesrow"], writes=["posT"], tag="scan2")
    kb.op("vector", lambda v: v.tensor_tensor(posT[:, :], posT[:, :], mskT[:, :], ALU.subtract),
          reads=["posT", "mskT"], writes=["posT"], tag="excl")
    kb.op("vector", lambda v: v.tensor_scalar(posT[:, :], posT[:, :], pref[:, :], None, ALU.add),
          reads=["posT", "pref"], writes=["posT"], tag="gpos")
    kb.op("vector", lambda v: v.tensor_scalar(posT[:, :], posT[:, :], float(CAP), None, ALU.is_lt),
          reads=["posT"], writes=["posT"], tag="keep")
    kb.op("vector", lambda v: v.tensor_tensor(gTm[:, :], gTm[:, :], posT[:, :], ALU.mult),
          reads=["gTm", "posT"], writes=["gTm"], tag="wmask")
    kb.op("vector", lambda v: v.tensor_tensor(gTm[:, :], gTm[:, :], mskT[:, :], ALU.mult),
          reads=["gTm", "mskT"], writes=["gTm"], tag="wmask2")
    for i in range(NTT):
        bank, bkey = ctx.banks.nb()
        kb.op("tensor", lambda t, bank=bank, i=i:
              t.transpose(bank[:, 0:8], gTm[0:8, i * 128:(i + 1) * 128], ctx.idb[0:8, 0:8]),
              reads=["gTm", "idb"], writes=[bkey], tag=f"wt{i}")
        kb.op("vector", lambda v, bank=bank, i=i: v.tensor_copy(wtok[:, i, :], bank[:, 0:8]),
              reads=[bkey], writes=["wtok"], tag=f"wtc{i}")

    zb = yac
    kb.op("vector", lambda v: v.memset(zb[:, :], 0.0), reads=["yac"], writes=["zb", "yac"], tag="z0")
    for i in range(NTT):
        kb.dma("qsp", accd[i * 128:(i + 1) * 128, :], zb[:, :], reads=["zb"], writes=["accd"], tag=f"za{i}")

    SQ2PI = 0.7978845608028654
    w1res = kb.sb("w1resm", [128, 8, 4096], BF16)
    w2m = [kb.sb(f"w2mm{j}", [128, 32, 128], BF16) for j in range(2)]
    b1B = kb.sb("b1Bm", [128, 4096], BF16)
    b2c = kb.sb("b2cm", [128, 8], F32)
    hTq = kb.sb("hTqm", [128, 32, 512], BF16)
    OTf = kb.sb("OTfm", [128, 8, 512], BF16)
    for e in range(E):
        kb.dma("qpool", w1res[:, :, :], w1_in[e].rearrange("(kt p) n -> p kt n", p=128),
               writes=["w1res"], tag=f"ldw1{e}")
        kb.dma("qpool", b1B[:, :], bass.AP(b1_in.tensor, e * 4096, [[0, 128], [1, 4096]]),
               writes=["b1B"], tag=f"ldb1{e}")
        kb.dma("qpool", b2c[:, :], b2_in[e].rearrange("(mt p) -> p mt", p=128), writes=["b2c"],
               allow_slow_non_contiguous=True, tag=f"ldb2{e}")
        for q in range(4):
            for il in range(4):
                i = q * 4 + il
                for nb_ in range(8):
                    bank, bkey = ctx.banks.nf()
                    mm_group(kb, bank[:, :], bkey,
                             [(xnT[:, k, i * 128:(i + 1) * 128], w1res[:, k, nb_ * 512:(nb_ + 1) * 512])
                              for k in range(8)],
                             reads=["w1res"] + xnT_keys, tag=f"w1_{e}_{i}_{nb_}")
                    kb.op("vector", lambda v, bank=bank, nb_=nb_:
                          v.tensor_tensor(hs[:, nb_ * 512:(nb_ + 1) * 512], bank[:, :],
                                          b1B[:, nb_ * 512:(nb_ + 1) * 512], ALU.add),
                          reads=[bkey, "b1B"], writes=["hs"], tag=f"he_{e}_{i}_{nb_}")
                kb.op("scalar", lambda s: s.activation(gw[:, :], hs[:, :], AF.Square),
                      reads=["hs"], writes=["gw"], tag=f"g1_{e}_{i}")
                kb.op("vector", lambda v: v.tensor_tensor(gw[:, :], gw[:, :], hs[:, :], ALU.mult),
                      reads=["gw", "hs"], writes=["gw"], tag=f"g2_{e}_{i}")
                kb.op("vector", lambda v: v.scalar_tensor_tensor(gw[:, :], gw[:, :], 0.044715, hs[:, :],
                                                                 ALU.mult, ALU.add),
                      reads=["gw", "hs"], writes=["gw"], tag=f"g3_{e}_{i}")
                kb.op("scalar", lambda s: s.activation(gw[:, :], gw[:, :], AF.Tanh, scale=SQ2PI),
                      reads=["gw"], writes=["gw"], tag=f"g4_{e}_{i}")
                kb.op("vector", lambda v: v.tensor_scalar(gw[:, :], gw[:, :], 1.0, 0.5, ALU.add, ALU.mult),
                      reads=["gw"], writes=["gw"], tag=f"g5_{e}_{i}")
                kb.op("vector", lambda v: v.tensor_tensor(hs[:, :], hs[:, :], gw[:, :], ALU.mult),
                      reads=["gw", "hs"], writes=["hs"], tag=f"g6_{e}_{i}")
                for hh in range(0, 32, 4):
                    bank, bkey = ctx.banks.nb()
                    items = [(bank[:, j * 128:(j + 1) * 128], hs[:, (hh + j) * 128:(hh + j + 1) * 128])
                             for j in range(4)]
                    transpose_group(kb, items, ctx.idb[:, :], "idb", reads=["hs"], writes=[bkey],
                                    tag=f"ht_{e}_{i}_{hh}")
                    kb.op("vector", lambda v, bank=bank, hh=hh, il=il:
                          v.tensor_copy(hTq[:, hh:hh + 4, il * 128:(il + 1) * 128], bank[:, 0:512]),
                          reads=[bkey], writes=["hTq"], tag=f"htc_{e}_{i}_{hh}")
            for m in range(8):
                wm = w2m[m % 2]
                wmk = f"w2mm{m % 2}"
                kb.dma("qpool", wm[:, :, :],
                       w2_in[e][:, m * 128:(m + 1) * 128].rearrange("(kt p) n -> p kt n", p=128),
                       writes=[wmk], tag=f"ldw2{e}_{q}_{m}")
                bank, bkey = ctx.banks.nf()
                mm_group(kb, bank[:, :], bkey,
                         [(wm[:, kt, :], hTq[:, kt, :]) for kt in range(32)],
                         reads=[wmk, "hTq"], tag=f"w2_{e}_{q}_{m}")
                kb.op("scalar", lambda s, bank=bank, m=m:
                      s.activation(OTf[:, m, :], bank[:, :], AF.Identity, bias=b2c[:, m:m + 1]),
                      reads=[bkey, "b2c"], writes=["OTf"], tag=f"w2e_{e}_{q}_{m}")
            for il in range(4):
                i = q * 4 + il
                kb.dma("qsp", yac[:, :], accd[i * 128:(i + 1) * 128, :], reads=["accd"], writes=["yac"],
                       tag=f"lda{e}_{i}")
                for fh in range(2):
                    bank, bkey = ctx.banks.nb()
                    items = [(bank[:, j * 128:(j + 1) * 128], OTf[:, fh * 4 + j, il * 128:(il + 1) * 128])
                             for j in range(4)]
                    transpose_group(kb, items, ctx.idb[:, :], "idb", reads=["OTf"], writes=[bkey],
                                    tag=f"ot_{e}_{i}_{fh}")
                    kb.op("vector", lambda v, bank=bank, fh=fh: v.tensor_copy(hs[:, fh * 512:(fh + 1) * 512],
                                                                              bank[:, 0:512]),
                          reads=[bkey], writes=["hs"], tag=f"otc_{e}_{i}_{fh}")
                kb.op("vector", lambda v, i=i, e=e:
                      v.tensor_scalar(gw[:, 0:1024], hs[:, 0:1024], wtok[:, i, e:e + 1], None, ALU.mult),
                      reads=["hs", "wtok"], writes=["gw"], tag=f"yw_{e}_{i}")
                kb.op("vector", lambda v: v.tensor_tensor(yac[:, :], yac[:, :], gw[:, 0:1024], ALU.add),
                      reads=["yac", "gw"], writes=["yac"], tag=f"ya_{e}_{i}")
                dst = out if e == E - 1 else accd
                kb.dma("qsp", dst[i * 128:(i + 1) * 128, :], yac[:, :],
                       reads=["yac"], writes=["accd"], tag=f"sta{e}_{i}")
    n = kb.build()
    return ctx, n



def stage_ln0():
    ctx = Ctx("ln0")
    kb = ctx.kb
    ctx.banks = Banks(kb, nf=2, nb=2)
    emit_eps_consts(ctx)
    x_in = ctx.inp("x", [T_FULL, D], F32, lambda c, e: e["x"][c])
    xn_out = ctx.out("xn", [T_FULL, D], F32)
    xt = kb.sb("xtg", [128, 1024], F32)
    xo = kb.sb("xog", [128, 1024], F32)
    ctx.ln_stats = kb.sb("lnstg", [128, 12], F32)
    ctx.ln_mvc = kb.sb("lnmvg", [128, 4], F32)
    for i in range(NTT):
        kb.dma("qsp", xt[:, :], x_in[i * 128:(i + 1) * 128, :], writes=["xt"], tag=f"ldx{i}")
        emit_ln_tile(ctx, xt, "xt", xo[:, :], "xo", f"g{i}")
        kb.dma("qsp", xn_out[i * 128:(i + 1) * 128, :], xo[:, :], reads=["xo"], tag=f"st{i}")
    n = kb.build()
    return ctx, n


def stage_glue(lidx):
    ctx = Ctx(f"glue{lidx}")
    kb = ctx.kb
    ctx.banks = Banks(kb, nf=2, nb=2)
    emit_eps_consts(ctx)
    x_in = ctx.inp("x", [T_FULL, D], F32, lambda c, e: e[f"x{lidx}"][c])
    a_in = ctx.inp("a", [T_FULL, D], F32, lambda c, e: e[f"a{lidx}"][c])
    f_in = ctx.inp("f", [T_FULL, D], F32, lambda c, e: e[f"f{lidx}"][c])
    xn_out = ctx.out("x_next", [T_FULL, D], F32)
    ln_out = ctx.out("xn_next", [T_FULL, D], F32)
    xt = kb.sb("xtg", [128, 1024], F32)
    at = kb.sb("atg", [128, 1024], F32)
    ft = kb.sb("ftg", [128, 1024], F32)
    xo = kb.sb("xog", [128, 1024], F32)
    ctx.ln_stats = kb.sb("lnstg", [128, 12], F32)
    ctx.ln_mvc = kb.sb("lnmvg", [128, 4], F32)
    for i in range(NTT):
        kb.dma("qsp", xt[:, :], x_in[i * 128:(i + 1) * 128, :], writes=["xt"], tag=f"ldx{i}")
        kb.dma("qsp", at[:, :], a_in[i * 128:(i + 1) * 128, :], writes=["at"], tag=f"lda{i}")
        kb.dma("qsp", ft[:, :], f_in[i * 128:(i + 1) * 128, :], writes=["ft"], tag=f"ldf{i}")
        kb.op("vector", lambda v: v.tensor_tensor(xt[:, :], xt[:, :], at[:, :], ALU.add),
              reads=["xt", "at"], writes=["xt"], tag=f"a1{i}")
        kb.op("vector", lambda v: v.tensor_tensor(xt[:, :], xt[:, :], ft[:, :], ALU.add),
              reads=["xt", "ft"], writes=["xt"], tag=f"a2{i}")
        kb.dma("qsp", xn_out[i * 128:(i + 1) * 128, :], xt[:, :], reads=["xt"], tag=f"stx{i}")
        emit_ln_tile(ctx, xt, "xt", xo[:, :], "xo", f"g{i}")
        kb.dma("qsp", ln_out[i * 128:(i + 1) * 128, :], xo[:, :], reads=["xo"], tag=f"stn{i}")
    n = kb.build()
    return ctx, n



def prep_weights(data):
    W = {}
    W["l0a"] = {
        "WqP": perm_l0attn_wqk(f32(data["L0_attn_Wq"])),
        "WkP": perm_l0attn_wqk(f32(data["L0_attn_Wk"])),
        "Wv": f32(data["L0_attn_Wv"]), "Wg": f32(data["L0_attn_Wg"]), "Wo": f32(data["L0_attn_Wo"]),
        "W1": f32(data["L0_attn_W1"]), "W2": f32(data["L0_attn_W2"]),
        "ln_s": f32(data["L0_attn_ln_scale"]), "ln_b": f32(data["L0_attn_ln_bias"]),
        "gns": f32(data["L0_attn_gn_scale"]), "gnb": f32(data["L0_attn_gn_bias"]),
    }
    W0f = {}
    for l in range(2):
        W0f[f"wqkv{l}"] = f32(data["L0_ffn_wqkv"][l]); W0f[f"wo{l}"] = f32(data["L0_ffn_wo"][l])
        W0f[f"qs{l}"] = f32(data["L0_ffn_q_scale"][l]); W0f[f"ks{l}"] = f32(data["L0_ffn_k_scale"][l])
        W0f[f"lg{l}"] = f32(data["L0_ffn_ln_g"][l]); W0f[f"lb{l}"] = f32(data["L0_ffn_ln_b"][l])
        w1p, b1p = perm_l0ffn_w1(f32(data["L0_ffn_w1"][l]), f32(data["L0_ffn_b1"][l]))
        W0f[f"w1p{l}"] = w1p; W0f[f"b1p{l}"] = b1p
        W0f[f"w2p{l}"] = perm_l0ffn_w2(f32(data["L0_ffn_w2"][l])); W0f[f"b2{l}"] = f32(data["L0_ffn_b2"][l])
    W["l0f"] = W0f
    W["l1a"] = {
        "WqT": f32(data["L1_attn_Wq"].T), "WkT": f32(data["L1_attn_Wk"].T), "WvT": f32(data["L1_attn_Wv"].T),
        "WbT": f32(data["L1_attn_Wb"].T),
        "qcw": f32(data["L1_attn_qconv_w"]), "kcw": f32(data["L1_attn_kconv_w"]),
        "vcw": f32(data["L1_attn_vconv_w"]),
        "fsw": f32(data["L1_attn_fir_short_w"]), "flw": f32(data["L1_attn_fir_long_w"]),
        "gw1T": f32(data["L1_attn_gate_w1"].T), "gb1": f32(data["L1_attn_gate_b1"]),
        "gw2T": f32(data["L1_attn_gate_w2"].T),
        "log_temp": f32(data["L1_attn_log_temp"]), "base_bias": f32(data["L1_attn_base_bias"]),
        "floor_raw": f32(data["L1_attn_floor_raw"]), "onorm_w": f32(data["L1_attn_onorm_w"]),
        "WoT": f32(data["L1_attn_Wo"].T),
    }
    W["l1f"] = {
        "wiT": f32(data["L1_ffn_w_freq_imag"].T),
        "wrT": f32(data["L1_ffn_w_freq_real"].T),
        "bias": f32(data["L1_ffn_bias"]),
    }
    W["l2a"] = {k: f32(data[f"L2_attn_{k}"]) for k in ["Wq", "Wk", "Wv", "Wo", "bq", "bk", "bv", "bo"]}
    W["l2f"] = {"wrT": f32(data["L2_ffn_wr"].T), "w1": f32(data["L2_ffn_w1"]), "w2": f32(data["L2_ffn_w2"]),
                "b1": f32(data["L2_ffn_b1"]), "b2": f32(data["L2_ffn_b2"])}
    W["l3a"] = {"w_qkv": f32(data["L3_attn_w_qkv"]), "w_out": f32(data["L3_attn_w_out"]),
                "gamma": f32(data["L3_attn_ln_gamma"]), "beta": f32(data["L3_attn_ln_beta"])}
    W["l3f"] = {
        "w1T": f32(data["L3_ffn_w1_w"].T), "w2T": f32(data["L3_ffn_w2_w"].T),
        "h1T": f32(data["L3_ffn_h1_w"].T), "h2T": f32(data["L3_ffn_h2_w"].T),
        "w3T": f32(data["L3_ffn_w3_w"].T),
        "s1": f32(data["L3_ffn_w1_s"]), "s2": f32(data["L3_ffn_w2_s"]),
        "sh1": f32(data["L3_ffn_h1_s"]), "sh2": f32(data["L3_ffn_h2_s"]),
        "s3": f32(data["L3_ffn_w3_s"]),
    }
    return W


def run_module(inputs):
    W = prep_weights(inputs)
    env = {"x": inputs["L0_attn_X"].astype(np.float32)}

    def gather(res, name):
        return np.stack([res.results[c][name] for c in range(8)])

    ctx, _ = stage_ln0()
    env["xn"] = gather(ctx.run(env), "xn")

    ctx, _ = stage_l0_attn(W["l0a"])
    env["a0"] = gather(ctx.run(env), "a")
    ctx, _ = stage_l0_ffn(W["l0f"])
    env["f0"] = gather(ctx.run(env), "f")
    env["x0"] = env["x"]
    ctx, _ = stage_glue(0)
    res = ctx.run(env)
    env["xn"] = gather(res, "xn_next")
    env["x1"] = gather(res, "x_next")
    ctx, _ = stage_l1_attn(W["l1a"])
    env["a1"] = gather(ctx.run(env), "a")
    ctx, _ = stage_l1_ffn(W["l1f"])
    env["f1"] = gather(ctx.run(env), "f")
    ctx, _ = stage_glue(1)
    res = ctx.run(env)
    env["xn"] = gather(res, "xn_next")
    env["x2"] = gather(res, "x_next")
    ctx, _ = stage_l2_attn(W["l2a"])
    env["a2"] = gather(ctx.run(env), "a")
    ctxA, _ = stage_l2_ffn_counts(W["l2f"])
    resA = ctxA.run(env)
    env["l2f_counts_all"] = np.stack([resA.results[c]["counts"][0] for c in range(8)])
    env["l2f_ltmask"] = np.stack([(np.arange(8) < c).astype(np.float32).reshape(8, 1) for c in range(8)])
    ctxB, _ = stage_l2_ffn_main(W["l2f"])
    env["f2"] = gather(ctxB.run(env), "f")
    ctx, _ = stage_glue(2)
    res = ctx.run(env)
    env["xn"] = gather(res, "xn_next")
    env["x3"] = gather(res, "x_next")
    ctx, _ = stage_l3_attn(W["l3a"])
    env["a3"] = gather(ctx.run(env), "a")
    ctx, _ = stage_l3_ffn(W["l3f"])
    env["f3"] = gather(ctx.run(env), "f")
    ctx, _ = stage_glue(3)
    res = ctx.run(env)
    out = gather(res, "x_next")
    return out



def kernel(**inputs):
    out = run_module(inputs)
    return np.ascontiguousarray(out.astype(np.float32))
